# version 18
# baseline (speedup 1.0000x reference)
import numpy as np
import ml_dtypes
import concourse.bass as bass
import concourse.mybir as mybir
from concourse.bass_utils import run_bass_kernel_spmd
from contextlib import ExitStack

F32 = mybir.dt.float32
BF16 = mybir.dt.bfloat16
AF = mybir.ActivationFunctionType
ALU = mybir.AluOpType
AX = mybir.AxisListType
BF16NP = ml_dtypes.bfloat16

B, T, D = 4, 4096, 1024
NCORE = 8
TT = 512
NTT = T // TT


def rep_ap(dram_ap, part_dims, free_dims, offset_elems=0):
    return bass.AP(
        tensor=dram_ap.tensor,
        offset=offset_elems,
        ap=[[s, c] for (s, c) in part_dims] + [[s, c] for (s, c) in free_dims],
    )


def setup_consts(nc, ctx, values, uid=""):
    seen = getattr(nc, "_constvals", None)
    if seen is None:
        seen = nc._constvals = set()
    new = [v for v in values if v not in seen]
    if not new:
        return None, (lambda eng: eng.nop())
    tile = ctx.enter_context(nc.sbuf_tensor("cpool" + uid, [128, len(new)], F32))
    for j, v in enumerate(new):
        nc.const_aps.register(F32, v, tile[:, j:j + 1])
        seen.add(v)

    def emit(eng):
        last = eng.nop()
        for j, v in enumerate(new):
            last = eng.memset(tile[:, j:j + 1], v)
        return last
    return tile, emit


class DramHelper:

    def __init__(self, nc, fused):
        self.nc = nc
        self.fused = fused
        if fused and not hasattr(nc, "_fusedreg"):
            nc._fusedreg = {}

    def __call__(self, name, shape, dtype, kind, host=False):
        if not self.fused:
            return self.nc.dram_tensor(name, shape, dtype, kind=kind).ap()
        reg = self.nc._fusedreg
        if name in reg:
            return reg[name]
        if host is True:
            k = "ExternalInput"
            if not hasattr(self.nc, "_fused_hostkeys"):
                self.nc._fused_hostkeys = set()
            self.nc._fused_hostkeys.add(name)
        elif host == "out":
            k = "ExternalOutput"
        else:
            k = "Internal"
        ap = self.nc.dram_tensor(name, shape, dtype, kind=k).ap()
        reg[name] = ap
        return ap


class Phase:

    _uid = [0]

    def __init__(self, nc, ctx, semctx=None):
        self.nc = nc
        self.ctx = ctx
        self.blk = ctx.enter_context(nc.Block())
        self.sem = {}
        self.cnt = {}
        Phase._uid[0] += 1
        u = Phase._uid[0]
        self._pipe_ctx = (semctx or ctx)
        for nm in ("ld", "lg", "ch", "st"):
            self.sem[nm] = (semctx or ctx).enter_context(nc.semaphore(f"{nm}{u}"))
            self.cnt[nm] = 0

    def _engine(self, name):
        return getattr(self.blk, name)

    def step(self, engine_name, fn, inc="ch", inc_by=1, wait=None, st_guard=True):
        waits = list(wait or []) + ([("st", self.cnt["st"])] if st_guard else [])
        me = self

        def body(eng):
            for smn, cval in waits:
                if cval > 0:
                    eng.wait_ge(me.sem[smn], cval)
            last = fn(eng)
            if inc is not None:
                last.then_inc(me.sem[inc], inc_by)
                me.cnt[inc] += inc_by
        self._engine(engine_name)(body)
        return self.cnt[inc] if inc else None

    def load(self, queue, out, in_, sem=None, wait=None):
        sem = sem or ("lg" if queue == "gpsimd" else "ld")
        me = self
        waits = list(wait or [])

        def body(eng):
            for smn, cval in waits:
                if cval > 0:
                    eng.wait_ge(me.sem[smn], cval)
            eng.dma_start(out=out, in_=in_).then_inc(me.sem[sem], 16)
            me.cnt[sem] += 16
        self._engine(queue)(body)
        return self.cnt[sem]

    def pipe(self, stages, nbanks=2, stage_waits=None, st_guard=True):
        n = len(stages)
        if not hasattr(self, "pjn"):
            self.pjn = {}
        for k in range(n):
            nm = f"p{n}_{k}"
            if nm not in self.sem:
                Phase._uid[0] += 1
                self.sem[nm] = self._pipe_ctx.enter_context(
                    self.nc.semaphore(f"{nm}u{Phase._uid[0]}"))
                self.cnt[nm] = 0
        j = self.pjn.get(n, 0)
        self.pjn[n] = j + 1
        me = self
        for k, (engine_name, fn) in enumerate(stages):
            waits = list((stage_waits or {}).get(k, []))
            if k == 0:
                waits.append((f"p{n}_{n - 1}", j - nbanks + 1))
            else:
                waits.append((f"p{n}_{k - 1}", j + 1))
            if st_guard:
                waits.append(("st", self.cnt["st"]))
            sem_inc = f"p{n}_{k}"

            def body(eng, waits=waits, fn=fn, sem_inc=sem_inc, j=j):
                for smn, cval in waits:
                    if cval > 0:
                        eng.wait_ge(me.sem[smn], cval)
                last = fn(eng, j)
                last.then_inc(me.sem[sem_inc], 1)
                me.cnt[sem_inc] += 1
            self._engine(engine_name)(body)
        return j

    def pipe_done(self, n):
        nm = f"p{n}_{n - 1}"
        return (nm, self.cnt.get(nm, 0))

    def pipe_mid(self, n, k):
        nm = f"p{n}_{k}"
        return (nm, self.cnt.get(nm, 0))

    def load_idx(self, queue, out, in_, idx_ap, element_offset=0, wait=None):
        me = self
        waits = list(wait or [])

        def body(eng):
            for smn, cval in waits:
                if cval > 0:
                    eng.wait_ge(me.sem[smn], cval)
            eng.indirect_dma_start(
                out=out, out_offset=None, in_=in_,
                in_offset=bass.IndirectOffsetOnAxis(ap=idx_ap, axis=0),
                element_offset=element_offset,
            ).then_inc(me.sem["lg"], 16)
            me.cnt["lg"] += 16
        self._engine(queue)(body)
        return self.cnt["lg"]

    def store(self, queue, out, in_, wait=None, sem="st"):
        me = self
        waits = list(wait or [])

        def body(eng):
            for smn, cval in waits:
                if cval > 0:
                    eng.wait_ge(me.sem[smn], cval)
            eng.dma_start(out=out, in_=in_).then_inc(me.sem[sem], 16)
            me.cnt[sem] += 16
        self._engine(queue)(body)
        return self.cnt[sem]

    def add_sem(self, nm):
        Phase._uid[0] += 1
        self.sem[nm] = self._pipe_ctx.enter_context(
            self.nc.semaphore(f"{nm}u{Phase._uid[0]}"))
        self.cnt[nm] = 0

    def finish(self):
        me = self
        tot = self.cnt["st"]

        def body(eng):
            eng.wait_ge(me.sem["st"], tot)
        self._engine("sync")(body)

    def ch(self):
        return self.cnt["ch"]


def build_pre0(nc=None, semctx=None, fused=False):
    standalone = nc is None
    if standalone:
        nc = bass.Bass("TRN2", target_bir_lowering=False, debug=False)
    dr = DramHelper(nc, fused)
    x_d = dr("x_b", [D, T], F32, "ExternalInput", host=True)
    wqkv_d = dr("wqkv", [128, 8 * 768], F32, "ExternalInput", host=True)
    wgab_d = dr("wgab", [128, 8 * 24], F32, "ExternalInput", host=True)
    wconv_d = dr("wconv", [768, 4], F32, "ExternalInput", host=True)
    alog_d = dr("alog8", [8, 1], F32, "ExternalInput", host=True)
    dtb_d = dr("dtb8", [8, 1], F32, "ExternalInput", host=True)
    onesc_d = dr("onesc", [128, 1], F32, "ExternalInput", host=True)
    onesr_d = dr("onesr", [1, 128], F32, "ExternalInput", host=True)

    xn0_d = dr("xn0bf", [D, T], BF16, "ExternalOutput")
    act_d = dr("act", [768, T], F32, "ExternalOutput")
    gab_d = dr("gab", [24, T], F32, "ExternalOutput")

    with ExitStack() as ctx:
        sb = lambda *a: ctx.enter_context(nc.sbuf_tensor(*a, allow_name_mangling=True))
        pst = lambda *a: ctx.enter_context(nc.psum_tensor(*a, allow_name_mangling=True))
        wqkv_s = sb("wqkv_s", [128, 8 * 768], BF16)
        wgab_s = sb("wgab_s", [128, 8 * 24], BF16)
        wconv_s = sb("wconv_s", [128, 6 * 4], F32)
        alog_s = sb("alog_s", [8, 1], F32)
        dtb_s = sb("dtb_s", [8, 1], F32)
        negA = sb("negA", [8, 1], F32)
        onesc = sb("onesc_s", [128, 1], BF16)
        onesr = sb("onesr_s", [1, 128], F32)
        xt = [sb(f"xt{i}", [128, 8 * TT], F32) for i in range(2)]
        xsq = sb("xsq", [128, 8 * TT], BF16)
        xbf = sb("xbf", [128, 8 * TT], BF16)
        lnscr = sb("lnscr", [128, TT], F32)
        stat = sb("stat", [1, 6 * TT], F32)
        xnt = sb("xnt", [128, 8 * TT], BF16)
        pre_ext = [sb(f"pre_ext{i}", [128, 6 * (TT + 3)], F32) for i in range(2)]
        convA = sb("convA", [128, TT], F32)
        convB = sb("convB", [128, TT], F32)
        actt = sb("actt", [128, 6 * TT], F32)
        gsp = sb("gsp", [8, 5 * TT], F32)
        g8 = sb("g8", [8, TT], F32)
        al8 = sb("al8", [8, TT], F32)
        be2 = sb("be2", [2, TT], F32)

        ps_sum = pst("ps_sum", [1, TT], F32)
        ps_sq = pst("ps_sq", [1, TT], F32)
        ps_b1 = pst("ps_b1", [128, TT], F32)
        ps_b2 = pst("ps_b2", [128, TT], F32)
        ps_mm = pst("ps_mm", [128, TT], F32)
        ps_g = pst("ps_g", [8, TT], F32)
        ps_al = pst("ps_al", [8, TT], F32)
        ps_be = pst("ps_be", [2, TT], F32)

        cpool, cemit = setup_consts(nc, (semctx or ctx), [1e-5, 1.0, -1.0, -0.5], uid='p0')
        cemit = getattr(nc, "_cemit_all", None) or cemit
        P = Phase(nc, ctx, semctx)
        P.step("gpsimd", cemit)

        P.load("sync", wconv_s[:, :], rep_ap(wconv_d, [(4, 128)], [(512, 6), (1, 4)]))
        P.load("sync", dtb_s[:], dtb_d)
        P.load("sync", alog_s[:], alog_d)
        ld_const = P.load("sync", onesr[:], onesr_d)
        P.load("gpsimd", wqkv_s[:], wqkv_d)
        P.load("gpsimd", wgab_s[:], wgab_d)
        lg_const = P.load("gpsimd", onesc[:], onesc_d)

        P.step("gpsimd", lambda e: (e.memset(pre_ext[0][:, :], 0.0), e.memset(pre_ext[1][:, :], 0.0))[-1])
        zer = P.ch()

        P.step("scalar", lambda e: e.activation(negA[:], alog_s[:], AF.Exp),
               wait=[("ld", ld_const)])
        P.step("vector", lambda e: e.tensor_scalar(negA[:], negA[:], -1.0, None, op0=ALU.mult),
               wait=[("ch", P.ch())])

        def load_xtile(i, gate_tick):
            t0 = i * TT
            return P.load(
                "sync", xt[i % 2][:],
                rep_ap(x_d, [(T, 128)], [(128 * T, 8), (1, TT)], t0),
                wait=[("ch", gate_tick)])

        ld_x = {0: load_xtile(0, 0), 1: load_xtile(1, 0)}
        tile_end = {}

        for i in range(NTT):
            t0 = i * TT
            par = i % 2
            xti = xt[par]
            w_ld = [("ld", ld_x[i]), ("lg", lg_const), ("ch", P.ch())]
            P.step("scalar", lambda e, xti=xti: e.activation(xsq[:], xti[:], AF.Square), wait=w_ld)
            P.step("vector", lambda e, xti=xti: e.tensor_copy(xbf[:], xti[:]), wait=[("ch", P.ch())])
            def mm_sums(e):
                for kc in range(8):
                    e.matmul(ps_sum[:], onesc[:, 0:1], xbf[:, kc * TT:(kc + 1) * TT],
                             start=(kc == 0), stop=(kc == 7))
                for kc in range(8):
                    i2 = e.matmul(ps_sq[:], onesc[:, 0:1], xsq[:, kc * TT:(kc + 1) * TT],
                                  start=(kc == 0), stop=(kc == 7))
                return i2
            P.step("tensor", mm_sums, wait=[("ch", P.ch())])
            def v_stats(e):
                e.tensor_scalar(stat[:, 0:TT], ps_sum[:], 1.0 / D, None, op0=ALU.mult)
                e.tensor_scalar(stat[:, TT:2 * TT], ps_sq[:], 1.0 / D, None, op0=ALU.mult)
                e.tensor_tensor(stat[:, 2 * TT:3 * TT], stat[:, 0:TT], stat[:, 0:TT], op=ALU.mult)
                return e.tensor_tensor(stat[:, 3 * TT:4 * TT], stat[:, TT:2 * TT],
                                       stat[:, 2 * TT:3 * TT], op=ALU.subtract)
            P.step("vector", v_stats, wait=[("ch", P.ch())])
            def s_rstd(e):
                e.activation(stat[:, 4 * TT:5 * TT], stat[:, 3 * TT:4 * TT], AF.Ln, bias=1e-5)
                return e.activation(stat[:, 4 * TT:5 * TT], stat[:, 4 * TT:5 * TT], AF.Exp, scale=-0.5)
            P.step("scalar", s_rstd, wait=[("ch", P.ch())])
            def mm_bc(e):
                e.matmul(ps_b1[:], onesr[0:1, :], stat[0:1, 0:TT], start=True, stop=True)
                return e.matmul(ps_b2[:], onesr[0:1, :], stat[0:1, 4 * TT:5 * TT], start=True, stop=True)
            P.step("tensor", mm_bc, wait=[("ch", P.ch())])
            def v_xn(e, xti=xti):
                last = None
                for kc in range(8):
                    e.tensor_tensor(lnscr[:], xti[:, kc * TT:(kc + 1) * TT], ps_b1[:], op=ALU.subtract)
                    last = e.tensor_tensor(xnt[:, kc * TT:(kc + 1) * TT], lnscr[:], ps_b2[:], op=ALU.mult)
                return last
            P.step("vector", v_xn, wait=[("ch", P.ch())])
            xn_done = P.ch()
            P.store("sync", rep_ap(xn0_d, [(T, 128)], [(128 * T, 8), (1, TT)], t0), xnt[:],
                    wait=[("ch", xn_done)])
            for oc in range(6):
                def mm_qkv(e, oc=oc):
                    last = None
                    for kc in range(8):
                        last = e.matmul(ps_mm[:], wqkv_s[:, kc * 768 + oc * 128: kc * 768 + (oc + 1) * 128],
                                        xnt[:, kc * TT:(kc + 1) * TT],
                                        start=(kc == 0), stop=(kc == 7))
                    return last
                P.step("tensor", mm_qkv, wait=[("ch", P.ch())])
                off = oc * (TT + 3)
                def v_conv(e, oc=oc, off=off, par=par, i=i):
                    e.tensor_copy(pre_ext[par][:, off + 3: off + 3 + TT], ps_mm[:])
                    if i > 0:
                        e.tensor_copy(pre_ext[par][:, off: off + 3],
                                      pre_ext[1 - par][:, off + TT: off + TT + 3])
                    e.tensor_scalar(convA[:], pre_ext[par][:, off: off + TT],
                                    wconv_s[:, oc * 4: oc * 4 + 1], None, op0=ALU.mult)
                    e.scalar_tensor_tensor(convB[:], pre_ext[par][:, off + 1: off + 1 + TT],
                                           wconv_s[:, oc * 4 + 1: oc * 4 + 2], convA[:],
                                           op0=ALU.mult, op1=ALU.add)
                    e.scalar_tensor_tensor(convA[:], pre_ext[par][:, off + 2: off + 2 + TT],
                                           wconv_s[:, oc * 4 + 2: oc * 4 + 3], convB[:],
                                           op0=ALU.mult, op1=ALU.add)
                    return e.scalar_tensor_tensor(convB[:], pre_ext[par][:, off + 3: off + 3 + TT],
                                                  wconv_s[:, oc * 4 + 3: oc * 4 + 4], convA[:],
                                                  op0=ALU.mult, op1=ALU.add)
                P.step("vector", v_conv, wait=[("ch", P.ch())])
                P.step("scalar", lambda e, oc=oc: e.activation(actt[:, oc * TT:(oc + 1) * TT], convB[:], AF.Silu),
                       wait=[("ch", P.ch())])
            act_done = P.ch()
            P.store("sync", rep_ap(act_d, [(T, 128)], [(128 * T, 6), (1, TT)], t0), actt[:],
                    wait=[("ch", act_done)])
            def mm_gab(e):
                last = None
                for kc in range(8):
                    e.matmul(ps_g[:], wgab_s[:, kc * 24: kc * 24 + 8],
                             xnt[:, kc * TT:(kc + 1) * TT], start=(kc == 0), stop=(kc == 7))
                for kc in range(8):
                    e.matmul(ps_al[:], wgab_s[:, kc * 24 + 8: kc * 24 + 16],
                             xnt[:, kc * TT:(kc + 1) * TT], start=(kc == 0), stop=(kc == 7))
                for kc in range(8):
                    last = e.matmul(ps_be[:], wgab_s[:, kc * 24 + 16: kc * 24 + 18],
                                    xnt[:, kc * TT:(kc + 1) * TT], start=(kc == 0), stop=(kc == 7))
                return last
            P.step("tensor", mm_gab, wait=[("ch", P.ch())])
            def s_gab(e):
                e.activation(gsp[:, 0:TT], ps_g[:], AF.Abs, bias=dtb_s[:, 0:1])
                e.activation(gsp[:, TT:2 * TT], gsp[:, 0:TT], AF.Exp, scale=-1.0)
                e.activation(gsp[:, 2 * TT:3 * TT], gsp[:, TT:2 * TT], AF.Ln, bias=1.0)
                e.activation(gsp[:, 3 * TT:4 * TT], ps_g[:], AF.Relu, bias=dtb_s[:, 0:1])
                e.activation(gsp[:, 4 * TT:5 * TT], ps_al[:], AF.Sigmoid)
                return e.activation(be2[:], ps_be[:], AF.Sigmoid)
            P.step("scalar", s_gab, wait=[("ch", P.ch())])
            def v_gab(e):
                e.tensor_tensor(g8[:], gsp[:, 2 * TT:3 * TT], gsp[:, 3 * TT:4 * TT], op=ALU.add)
                e.tensor_scalar(g8[:], g8[:], negA[:, 0:1], None, op0=ALU.mult)
                return e.tensor_scalar(al8[:], gsp[:, 4 * TT:5 * TT], 0.125, None, op0=ALU.mult)
            P.step("vector", v_gab, wait=[("ch", P.ch())])
            gdone = P.ch()
            P.store("sync", rep_ap(gab_d, [(T, 8)], [(1, TT)], t0), g8[:], wait=[("ch", gdone)])
            P.store("sync", rep_ap(gab_d, [(T, 8)], [(1, TT)], 8 * T + t0), al8[:], wait=[("ch", gdone)])
            P.store("sync", rep_ap(gab_d, [(T, 2)], [(1, TT)], 16 * T + t0), be2[:], wait=[("ch", gdone)])
            tile_end[i] = P.ch()
            if i + 2 < NTT:
                ld_x[i + 2] = load_xtile(i + 2, tile_end[i])
        P.finish()
    return nc


def build_scan0(nc=None, semctx=None, fused=False, debug=False):
    C = 128
    NCH = T // C
    standalone = nc is None
    if standalone:
        nc = bass.Bass("TRN2", target_bir_lowering=False, debug=False)
    dr = DramHelper(nc, fused)
    act_d = dr("act", [768, T], F32, "ExternalInput")
    gab_d = dr("gab", [24, T], F32, "ExternalInput")
    ltri_d = dr("ltri", [128, 128], F32, "ExternalInput", host=True)
    mneg_d = dr("mneg", [128, 128], F32, "ExternalInput", host=True)
    idf_d = dr("idf", [128, 128], F32, "ExternalInput", host=True)
    idb_d = dr("idb", [128, 128], BF16, "ExternalInput", host=True)
    eh8_d = dr("eh8", [8, 8 * 128], F32, "ExternalInput", host=True)
    onesr_d = dr("onesr", [1, 128], F32, "ExternalInput", host=True)
    nw8_d = dr("nw8", [128, 512], F32, "ExternalInput", host=True)
    o_d = dr("o_out", [T, 512], BF16, "ExternalOutput")
    if debug:
        dbg_d = {nm: nc.dram_tensor("dbg_" + nm, shp, F32, kind="ExternalOutput").ap()
                 for nm, shp in [("Gs", [128, 8]), ("Grow", [8, 128]), ("Msb", [128, 8 * 128]),
                                 ("lam8", [128, 8]), ("ktm", [128, 128]), ("ktm_r", [128, 128]), ("kss", [128, 2]), ("Asb", [128, 8 * 128]),
                                 ("qtm", [128, 512]), ("Oraw", [128, 512]), ("wcol8", [128, 8]),
                                 ("Hkk0", [64, 512]), ("obuf", [128, 512]), ("ms8", [128, 8]), ("rstd8", [128, 8]), ("O2", [128, 512]), ("O3", [128, 512]), ("nwbc", [128, 512])]}

    with ExitStack() as ctx:
        sb = lambda *a: ctx.enter_context(nc.sbuf_tensor(*a, allow_name_mangling=True))
        pst = lambda *a: ctx.enter_context(nc.psum_tensor(*a, allow_name_mangling=True))
        ltri = sb("ltri_s", [128, 128], F32)
        mneg = sb("mneg_s", [128, 128], F32)
        idf = sb("idf_s", [128, 128], F32)
        idb = sb("idb_s", [128, 128], BF16)
        eh8 = sb("eh8_s", [8, 8 * 128], F32)
        onesr = sb("onesr_s", [1, 128], F32)
        nwbc = sb("nwbc", [128, 512], F32)
        gab_t = [sb(f"gab_t{i}", [18, 128], F32) for i in range(2)]
        gat = [sb(f"gat{i}", [128, 18], F32) for i in range(2)]
        qf = [sb(f"qf{i}", [64, 8 * 128], BF16) for i in range(2)]
        kfv = [sb(f"kfv{i}", [64, 2 * 128], BF16) for i in range(2)]
        vfv = [sb(f"vfv{i}", [64, 2 * 128], BF16) for i in range(2)]
        Gs = sb("Gs", [128, 8], F32)
        lam8 = sb("lam8", [128, 8], F32)
        Grow = sb("Grow", [8, 128], F32)
        glrow = sb("glrow", [1, 8], F32)
        wpre = sb("wpre", [128, 8], F32)
        wcol8 = sb("wcol8", [128, 8], F32)
        lamC8 = sb("lamC8", [128, 8], F32)
        ktm_r = sb("ktm_r", [128, 2 * 64], BF16)
        vtm_r = sb("vtm_r", [128, 2 * 64], BF16)
        ktm = sb("ktm", [128, 2 * 64], BF16)
        vtm = sb("vtm", [128, 2 * 64], BF16)
        kfm = sb("kfm", [64, 2 * 128], BF16)
        ksq = sb("ksq", [128, 2 * 64], F32)
        kss = sb("kss", [128, 2], F32)
        krcp = sb("krcp", [128, 2], F32)
        Ysb = sb("Ysb", [128, 8 * 128], F32)
        Msb = sb("Msb", [128, 8 * 128], F32)
        Asb = sb("Asb", [128, 8 * 128], BF16)
        qtm = sb("qtm", [128, 8 * 64], BF16)
        xq1f = sb("xq1f", [64, 8 * 128], BF16)
        xq2f = sb("xq2f", [64, 8 * 128], BF16)
        T1 = sb("T1", [128, 8 * 64], F32)
        T2 = sb("T2", [128, 8 * 64], F32)
        xqtm = sb("xqtm", [128, 8 * 64], BF16)
        Oraw = sb("Oraw", [128, 8 * 64], F32)
        O2 = sb("O2", [128, 8 * 64], F32)
        Osq = sb("Osq", [128, 8 * 64], F32)
        ms8 = sb("ms8", [128, 8], F32)
        rstd8 = sb("rstd8", [128, 8], F32)
        O3 = sb("O3", [128, 8 * 64], F32)
        obuf = sb("obuf", [128, 512], BF16)
        Kwb = sb("Kwb", [128, 8 * 64], BF16)
        Hkk = [sb(f"Hkk{i}", [64, 8 * 64], F32) for i in range(2)]
        Hkv = [sb(f"Hkv{i}", [64, 8 * 64], F32) for i in range(2)]
        Hkkbf = sb("Hkkbf", [64, 8 * 64], BF16)
        Hkvbf = sb("Hkvbf", [64, 8 * 64], BF16)
        psA = pst("psA", [128, 8 * 128], F32)
        psB = pst("psB", [128, 8 * 64], F32)
        psC = pst("psC", [128, 8 * 64], F32)
        psD = pst("psD", [128, 1024], BF16)
        psE = pst("psE", [128, 128], F32)

        cpool, cemit = setup_consts(nc, (semctx or ctx), [1e-6, -0.5, -1.0], uid='s0')
        P = Phase(nc, ctx, semctx)
        P.step("gpsimd", cemit)

        P.load("sync", ltri[:], ltri_d)
        P.load("sync", mneg[:], mneg_d)
        P.load("sync", idf[:], idf_d)
        P.load("sync", eh8[:], eh8_d)
        P.load("sync", onesr[:], onesr_d)
        lds = P.load("sync", nwbc[:], nw8_d)
        ldg = P.load("gpsimd", idb[:], idb_d)

        def zinit(e):
            e.memset(Hkk[1][:], 0.0)
            e.memset(Hkv[1][:], 0.0)
            e.memset(Hkkbf[:], 0.0)
            return e.memset(Hkvbf[:], 0.0)
        P.step("gpsimd", zinit)

        ld_q = {}
        def load_chunk(j, gate):
            t0 = j * C
            par = j % 2
            P.load("sync", gab_t[par][:], rep_ap(gab_d, [(T, 18)], [(1, C)], t0), wait=[("ch", gate)])
            P.load("gpsimd", qf[par][:], rep_ap(act_d, [(T, 64)], [(64 * T, 8), (1, C)], t0),
                   wait=[("ch", gate)])
            P.load("gpsimd", kfv[par][:], rep_ap(act_d, [(T, 64)], [(64 * T, 2), (1, C)], 512 * T + t0),
                   wait=[("ch", gate)])
            return P.load("gpsimd", vfv[par][:], rep_ap(act_d, [(T, 64)], [(64 * T, 2), (1, C)], 640 * T + t0),
                          wait=[("ch", gate)]), P.cnt["ld"]

        ld_q[0] = load_chunk(0, 0)
        ld_q[1] = load_chunk(1, 0)
        chunk_end = {}

        for j in range(NCH):
            t0 = j * C
            par = j % 2
            lgv, ldv = ld_q[j]
            qfj, kfj, vfj = qf[par], kfv[par], vfv[par]
            wld = [("lg", lgv), ("ld", ldv), ("ch", P.ch())]
            P.step("tensor", lambda e: e.transpose(psE[:, 0:18], gab_t[par][:], idf[0:18, 0:18]), wait=wld)
            P.step("vector", lambda e: e.tensor_copy(gat[par][:], psE[:, 0:18]), wait=[("ch", P.ch())])
            def s_gkv(e):
                e.matmul(psE[:, 32:40], ltri[:], gat[par][:, 0:8], start=True, stop=True)
                for kv in range(2):
                    e.transpose(psD[:, 64 * kv: 64 * kv + 64], kfj[:, 128 * kv:128 * (kv + 1)], idb[0:64, 0:64])
                last = None
                for kv in range(2):
                    last = e.transpose(psD[:, 128 + 64 * kv: 128 + 64 * kv + 64],
                                       vfj[:, 128 * kv:128 * (kv + 1)], idb[0:64, 0:64])
                return last
            P.step("tensor", s_gkv, wait=[("ch", P.ch())])
            def v_gkv(e):
                e.tensor_copy(Gs[:], psE[:, 32:40])
                e.tensor_copy(ktm_r[:], psD[:, 0:128])
                return e.tensor_copy(vtm_r[:], psD[:, 128:256])
            P.step("vector", v_gkv, wait=[("ch", P.ch())])
            def s_lamk(e):
                e.activation(lam8[:], Gs[:], AF.Exp)
                return e.activation(ksq[:], ktm_r[:], AF.Square)
            P.step("scalar", s_lamk, wait=[("ch", P.ch())])
            P.step("vector", lambda e: e.reduce_sum(kss[:], ksq.rearrange("p (kv f) -> p kv f", kv=2), axis=AX.X),
                   wait=[("ch", P.ch())])
            def s_krcp(e):
                e.activation(kss[:], kss[:], AF.Sqrt)
                e.activation(krcp[:], kss[:], AF.Ln, bias=1e-6)
                return e.activation(krcp[:], krcp[:], AF.Exp, scale=-1.0)
            P.step("scalar", s_krcp, wait=[("ch", P.ch())])
            def v_knorm(e):
                for kv in range(2):
                    e.tensor_scalar(ktm[:, 64 * kv:64 * (kv + 1)], ktm_r[:, 64 * kv:64 * (kv + 1)],
                                    krcp[:, kv:kv + 1], None, op0=ALU.mult)
                last = None
                for kv in range(2):
                    last = e.tensor_scalar(vtm[:, 64 * kv:64 * (kv + 1)], vtm_r[:, 64 * kv:64 * (kv + 1)],
                                           gat[par][:, 16 + kv:17 + kv], None, op0=ALU.mult)
                return last
            P.step("vector", v_knorm, wait=[("ch", P.ch())])
            def t_kfm(e):
                last = None
                for kv in range(2):
                    last = e.transpose(psD[0:64, 256 + 128 * kv: 256 + 128 * (kv + 1)],
                                       ktm[:, 64 * kv:64 * (kv + 1)], idb[:, :])
                return last
            P.step("tensor", t_kfm, wait=[("ch", P.ch())])
            P.step("vector", lambda e: e.tensor_copy(kfm[:], psD[0:64, 256:512]), wait=[("ch", P.ch())])
            P.step("tensor", lambda e: e.transpose(psE[0:8, 0:128], Gs[:], idf[:, :]), wait=[("ch", P.ch())])
            P.step("vector", lambda e: e.tensor_copy(Grow[:], psE[0:8, 0:128]), wait=[("ch", P.ch())])
            P.step("tensor", lambda e: e.transpose(psE[0:1, 120:128], Grow[:, 127:128], idf[0:8, 0:8]),
                   wait=[("ch", P.ch())])
            P.step("vector", lambda e: e.tensor_copy(glrow[:], psE[0:1, 120:128]), wait=[("ch", P.ch())])
            def t_bcast(e):
                e.matmul(psE[:, 96:104], onesr[0:1, :], glrow[0:1, :], start=True, stop=True)
                last = None
                for h in range(8):
                    last = e.matmul(psA[:, 128 * h:128 * (h + 1)], eh8[:, 128 * h:128 * (h + 1)],
                                    Grow[:], start=True, stop=True)
                return last
            P.step("tensor", t_bcast, wait=[("ch", P.ch())])
            def v_masks(e):
                e.tensor_tensor(wpre[:], psE[:, 96:104], Gs[:], op=ALU.subtract)
                last = None
                for h in range(8):
                    last = e.scalar_tensor_tensor(Ysb[:, 128 * h:128 * (h + 1)], psA[:, 128 * h:128 * (h + 1)],
                                                  Gs[:, h:h + 1], mneg[:], op0=ALU.subtract, op1=ALU.add)
                return last
            P.step("vector", v_masks, wait=[("ch", P.ch())])
            def s_masks(e):
                e.activation(Msb[:], Ysb[:], AF.Exp)
                e.activation(wcol8[:], wpre[:], AF.Exp)
                return e.activation(lamC8[:], psE[:, 96:104], AF.Exp)
            P.step("scalar", s_masks, wait=[("ch", P.ch())])

            def emit_apply(U_fm, is_last):
                def t_A(e):
                    last = None
                    for h in range(8):
                        kv = h // 4
                        last = e.matmul(psA[:, 128 * h:128 * (h + 1)], kfm[:, 128 * kv:128 * (kv + 1)],
                                        U_fm[:, 128 * h:128 * (h + 1)], start=True, stop=True)
                    return last
                P.step("tensor", t_A, wait=[("ch", P.ch())])
                def v_A(e):
                    last = None
                    for h in range(8):
                        last = e.tensor_tensor(Asb[:, 128 * h:128 * (h + 1)], psA[:, 128 * h:128 * (h + 1)],
                                               Msb[:, 128 * h:128 * (h + 1)], op=ALU.mult)
                    return last
                P.step("vector", v_A, wait=[("ch", P.ch())])
                def t_z(e):
                    for h in range(8):
                        kv = h // 4
                        rhs = vtm if is_last else ktm
                        e.matmul(psB[:, 64 * h:64 * (h + 1)], Asb[:, 128 * h:128 * (h + 1)],
                                 rhs[:, 64 * kv:64 * (kv + 1)], start=True, stop=True)
                    last = None
                    for h in range(8):
                        hs = Hkvbf if is_last else Hkkbf
                        last = e.matmul(psC[:, 64 * h:64 * (h + 1)], U_fm[:, 128 * h:128 * (h + 1)],
                                        hs[:, 64 * h:64 * (h + 1)], start=True, stop=True)
                    if U_fm is qf[par]:
                        for h in range(8):
                            last = e.transpose(psD[:, 64 * h:64 * (h + 1)],
                                               U_fm[:, 128 * h:128 * (h + 1)], idb[0:64, 0:64])
                    return last
                P.step("tensor", t_z, wait=[("ch", P.ch())])
                return None
            emit_apply(qfj, False)
            def v_comb1(e):
                e.tensor_copy(qtm[:], psD[:, 0:512])
                for h in range(8):
                    e.tensor_scalar(T1[:, 64 * h:64 * (h + 1)], psC[:, 64 * h:64 * (h + 1)],
                                    lam8[:, h:h + 1], -0.25, op0=ALU.mult, op1=ALU.mult)
                for h in range(8):
                    e.scalar_tensor_tensor(T2[:, 64 * h:64 * (h + 1)], psB[:, 64 * h:64 * (h + 1)],
                                           -0.25, qtm[:, 64 * h:64 * (h + 1)], op0=ALU.mult, op1=ALU.add)
                return e.tensor_tensor(xqtm[:], T1[:], T2[:], op=ALU.add)
            P.step("vector", v_comb1, wait=[("ch", P.ch())])
            def t_xf1(e):
                last = None
                for h in range(8):
                    last = e.transpose(psD[0:64, 128 * h:128 * (h + 1)],
                                       xqtm[:, 64 * h:64 * (h + 1)], idb[:, :])
                return last
            P.step("tensor", t_xf1, wait=[("ch", P.ch())])
            P.step("vector", lambda e: e.tensor_copy(xq1f[:], psD[0:64, :]), wait=[("ch", P.ch())])
            emit_apply(xq1f, False)
            def v_comb2(e):
                for h in range(8):
                    e.tensor_scalar(T1[:, 64 * h:64 * (h + 1)], psC[:, 64 * h:64 * (h + 1)],
                                    lam8[:, h:h + 1], -0.25, op0=ALU.mult, op1=ALU.mult)
                for h in range(8):
                    e.scalar_tensor_tensor(T2[:, 64 * h:64 * (h + 1)], psB[:, 64 * h:64 * (h + 1)],
                                           -0.25, qtm[:, 64 * h:64 * (h + 1)], op0=ALU.mult, op1=ALU.add)
                return e.tensor_tensor(xqtm[:], T1[:], T2[:], op=ALU.add)
            P.step("vector", v_comb2, wait=[("ch", P.ch())])
            def t_xf2(e):
                last = None
                for h in range(8):
                    last = e.transpose(psD[0:64, 128 * h:128 * (h + 1)],
                                       xqtm[:, 64 * h:64 * (h + 1)], idb[:, :])
                return last
            P.step("tensor", t_xf2, wait=[("ch", P.ch())])
            P.step("vector", lambda e: e.tensor_copy(xq2f[:], psD[0:64, :]), wait=[("ch", P.ch())])
            emit_apply(xq2f, True)
            def v_oraw(e):
                for h in range(8):
                    e.tensor_scalar(T1[:, 64 * h:64 * (h + 1)], psC[:, 64 * h:64 * (h + 1)],
                                    lam8[:, h:h + 1], None, op0=ALU.mult)
                e.tensor_tensor(Oraw[:], psB[:, :], T1[:], op=ALU.add)
                last = None
                for h in range(8):
                    last = e.tensor_scalar(O2[:, 64 * h:64 * (h + 1)], Oraw[:, 64 * h:64 * (h + 1)],
                                           gat[par][:, 8 + h:9 + h], None, op0=ALU.mult)
                return last
            P.step("vector", v_oraw, wait=[("ch", P.ch())])
            P.step("scalar", lambda e: e.activation(Osq[:], O2[:], AF.Square), wait=[("ch", P.ch())])
            P.step("vector", lambda e: e.reduce_sum(ms8[:], Osq.rearrange("p (h f) -> p h f", h=8), axis=AX.X),
                   wait=[("ch", P.ch())])
            P.step("vector", lambda e: e.tensor_scalar(rstd8[:], ms8[:], 1.0 / 64, 1e-6, op0=ALU.mult, op1=ALU.add),
                   wait=[("ch", P.ch())])
            def s_rstd(e):
                e.activation(rstd8[:], rstd8[:], AF.Ln)
                return e.activation(rstd8[:], rstd8[:], AF.Exp, scale=-0.5)
            P.step("scalar", s_rstd, wait=[("ch", P.ch())])
            def v_o(e):
                for h in range(8):
                    e.tensor_scalar(O3[:, 64 * h:64 * (h + 1)], O2[:, 64 * h:64 * (h + 1)],
                                    rstd8[:, h:h + 1], None, op0=ALU.mult)
                return e.tensor_tensor(obuf[:], O3[:], nwbc[:], op=ALU.mult)
            P.step("vector", v_o, wait=[("ch", P.ch())])
            odone = P.ch()
            P.store("sync", rep_ap(o_d, [(512, 128)], [(1, 512)], t0 * 512), obuf[:], wait=[("ch", odone)])
            def v_kw(e):
                last = None
                for h in range(8):
                    kv = h // 4
                    last = e.tensor_scalar(Kwb[:, 64 * h:64 * (h + 1)], ktm[:, 64 * kv:64 * (kv + 1)],
                                           wcol8[:, h:h + 1], None, op0=ALU.mult)
                return last
            P.step("vector", v_kw, wait=[("ch", P.ch())])
            def t_state(e):
                for h in range(8):
                    kv = h // 4
                    e.matmul(psB[0:64, 64 * h:64 * (h + 1)], Kwb[:, 64 * h:64 * (h + 1)],
                             ktm[:, 64 * kv:64 * (kv + 1)], start=True, stop=True)
                last = None
                for h in range(8):
                    kv = h // 4
                    last = e.matmul(psC[0:64, 64 * h:64 * (h + 1)], Kwb[:, 64 * h:64 * (h + 1)],
                                    vtm[:, 64 * kv:64 * (kv + 1)], start=True, stop=True)
                return last
            P.step("tensor", t_state, wait=[("ch", P.ch())])
            def v_state(e):
                for h in range(8):
                    e.scalar_tensor_tensor(Hkk[par][:, 64 * h:64 * (h + 1)], Hkk[1 - par][:, 64 * h:64 * (h + 1)],
                                           lamC8[0:64, h:h + 1], psB[0:64, 64 * h:64 * (h + 1)],
                                           op0=ALU.mult, op1=ALU.add)
                for h in range(8):
                    e.scalar_tensor_tensor(Hkv[par][:, 64 * h:64 * (h + 1)], Hkv[1 - par][:, 64 * h:64 * (h + 1)],
                                           lamC8[0:64, h:h + 1], psC[0:64, 64 * h:64 * (h + 1)],
                                           op0=ALU.mult, op1=ALU.add)
                e.tensor_copy(Hkkbf[:], Hkk[par][:])
                return e.tensor_copy(Hkvbf[:], Hkv[par][:])
            P.step("vector", v_state, wait=[("ch", P.ch())])
            if debug and j == 0:
                dd = P.ch()
                for nm, tile in [("Gs", Gs), ("Grow", Grow), ("Msb", Msb), ("lam8", lam8),
                                 ("ktm", ktm), ("ktm_r", ktm_r), ("kss", kss), ("Asb", Asb), ("qtm", qtm), ("Oraw", Oraw),
                                 ("wcol8", wcol8), ("Hkk0", Hkk[0]), ("obuf", obuf), ("ms8", ms8), ("rstd8", rstd8), ("O2", O2), ("O3", O3), ("nwbc", nwbc)]:
                    scr = sb("dbg_s_" + nm, list(tile.shape), F32)
                    P.step("vector", lambda e, scr=scr, tile=tile: e.tensor_copy(scr[:], tile[:]),
                           wait=[("ch", P.ch())])
                    P.store("sync", dbg_d[nm], scr[:], wait=[("ch", P.ch())])
            chunk_end[j] = P.ch()
            if j + 2 < NCH:
                ld_q[j + 2] = load_chunk(j + 2, chunk_end[j])
        P.finish()
    return nc


def _scan0_inputs(ins, core):
    b, g = core // 2, core % 2
    C = 128
    s = np.arange(C)[:, None]
    t = np.arange(C)[None, :]
    ltri = (s <= t).astype(np.float32)
    mneg = np.where(s <= t, 0.0, -1e30).astype(np.float32)
    idf = np.eye(128, dtype=np.float32)
    idb = np.eye(128).astype(BF16NP)
    eh8 = np.zeros((8, 8 * 128), np.float32)
    for h in range(8):
        eh8[h, 128 * h:128 * (h + 1)] = 1.0
    nw8 = np.tile(np.tile(ins["L0_attn_norm_w"], 8)[None, :].astype(np.float32), (128, 1))
    return {
        "ltri": ltri, "mneg": mneg, "idf": idf, "idb": idb, "eh8": eh8,
        "onesr": np.ones((1, 128), np.float32), "nw8": np.ascontiguousarray(nw8),
    }


def sb_ap(tile, row_len, part_count, free_dims, offset_cols=0, part0=0):
    base = tile[:]
    return bass.AP(tensor=base.tensor, offset=part0 * row_len + offset_cols,
                   ap=[[row_len, part_count]] + [[s, c] for (s, c) in free_dims])


def build_ufold(nc=None, semctx=None, fused=False, stop_after=None):
    standalone = nc is None
    if standalone:
        nc = bass.Bass("TRN2", target_bir_lowering=False, debug=False)
    dr = DramHelper(nc, fused)
    din = {}
    for nm, shp, dt in [
        ("l_self_up", [7, 4096], F32), ("l_part_up", [7, 4096], F32),
        ("l_self_dn", [7, 4096], F32), ("l_part_dn", [7, 4096], F32),
        ("h_self_up", [128, 5 * 1024], F32), ("h_part_up", [128, 5 * 1024], F32),
        ("h_self_dn", [128, 5 * 1024], F32), ("h_part_dn", [128, 5 * 1024], F32),
        ("idfu", [128, 128], F32), ("idbu", [128, 128], BF16),
        ("onesrb", [1, 128], F32), ("selu", [128, 8 * 128], F32),
        ("wexpT_h", [8, 32 * 128 * 128], F32), ("bexp_t", [128, 32], F32),
        ("upbias", [1, 512], F32), ("wcon_sl", [32, 128 * 128], F32),
    ]:
        din[nm] = dr(nm, shp, dt, "ExternalInput", host=True)
    wexpf_d = dr("wexpf", [128, 8 * 512], BF16, "ExternalOutput")
    bupf_d = dr("bupf", [128, 4], F32, "ExternalOutput")
    wconf_d = dr("wconf", [128, 32 * 128], BF16, "ExternalOutput")

    with ExitStack() as ctx:
        sb = lambda *a: ctx.enter_context(nc.sbuf_tensor(*a, allow_name_mangling=True))
        pst = lambda *a: ctx.enter_context(nc.psum_tensor(*a, allow_name_mangling=True))
        idf = sb("idf_s", [128, 128], F32)
        idb = sb("idb_s", [128, 128], BF16)
        onesrb = sb("onesrb_s", [1, 128], BF16)
        selu = sb("selu_s", [128, 8 * 128], BF16)
        coefHs = sb("coefHs", [128, 5 * 1024], BF16)
        coefHp = sb("coefHp", [128, 5 * 1024], BF16)
        selfrow = sb("selfrow", [1, 4096], BF16)
        partrow = sb("partrow", [1, 4096], BF16)
        Lts = [sb(f"Lts{i}", [128, 32 * 128], F32) for i in range(2)]
        Lnon = sb("Lnon", [128, 32 * 128], F32)
        MhA = [sb(f"MhA{i}", [128, 1024], F32) for i in range(2)]
        scr512 = sb("scr512", [128, 512], F32)
        t1g = sb("t1g", [128, 1024], F32)
        MhT = sb("MhT", [128, 8 * 128], BF16)
        Mcols = sb("Mcols", [128, 128], F32)
        tmpU = sb("tmpU", [128, 128], BF16)
        UT = sb("UT", [128, 32 * 512], BF16)
        lhsS = [sb(f"lhsS{i}", [128, 32 * 128], BF16) for i in range(2)]

        outW = sb("outW", [128, 512], BF16)
        outB = sb("outB", [1, 512], F32)
        upb = sb("upb_s", [1, 512], F32)
        bexp = sb("bexp_s", [128, 32], BF16)
        psCf = pst("psCf", [128, 1024], F32)
        psCf2 = pst("psCf2", [128, 1024], F32)
        psF = pst("psF", [128, 512], F32)
        psT = pst("psT", [128, 512], BF16)
        psM = pst("psM", [128, 512], F32)
        psBias = pst("psBias", [1, 512], F32)

        P = Phase(nc, ctx, semctx)
        P.load("sync", idf[:], din["idfu"])
        P.load("sync", upb[:], din["upbias"])
        P.load("gpsimd", coefHs[:], din["h_self_up"])
        lds0 = P.load("gpsimd", coefHp[:], din["h_part_up"], sem="ld")
        P.load("gpsimd", idb[:], din["idbu"])
        P.load("gpsimd", onesrb[:], din["onesrb"])
        P.load("gpsimd", selu[:], din["selu"])
        P.load("gpsimd", bexp[:], din["bexp_t"])

        LROW = 32 * 128
        MROW = 1024

        def build_bfly(l_self_d, l_part_d, hs_gate):
            def vinit(e):
                for b in range(32):
                    e.tensor_copy(Lts[0][:, 128 * b:128 * (b + 1)], idf[:])
                e.memset(MhA[0][:], 0.0)
                last = None
                for b in range(32):
                    last = e.memset(MhA[0][:, 33 * b:33 * b + 1], 1.0)
                return last
            P.step("vector", vinit, wait=[("ld", lds0), ("ch", P.ch())])
            par = 0
            for s in range(7):
                sg = 1 << s
                lr1 = P.load("gpsimd", selfrow[:], rep_ap(l_self_d, [(4096, 1)], [(1, 4096)], 4096 * s),
                             wait=[("ch", P.ch())])
                lr2 = P.load("gpsimd", partrow[:], rep_ap(l_part_d, [(4096, 1)], [(1, 4096)], 4096 * s),
                             wait=[("ch", P.ch())])
                for cg in range(4):
                    def t_bc(e, cg=cg):
                        for half in range(2):
                            e.matmul(psCf[:, 512 * half:512 * (half + 1)], onesrb[0:1, :],
                                     selfrow[0:1, 1024 * cg + 512 * half: 1024 * cg + 512 * (half + 1)],
                                     start=True, stop=True)
                        last = None
                        for half in range(2):
                            last = e.matmul(psCf2[:, 512 * half:512 * (half + 1)], onesrb[0:1, :],
                                            partrow[0:1, 1024 * cg + 512 * half: 1024 * cg + 512 * (half + 1)],
                                            start=True, stop=True)
                        return last
                    P.step("tensor", t_bc, wait=[("lg", lr2), ("ch", P.ch())])
                    def v_stage(e, cg=cg, s=s, sg=sg, par=par):
                        OLD = Lts[par]
                        NEW = Lts[1 - par]
                        last = None
                        for bl in range(8):
                            b = 8 * cg + bl
                            off = 128 * b
                            cf = 128 * bl
                            e.tensor_tensor(t1g[:, 128 * bl:128 * (bl + 1)],
                                            OLD[:, off:off + 128], psCf[:, cf:cf + 128], op=ALU.mult)
                            for a in range(2):
                                dsl = sb_ap(Lts[par], LROW, 128, [(2 * sg, 64 // sg), (1, sg)],
                                            off + (1 - a) * sg)
                                csl = bass.AP(tensor=psCf2[:].tensor, offset=psCf2[:].offset + cf + a * sg,
                                              ap=[[1024, 128], [2 * sg, 64 // sg], [1, sg]])
                                e.tensor_tensor(scr512[:, 0:64], dsl, csl, op=ALU.mult)
                                t1sl = sb_ap(t1g, 1024, 128, [(2 * sg, 64 // sg), (1, sg)], 128 * bl + a * sg)
                                nsl = sb_ap(Lts[1 - par], LROW, 128, [(2 * sg, 64 // sg), (1, sg)],
                                            off + a * sg)
                                last = e.tensor_tensor(nsl, t1sl, scr512[:, 0:64], op=ALU.add)
                        return last
                    P.step("vector", v_stage, wait=[("ch", P.ch())])
                par = 1 - par
            lpar = par
            mpar = 0
            for s in range(5):
                sgb = 1 << s
                w = 32 * sgb
                def v_mh(e, s=s, sgb=sgb, w=w, mpar=mpar):
                    MOLD, MNEW = MhA[mpar], MhA[1 - mpar]
                    e.tensor_tensor(t1g[:], MOLD[:], coefHs[:, 1024 * s: 1024 * (s + 1)] if hs_gate is None
                                    else coefHs[:, 1024 * s: 1024 * (s + 1)], op=ALU.mult)
                    last = None
                    for a in range(2):
                        dsl = sb_ap(MhA[mpar], MROW, 128, [(2 * w, 16 // sgb), (1, w)], (1 - a) * w)
                        csl = sb_ap(coefHp, 5 * 1024, 128, [(2 * w, 16 // sgb), (1, w)], 1024 * s + a * w)
                        e.tensor_tensor(scr512[:], dsl, csl, op=ALU.mult)
                        t1sl = sb_ap(t1g, 1024, 128, [(2 * w, 16 // sgb), (1, w)], a * w)
                        nsl = sb_ap(MhA[1 - mpar], MROW, 128, [(2 * w, 16 // sgb), (1, w)], a * w)
                        last = e.tensor_tensor(nsl, t1sl, scr512[:], op=ALU.add)
                    return last
                P.step("vector", v_mh, wait=[("ch", P.ch())])
                mpar = 1 - mpar
            for grp in range(8):
                def t_tr(e, grp=grp, lpar=lpar):
                    last = None
                    for i in range(4):
                        b = 4 * grp + i
                        last = e.transpose(psF[:, 128 * i:128 * (i + 1)],
                                          Lts[lpar][:, 128 * b:128 * (b + 1)], idf[:])
                    return last
                P.step("tensor", t_tr, wait=[("ch", P.ch())])
                P.step("vector", lambda e, grp=grp: e.tensor_copy(Lnon[:, 512 * grp:512 * (grp + 1)], psF[:]),
                       wait=[("ch", P.ch())])
            return mpar

        if stop_after == "A":
            P.finish()
            return nc
        mpar_up = build_bfly(din["l_self_up"], din["l_part_up"], None)
        if stop_after == "B":
            P.finish()
            return nc
        for grp in range(2):
            def t_mt(e, grp=grp):
                last = None
                for i in range(4):
                    cidx = 4 * grp + i
                    last = e.transpose(psF[:, 128 * i:128 * (i + 1)],
                                      MhA[mpar_up][:, 128 * cidx:128 * (cidx + 1)], idf[:])
                return last
            P.step("tensor", t_mt, wait=[("ch", P.ch())])
            P.step("vector", lambda e, grp=grp: e.tensor_copy(MhT[:, 512 * grp:512 * (grp + 1)], psF[:]),
                   wait=[("ch", P.ch())])
        if stop_after == "C":
            P.finish()
            return nc
        def t_mc(e):
            last = None
            for pc in range(8):
                last = e.matmul(psF[:, 0:128], MhT[:, 128 * pc:128 * (pc + 1)],
                                selu[:, 128 * pc:128 * (pc + 1)], start=(pc == 0), stop=(pc == 7))
            return last
        P.step("tensor", t_mc, wait=[("ch", P.ch())])
        P.step("vector", lambda e: e.tensor_copy(Mcols[:], psF[:, 0:128]), wait=[("ch", P.ch())])
        if stop_after == "D":
            P.finish()
            return nc
        tmpU4 = sb("tmpU4", [128, 4 * 128], BF16)
        for bp in range(32):
            def v_ts(e, bp=bp):
                last = None
                for bb in range(4):
                    last = e.tensor_scalar(tmpU4[:, 128 * bb:128 * (bb + 1)],
                                           Lnon[:, 128 * bp:128 * (bp + 1)],
                                           Mcols[:, 32 * bb + bp:32 * bb + bp + 1], None, op0=ALU.mult)
                return last
            P.step("vector", v_ts, wait=[("ch", P.ch())])
            def t_tr4(e, bp=bp):
                last = None
                for bb in range(4):
                    last = e.transpose(psT[:, 128 * bb:128 * (bb + 1)],
                                      tmpU4[:, 128 * bb:128 * (bb + 1)], idb[:])
                return last
            P.step("tensor", t_tr4, wait=[("ch", P.ch())])
            P.step("vector", lambda e, bp=bp: e.tensor_copy(UT[:, 512 * bp:512 * (bp + 1)], psT[:]),
                   wait=[("ch", P.ch())])
        if stop_after == "E":
            P.finish()
            return nc
        lw = {0: P.load("gpsimd", lhsS[0][:], rep_ap(din["wexpT_h"], [(128, 128)], [(128 * 128, 32), (1, 128)], 0),
                        wait=[("ch", P.ch())])}
        for ic in range(8):
            if ic + 1 < 8:
                lw[ic + 1] = P.load("gpsimd", lhsS[(ic + 1) % 2][:],
                                    rep_ap(din["wexpT_h"], [(128, 128)], [(128 * 128, 32), (1, 128)],
                                           (ic + 1) * 32 * 128 * 128),
                                    wait=[("ch", P.ch() if ic == 0 else fold_done[ic - 1])])
            def t_fold(e, ic=ic):
                last = None
                for jc in range(32):
                    last = e.matmul(psM[:], lhsS[ic % 2][:, 128 * jc:128 * (jc + 1)],
                                    UT[:, 512 * jc:512 * (jc + 1)], start=(jc == 0), stop=(jc == 31))
                return last
            P.step("tensor", t_fold, wait=[("lg", lw[ic]), ("ch", P.ch())])
            P.step("vector", lambda e: e.tensor_copy(outW[:], psM[:]), wait=[("ch", P.ch())])
            if ic == 0:
                fold_done = {}
            fold_done[ic] = P.ch()
            P.store("sync", rep_ap(wexpf_d, [(8 * 512, 128)], [(1, 512)], ic * 512), outW[:],
                    wait=[("ch", fold_done[ic])])
        if stop_after == "F":
            P.finish()
            return nc
        def t_bias(e):
            last = None
            for jc in range(32):
                last = e.matmul(psBias[:], bexp[:, jc:jc + 1], UT[:, 512 * jc:512 * (jc + 1)],
                                start=(jc == 0), stop=(jc == 31))
            return last
        P.step("tensor", t_bias, wait=[("ch", P.ch())])
        P.step("vector", lambda e: e.tensor_tensor(outB[:], psBias[:], upb[:], op=ALU.add),
               wait=[("ch", P.ch())])
        bupscr_d = dr("bup_scr", [1, 512], F32, "Internal")
        P.store("sync", bupscr_d, outB[:], wait=[("ch", P.ch())])
        with nc.allow_non_contiguous_dma(reason="tiny 512-elem bias relayout"):
            P.store("sync", bupf_d, rep_ap(bupscr_d, [(1, 128)], [(128, 4)]),
                    wait=[("st", P.cnt["st"])])

        if stop_after == "G":
            P.finish()
            return nc
        gate = P.ch()
        P.load("gpsimd", coefHs[:], din["h_self_dn"], wait=[("ch", gate)])
        lds1 = P.load("gpsimd", coefHp[:], din["h_part_dn"], sem="ld", wait=[("ch", gate)])
        lgw2 = P.load("gpsimd", UT[:, 0:32 * 128],
                      rep_ap(din["wcon_sl"], [(128, 128)], [(128 * 128, 32), (1, 128)]),
                      wait=[("ch", gate)])
        P.step("gpsimd", lambda e: e.nop(), wait=[("lg", lgw2), ("ld", lds1)])
        mpar_dn = build_bfly(din["l_self_dn"], din["l_part_dn"], None)
        if stop_after == "H":
            P.finish()
            return nc
        for mc in range(32):
            def v_dnl(e, mc=mc):
                last = None
                for jc in range(32):
                    last = e.tensor_scalar(lhsS[0][:, 128 * jc:128 * (jc + 1)],
                                           Lnon[:, 128 * mc:128 * (mc + 1)],
                                           MhA[mpar_dn][:, 32 * jc + mc:32 * jc + mc + 1], None, op0=ALU.mult)
                return last
            P.step("vector", v_dnl, wait=[("ch", P.ch())])
            def t_dfold(e, mc=mc):
                last = None
                for jc in range(32):
                    last = e.matmul(psM[:, 0:128], lhsS[0][:, 128 * jc:128 * (jc + 1)],
                                    UT[:, 128 * jc:128 * (jc + 1)], start=(jc == 0), stop=(jc == 31))
                return last
            P.step("tensor", t_dfold, wait=[("ch", P.ch())])
            P.step("vector", lambda e: e.tensor_copy(outW[:, 0:128], psM[:, 0:128]), wait=[("ch", P.ch())])
            P.store("sync", rep_ap(wconf_d, [(32 * 128, 128)], [(1, 128)], mc * 128), outW[:, 0:128],
                    wait=[("ch", P.ch())])
            if stop_after == "I":
                break
        P.finish()
    return nc


def _bfly_coefs(weights):
    l_self = np.zeros((7, 4096), np.float32)
    l_part = np.zeros((7, 4096), np.float32)
    for s in range(7):
        srt = 1 << s
        nb = 4096 // (2 * srt)
        w = weights[s].reshape(nb, srt, 2, 2)
        i = np.arange(4096)
        q = i // (2 * srt)
        a = (i // srt) % 2
        lo = i % srt
        l_self[s] = w[q, lo, a, a]
        l_part[s] = w[q, lo, a, 1 - a]
    h_self = np.zeros((5, 128, 32), np.float32)
    h_part = np.zeros((5, 128, 32), np.float32)
    for s5 in range(5):
        s = 7 + s5
        srt = 1 << s
        nb = 4096 // (2 * srt)
        w = weights[s].reshape(nb, srt, 2, 2)
        i = np.arange(4096)
        q = i // (2 * srt)
        a = (i // srt) % 2
        lo = i % srt
        sc = w[q, lo, a, a].reshape(32, 128)
        pc = w[q, lo, a, 1 - a].reshape(32, 128)
        h_self[s5] = sc.T
        h_part[s5] = pc.T
    hs = np.repeat(h_self.transpose(1, 0, 2).reshape(128, 5 * 32), 32, axis=1)
    hp = np.repeat(h_part.transpose(1, 0, 2).reshape(128, 5 * 32), 32, axis=1)
    return l_self, l_part, np.ascontiguousarray(hs), np.ascontiguousarray(hp)


def _ufold_inputs(ins, core):
    c = core
    lsu, lpu, hsu, hpu = _bfly_coefs(ins["L0_ffn_up_weights"])
    lsd, lpd, hsd, hpd = _bfly_coefs(ins["L0_ffn_down_weights"])
    sel = np.zeros((1024, 128), np.float32)
    for col in range(128):
        bb, bp = col // 32, col % 32
        sel[32 * (4 * c + bb) + bp, col] = 1.0
    Wexp = ins["L0_ffn_W_exp"]
    wexpT = np.ascontiguousarray(
        Wexp.T.reshape(32, 128, 8, 128).transpose(2, 0, 1, 3).reshape(8, 32 * 128 * 128))
    bexp_t = np.ascontiguousarray(ins["L0_ffn_b_exp"].reshape(32, 128).T)
    Wcon = ins["L0_ffn_W_con"]
    wcon_sl = np.ascontiguousarray(
        Wcon[:, 128 * c:128 * (c + 1)].reshape(32, 128 * 128))
    return {
        "l_self_up": lsu, "l_part_up": lpu, "l_self_dn": lsd, "l_part_dn": lpd,
        "h_self_up": hsu, "h_part_up": hpu, "h_self_dn": hsd, "h_part_dn": hpd,
        "idfu": np.eye(128, dtype=np.float32), "idbu": np.eye(128).astype(BF16NP),
        "onesrb": np.ones((1, 128), np.float32),
        "selu": np.ascontiguousarray(sel.reshape(8, 128, 128).transpose(1, 0, 2).reshape(128, 8 * 128)),
        "wexpT_h": wexpT, "bexp_t": bexp_t,
        "upbias": np.ascontiguousarray(ins["L0_ffn_up_bias"][512 * c:512 * (c + 1)].reshape(1, 512)),
        "wcon_sl": wcon_sl,
    }


def build_l0tail(nc=None, semctx=None, fused=False):
    NTOK = 2048
    NT4 = NTOK // TT
    standalone = nc is None
    if standalone:
        nc = bass.Bass("TRN2", target_bir_lowering=False, debug=False)
    dr = DramHelper(nc, fused)
    opair_d = dr("o_pair", [2 * T, 512], BF16, "ExternalInput")
    oidx_d = dr("oidx", [128, 32], mybir.dt.int32, "ExternalInput", host=True)
    x_d = dr("x_my", [D, NTOK], F32, "ExternalInput", host=True)
    wo_d = dr("wo_t", [128, 8 * 1024], F32, "ExternalInput", host=True)
    wexpA_d = dr("wexpfA", [1024, 4096], BF16, "ExternalInput")
    wconA_d = dr("wconfA", [1024, 4096], BF16, "ExternalInput")
    bupA_d = dr("bupfA", [1024, 4], F32, "ExternalInput")
    bcon_d = dr("bcon_t", [128, 8], F32, "ExternalInput", host=True)
    db_d = dr("db_t", [128, 8], F32, "ExternalInput", host=True)
    idb_d = dr("idb2", [128, 128], BF16, "ExternalInput", host=True)
    onesc_d = dr("onesc2", [128, 1], F32, "ExternalInput", host=True)
    onesr_d = dr("onesr2", [1, 128], F32, "ExternalInput", host=True)

    x1_d = dr("x1_my", [D, NTOK], F32, "ExternalOutput")
    xn1_d = dr("xn1_my", [D, NTOK], BF16, "ExternalOutput")

    with ExitStack() as ctx:
        sb = lambda *a: ctx.enter_context(nc.sbuf_tensor(*a, allow_name_mangling=True))
        pst = lambda *a: ctx.enter_context(nc.psum_tensor(*a, allow_name_mangling=True))
        wo_s = sb("wo_s", [128, 8 * 1024], BF16)
        idb = sb("idb_s", [128, 128], BF16)
        onesc = sb("onesc_s", [128, 1], BF16)
        onesr = sb("onesr_s", [1, 128], F32)
        oidx_s = sb("oidx_s", [128, 32], mybir.dt.int32)
        bup = sb("bup_s", [128, 32], F32)
        bfin = sb("bfin_s", [128, 8], F32)
        dbv = sb("dbv_s", [128, 8], F32)
        otmA = [sb(f"otmA{i}", [128, 4 * 512], BF16) for i in range(2)]
        otmB = [sb(f"otmB{i}", [128, 4 * 512], BF16) for i in range(2)]
        ofm = sb("ofm", [128, 8 * TT], BF16)
        xnt = sb("xnt", [128, 8 * TT], BF16)
        xmt = [sb(f"xmt{i}", [128, 8 * TT], F32) for i in range(2)]
        wexpS = [sb(f"wexpS{i}", [128, 8 * 128], BF16) for i in range(2)]
        wconS = [sb(f"wconS{i}", [128, 32 * 128], BF16) for i in range(2)]
        hbuf = sb("hbuf", [128, 32 * TT], BF16)
        x1f = sb("x1f", [128, 8 * TT], F32)
        x1bf = sb("x1bf", [128, 8 * TT], BF16)
        x1sq = sb("x1sq", [128, 8 * TT], BF16)
        lnscr = sb("lnscr", [128, TT], F32)
        stat = sb("stat", [1, 6 * TT], F32)
        xn1t = sb("xn1t", [128, 8 * TT], BF16)
        psA = pst("psA", [128, TT], F32)
        psH = pst("psH", [128, TT], F32)
        psT = pst("psT", [128, TT], BF16)
        ps_sum = pst("ps_sum", [1, TT], F32)
        ps_sq = pst("ps_sq", [1, TT], F32)
        ps_b1 = pst("ps_b1", [128, TT], F32)
        ps_b2 = pst("ps_b2", [128, TT], F32)

        cpool, cemit = setup_consts(nc, (semctx or ctx), [1e-5, -0.5], uid='l0')
        P = Phase(nc, ctx, semctx)
        P.step("gpsimd", cemit)
        P.load("sync", bup[:], rep_ap(bupA_d, [(4, 128)], [(512, 8), (1, 4)]))
        P.load("sync", bfin[:], bcon_d)
        P.load("sync", onesr[:], onesr_d)
        lds = P.load("sync", dbv[:], db_d)
        P.load("gpsimd", wo_s[:], wo_d)
        P.load("gpsimd", idb[:], idb_d)
        P.load("gpsimd", oidx_s[:], oidx_d)
        lgc = P.load("gpsimd", onesc[:], onesc_d)
        P.step("vector", lambda e: e.tensor_tensor(bfin[:], bfin[:], dbv[:], op=ALU.add),
               wait=[("ld", lds), ("ch", P.ch())])

        def emit_ln(src_f32, out_bf, gate_tick, extra_wait=()):
            def s_sq(e):
                e.activation(x1sq[:], src_f32[:], AF.Square)
                return e.activation(x1bf[:], src_f32[:], AF.Copy)
            P.step("scalar", s_sq, wait=[("ch", gate_tick)] + list(extra_wait))
            def mm_sums(e):
                for kc in range(8):
                    e.matmul(ps_sum[:], onesc[:, 0:1], x1bf[:, kc * TT:(kc + 1) * TT],
                             start=(kc == 0), stop=(kc == 7))
                last = None
                for kc in range(8):
                    last = e.matmul(ps_sq[:], onesc[:, 0:1], x1sq[:, kc * TT:(kc + 1) * TT],
                                    start=(kc == 0), stop=(kc == 7))
                return last
            P.step("tensor", mm_sums, wait=[("ch", P.ch())])
            def v_stats(e):
                e.tensor_scalar(stat[:, 0:TT], ps_sum[:], 1.0 / D, None, op0=ALU.mult)
                e.tensor_scalar(stat[:, TT:2 * TT], ps_sq[:], 1.0 / D, None, op0=ALU.mult)
                e.tensor_tensor(stat[:, 2 * TT:3 * TT], stat[:, 0:TT], stat[:, 0:TT], op=ALU.mult)
                return e.tensor_tensor(stat[:, 3 * TT:4 * TT], stat[:, TT:2 * TT],
                                       stat[:, 2 * TT:3 * TT], op=ALU.subtract)
            P.step("vector", v_stats, wait=[("ch", P.ch())])
            def s_rstd(e):
                e.activation(stat[:, 4 * TT:5 * TT], stat[:, 3 * TT:4 * TT], AF.Ln, bias=1e-5)
                return e.activation(stat[:, 4 * TT:5 * TT], stat[:, 4 * TT:5 * TT], AF.Exp, scale=-0.5)
            P.step("scalar", s_rstd, wait=[("ch", P.ch())])
            def mm_bc(e):
                e.matmul(ps_b1[:], onesr[0:1, :], stat[0:1, 0:TT], start=True, stop=True)
                return e.matmul(ps_b2[:], onesr[0:1, :], stat[0:1, 4 * TT:5 * TT], start=True, stop=True)
            P.step("tensor", mm_bc, wait=[("ch", P.ch())])
            def v_xn(e):
                last = None
                for kc in range(8):
                    e.tensor_tensor(lnscr[:], src_f32[:, kc * TT:(kc + 1) * TT], ps_b1[:], op=ALU.subtract)
                    last = e.tensor_tensor(out_bf[:, kc * TT:(kc + 1) * TT], lnscr[:], ps_b2[:], op=ALU.mult)
                return last
            P.step("vector", v_xn, wait=[("ch", P.ch())])

        ldq = {}
        def load_tile(i, gate):
            t0 = i * TT
            for tc in range(4):
                for half in range(2):
                    dst = (otmA if half == 0 else otmB)[i % 2]
                    P.load_idx("gpsimd", dst[:, tc * 512:(tc + 1) * 512], opair_d,
                               oidx_s[:, i * 8 + tc * 2 + half: i * 8 + tc * 2 + half + 1],
                               wait=[("ch", gate), ("lg", lgc)])
            lgo = P.cnt["lg"]
            ldv = P.load("sync", xmt[i % 2][:], rep_ap(x_d, [(NTOK, 128)], [(128 * NTOK, 8), (1, TT)], t0),
                         wait=[("ch", gate)])
            return ldv, lgo

        ldq[0] = load_tile(0, 0)
        ldq[1] = load_tile(1, 0)
        tile_end = {}
        for i in range(NT4):
            t0 = i * TT
            par = i % 2
            ldv, lgo = ldq[i]
            emit_ln(xmt[par], xnt, P.ch(), extra_wait=[("ld", ldv)])
            for kc in range(8):
                def t_tr(e, kc=kc, par=par):
                    last = None
                    src = otmA[par] if kc < 4 else otmB[par]
                    kk = kc % 4
                    for tc in range(4):
                        last = e.transpose(psT[:, 128 * tc:128 * (tc + 1)],
                                          src[:, tc * 512 + 128 * kk: tc * 512 + 128 * (kk + 1)],
                                          idb[:])
                    return last
                P.step("tensor", t_tr, wait=[("lg", lgo), ("ch", P.ch())])
                P.step("vector", lambda e, kc=kc: e.tensor_copy(ofm[:, TT * kc:TT * (kc + 1)], psT[:]),
                       wait=[("ch", P.ch())])
            for oc in range(8):
                def t_wo(e, oc=oc):
                    last = None
                    for kc in range(8):
                        last = e.matmul(psA[:], wo_s[:, kc * 1024 + oc * 128: kc * 1024 + (oc + 1) * 128],
                                        ofm[:, TT * kc:TT * (kc + 1)], start=(kc == 0), stop=(kc == 7))
                    return last
                P.step("tensor", t_wo, wait=[("ch", P.ch())])
                P.step("vector", lambda e, oc=oc, par=par: e.tensor_tensor(
                    x1f[:, TT * oc:TT * (oc + 1)], xmt[par][:, TT * oc:TT * (oc + 1)], psA[:], op=ALU.add),
                    wait=[("ch", P.ch())])
            lwx = {0: P.load("gpsimd", wexpS[0][:],
                             rep_ap(wexpA_d, [(4096, 128)], [(512, 8), (1, 128)], 0),
                             wait=[("ch", P.ch())])}
            for hc in range(32):
                if hc + 1 < 32:
                    hc2 = hc + 1
                    off2 = 128 * (hc2 // 4) * 4096 + (hc2 % 4) * 128
                    lwx[hc2] = P.load("gpsimd", wexpS[hc2 % 2][:],
                                      rep_ap(wexpA_d, [(4096, 128)], [(512, 8), (1, 128)], off2),
                                      wait=[("ch", P.ch() if hc == 0 else hdone[hc - 1])])
                def t_h(e, hc=hc):
                    last = None
                    for kc in range(8):
                        last = e.matmul(psH[:], wexpS[hc % 2][:, kc * 128:(kc + 1) * 128],
                                        xnt[:, TT * kc:TT * (kc + 1)], start=(kc == 0), stop=(kc == 7))
                    return last
                P.step("tensor", t_h, wait=[("lg", lwx[hc]), ("ch", P.ch())])
                if hc == 0:
                    hdone = {}
                P.step("scalar", lambda e, hc=hc: e.activation(
                    hbuf[:, TT * hc:TT * (hc + 1)], psH[:], AF.Gelu, bias=bup[:, hc:hc + 1]),
                    wait=[("ch", P.ch())])
                hdone[hc] = P.ch()
            lwc = {0: P.load("gpsimd", wconS[0][:],
                             rep_ap(wconA_d, [(4096, 128)], [(128, 32), (1, 128)], 0),
                             wait=[("ch", P.ch())])}
            for oc in range(8):
                if oc + 1 < 8:
                    lwc[oc + 1] = P.load("gpsimd", wconS[(oc + 1) % 2][:],
                                         rep_ap(wconA_d, [(4096, 128)], [(128, 32), (1, 128)],
                                                128 * (oc + 1) * 4096),
                                         wait=[("ch", P.ch() if oc == 0 else ydone[oc - 1])])
                def t_y(e, oc=oc):
                    last = None
                    for mc in range(32):
                        last = e.matmul(psH[:], wconS[oc % 2][:, mc * 128:(mc + 1) * 128],
                                        hbuf[:, TT * mc:TT * (mc + 1)], start=(mc == 0), stop=(mc == 31))
                    return last
                P.step("tensor", t_y, wait=[("lg", lwc[oc]), ("ch", P.ch())])
                if oc == 0:
                    ydone = {}
                P.step("vector", lambda e, oc=oc: e.scalar_tensor_tensor(
                    x1f[:, TT * oc:TT * (oc + 1)], psH[:], bfin[:, oc:oc + 1],
                    x1f[:, TT * oc:TT * (oc + 1)], op0=ALU.add, op1=ALU.add),
                    wait=[("ch", P.ch())])
                ydone[oc] = P.ch()
            x1done = P.ch()
            P.store("sync", rep_ap(x1_d, [(NTOK, 128)], [(128 * NTOK, 8), (1, TT)], t0), x1f[:],
                    wait=[("ch", x1done)])
            emit_ln(x1f, xn1t, P.ch())
            P.store("sync", rep_ap(xn1_d, [(NTOK, 128)], [(128 * NTOK, 8), (1, TT)], t0), xn1t[:],
                    wait=[("ch", P.ch())])
            tile_end[i] = P.ch()
            if i + 2 < NT4:
                ldq[i + 2] = load_tile(i + 2, tile_end[i])
        P.finish()
    if standalone:
        return nc


def _l0tail_inputs(ins, core, o_pair_b, wexpfA, bupfA, wconfA):
    b, g = core // 2, core % 2
    t0 = 2048 * g
    x_my = np.ascontiguousarray(ins["L0_attn_x"][b].T[:, t0:t0 + 2048])
    Wo = ins["L0_attn_Wo"]
    wo_t = np.ascontiguousarray(Wo.reshape(8, 128, 1024).transpose(1, 0, 2).reshape(128, 8 * 1024))
    oidx = np.zeros((128, 32), np.int32)
    p = np.arange(128)
    for i in range(4):
        for tc in range(4):
            for half in range(2):
                oidx[:, i * 8 + tc * 2 + half] = half * 4096 + 2048 * g + i * 512 + tc * 128 + p
    return {
        "o_pair": o_pair_b, "oidx": oidx, "x_my": x_my, "wo_t": wo_t,
        "wexpfA": wexpfA, "wconfA": wconfA, "bupfA": bupfA,
        "bcon_t": np.ascontiguousarray(ins["L0_ffn_b_con"].reshape(8, 128).T),
        "db_t": np.ascontiguousarray(ins["L0_ffn_down_bias"].reshape(8, 128).T),
        "idb2": np.eye(128).astype(BF16NP),
        "onesc2": np.ones((128, 1), np.float32),
        "onesr2": np.ones((1, 128), np.float32),
    }


def build_pre1(nc=None, semctx=None, fused=False):
    standalone = nc is None
    if standalone:
        nc = bass.Bass("TRN2", target_bir_lowering=False, debug=False)
    dr = DramHelper(nc, fused)
    xn1_d = dr("xn1_pair", [2048, 2048], BF16, "ExternalInput")
    win_d = dr("w_in_t", [128, 8 * 2048], F32, "ExternalInput", host=True)
    cw3_d = dr("cw3", [1024, 3], F32, "ExternalInput", host=True)
    cb_d = dr("cb", [128, 8], F32, "ExternalInput", host=True)
    hf_d = dr("h_f", [1024, T], F32, "ExternalOutput")
    hbf_d = dr("h_bf", [1024, T], BF16, "ExternalOutput")
    sg_d = dr("sg", [1024, T], BF16, "ExternalOutput")
    pre_d = dr("pre_scr", [1024, T + 2], F32, "Internal")

    with ExitStack() as ctx:
        sb = lambda *a: ctx.enter_context(nc.sbuf_tensor(*a, allow_name_mangling=True))
        pst = lambda *a: ctx.enter_context(nc.psum_tensor(*a, allow_name_mangling=True))
        win_s = sb("win_s", [128, 8 * 2048], BF16)
        cw3_s = sb("cw3_s", [128, 8 * 3], F32)
        cb_s = sb("cb_s", [128, 8], F32)
        xnt = [sb(f"xnt{i}", [128, 8 * TT], BF16) for i in range(2)]
        innt = sb("innt", [128, 8 * TT], F32)
        sgt = sb("sgt", [128, 8 * TT], BF16)
        prw = [sb(f"prw{i}", [128, 8 * (TT + 2)], F32) for i in range(2)]
        convA = sb("convA", [128, TT], F32)
        convB = sb("convB", [128, TT], F32)
        ht = sb("ht", [128, 8 * TT], F32)
        htb = sb("htb", [128, 8 * TT], BF16)
        zcol = sb("zcol", [128, 8], F32)
        psI = pst("psI", [128, TT], F32)
        psG = pst("psG", [128, TT], F32)

        P = Phase(nc, ctx, semctx)
        P.load("sync", cw3_s[:], rep_ap(cw3_d, [(3, 128)], [(384, 8), (1, 3)]))
        lds = P.load("sync", cb_s[:], cb_d)
        lgw = P.load("gpsimd", win_s[:], win_d)
        P.step("gpsimd", lambda e: e.memset(zcol[:], 0.0))
        zdone = P.ch()
        with nc.allow_non_contiguous_dma(reason="one-time 1-col zero pad stores"):
            P.store("sync", rep_ap(pre_d, [(T + 2, 128)], [(128 * (T + 2), 8), (1, 1)], 0),
                    zcol[:], wait=[("ch", zdone)])
            P.store("sync", rep_ap(pre_d, [(T + 2, 128)], [(128 * (T + 2), 8), (1, 1)], T + 1),
                    zcol[:], wait=[("ch", zdone)])

        ldq = {}
        def load_xtile(i, gate):
            blk, tloc = i // 4, (i % 4) * TT
            return P.load("sync", xnt[i % 2][:],
                          rep_ap(xn1_d, [(2048, 128)], [(128 * 2048, 8), (1, TT)],
                                 blk * 1024 * 2048 + tloc),
                          wait=[("ch", gate)])
        ldq[0] = load_xtile(0, 0)
        ldq[1] = load_xtile(1, 0)
        tend = {}
        for i in range(NTT):
            t0 = i * TT
            par = i % 2
            for oc in range(8):
                def t_in(e, oc=oc, par=par):
                    last = None
                    for kc in range(8):
                        last = e.matmul(psI[:], win_s[:, kc * 2048 + oc * 128: kc * 2048 + (oc + 1) * 128],
                                        xnt[par][:, TT * kc:TT * (kc + 1)], start=(kc == 0), stop=(kc == 7))
                    return last
                P.step("tensor", t_in, wait=[("lg", lgw), ("ld", ldq[i]), ("ch", P.ch())])
                P.step("vector", lambda e, oc=oc: e.tensor_copy(innt[:, TT * oc:TT * (oc + 1)], psI[:]),
                       wait=[("ch", P.ch())])
            for oc in range(8):
                def t_g(e, oc=oc, par=par):
                    last = None
                    for kc in range(8):
                        last = e.matmul(psG[:], win_s[:, kc * 2048 + 1024 + oc * 128: kc * 2048 + 1024 + (oc + 1) * 128],
                                        xnt[par][:, TT * kc:TT * (kc + 1)], start=(kc == 0), stop=(kc == 7))
                    return last
                P.step("tensor", t_g, wait=[("ch", P.ch())])
                P.step("scalar", lambda e, oc=oc: e.activation(sgt[:, TT * oc:TT * (oc + 1)], psG[:], AF.Silu),
                       wait=[("ch", P.ch())])
            done = P.ch()
            P.store("sync", rep_ap(pre_d, [(T + 2, 128)], [(128 * (T + 2), 8), (1, TT)], 1 + t0),
                    innt[:], wait=[("ch", done)])
            P.store("sync", rep_ap(sg_d, [(T, 128)], [(128 * T, 8), (1, TT)], t0), sgt[:],
                    wait=[("ch", done)])
            tend[i] = P.ch()
            if i + 2 < NTT:
                ldq[i + 2] = load_xtile(i + 2, tend[i])
        allst = P.cnt["st"]
        lpq = {}
        def load_ptile(i, gate):
            return P.load("sync", prw[i % 2][:],
                          rep_ap(pre_d, [(T + 2, 128)], [(128 * (T + 2), 8), (1, TT + 2)], i * TT),
                          wait=[("st", allst), ("ch", gate)])
        lpq[0] = load_ptile(0, 0)
        lpq[1] = load_ptile(1, 0)
        for i in range(NTT):
            t0 = i * TT
            par = i % 2
            for oc in range(8):
                W = TT + 2
                def v_c(e, oc=oc, par=par, W=W):
                    e.tensor_scalar(convA[:], prw[par][:, oc * W: oc * W + TT],
                                    cw3_s[:, oc * 3: oc * 3 + 1], None, op0=ALU.mult)
                    e.scalar_tensor_tensor(convB[:], prw[par][:, oc * W + 1: oc * W + 1 + TT],
                                           cw3_s[:, oc * 3 + 1: oc * 3 + 2], convA[:],
                                           op0=ALU.mult, op1=ALU.add)
                    return e.scalar_tensor_tensor(convA[:], prw[par][:, oc * W + 2: oc * W + 2 + TT],
                                                  cw3_s[:, oc * 3 + 2: oc * 3 + 3], convB[:],
                                                  op0=ALU.mult, op1=ALU.add)
                P.step("vector", v_c, wait=[("ld", lpq[i]), ("ch", P.ch())])
                def s_h(e, oc=oc):
                    e.activation(ht[:, TT * oc:TT * (oc + 1)], convA[:], AF.Silu, bias=cb_s[:, oc:oc + 1])
                    return e.activation(htb[:, TT * oc:TT * (oc + 1)], convA[:], AF.Silu, bias=cb_s[:, oc:oc + 1])
                P.step("scalar", s_h, wait=[("ch", P.ch())])
            done = P.ch()
            P.store("sync", rep_ap(hf_d, [(T, 128)], [(128 * T, 8), (1, TT)], t0), ht[:], wait=[("ch", done)])
            P.store("sync", rep_ap(hbf_d, [(T, 128)], [(128 * T, 8), (1, TT)], t0), htb[:], wait=[("ch", done)])
            tend2 = P.ch()
            if i + 2 < NTT:
                lpq[i + 2] = load_ptile(i + 2, tend2)
        P.finish()
    return nc


def _pre1_inputs(ins, core, xn1_full_b):
    b, g = core // 2, core % 2
    Win = ins["L1_attn_W_in"]
    wi = Win[:, 1024 * g:1024 * (g + 1)]
    wg = Win[:, 2048 + 1024 * g:2048 + 1024 * (g + 1)]
    w = np.concatenate([wi, wg], axis=1)
    w_t = np.ascontiguousarray(w.reshape(8, 128, 2048).transpose(1, 0, 2).reshape(128, 8 * 2048))
    return {
        "xn1_pair": np.ascontiguousarray(xn1_full_b),
        "w_in_t": w_t,
        "cw3": np.ascontiguousarray(ins["L1_attn_conv_w"][1024 * g:1024 * (g + 1)]),
        "cb": np.ascontiguousarray(ins["L1_attn_conv_b"][1024 * g:1024 * (g + 1)].reshape(8, 128).T),
    }


def build_dt1(nc=None, semctx=None, fused=False):
    standalone = nc is None
    if standalone:
        nc = bass.Bass("TRN2", target_bir_lowering=False, debug=False)
    dr = DramHelper(nc, fused)
    hbf_d = dr("h_pair", [2048, T], BF16, "ExternalInput")
    wt_d = dr("wt_t", [128, 16 * 1024], F32, "ExternalInput", host=True)
    btc_d = dr("btc", [128, 8], F32, "ExternalInput", host=True)
    wsB_d = dr("wsB", [128, 16 * 16], F32, "ExternalInput", host=True)
    hf_d = dr("h_f", [1024, T], F32, "ExternalInput")
    dt_d = dr("dt_bf", [1024, T], BF16, "ExternalOutput")
    u_d = dr("u_bf", [1024, T], BF16, "ExternalOutput")
    B_d = dr("B_bf", [16, T], BF16, "ExternalOutput")

    with ExitStack() as ctx:
        sb = lambda *a: ctx.enter_context(nc.sbuf_tensor(*a, allow_name_mangling=True))
        pst = lambda *a: ctx.enter_context(nc.psum_tensor(*a, allow_name_mangling=True))
        wt_s = sb("wt_s", [128, 16 * 1024], BF16)
        wsB_s = sb("wsB_s", [128, 16 * 16], BF16)
        btc = sb("btc_s", [128, 8], F32)
        hbt = [sb(f"hbt{i}", [128, 16 * TT], BF16) for i in range(2)]
        hft = [sb(f"hft{i}", [128, 8 * TT], F32) for i in range(2)]
        spl = sb("spl", [128, 4 * TT], F32)
        dtt = sb("dtt", [128, 8 * TT], F32)
        dttb = sb("dttb", [128, 8 * TT], BF16)
        utb = sb("utb", [128, 8 * TT], BF16)
        Btb = sb("Btb", [16, TT], BF16)
        psD = pst("psD", [128, TT], F32)
        psB16 = pst("psB16", [16, TT], F32)

        cpool, cemit = setup_consts(nc, (semctx or ctx), [1.0, -1.0], uid='dx')
        P = Phase(nc, ctx, semctx)
        P.step("gpsimd", cemit)
        lds = P.load("sync", btc[:], btc_d)
        P.load("gpsimd", wt_s[:], wt_d)
        lgw = P.load("gpsimd", wsB_s[:], wsB_d)

        ldq = {}
        def load_tile(i, gate):
            l1 = P.load("sync", hbt[i % 2][:],
                        rep_ap(hbf_d, [(T, 128)], [(128 * T, 16), (1, TT)], i * TT),
                        wait=[("ch", gate)])
            return P.load("sync", hft[i % 2][:],
                          rep_ap(hf_d, [(T, 128)], [(128 * T, 8), (1, TT)], i * TT),
                          wait=[("ch", gate)])
        ldq[0] = load_tile(0, 0)
        ldq[1] = load_tile(1, 0)
        tend = {}
        for i in range(NTT):
            t0 = i * TT
            par = i % 2
            for oc in range(8):
                def t_dt(e, oc=oc, par=par):
                    last = None
                    for kc in range(16):
                        last = e.matmul(psD[:], wt_s[:, kc * 1024 + oc * 128: kc * 1024 + (oc + 1) * 128],
                                        hbt[par][:, TT * kc:TT * (kc + 1)], start=(kc == 0), stop=(kc == 15))
                    return last
                P.step("tensor", t_dt, wait=[("ld", ldq[i]), ("lg", lgw), ("ch", P.ch())])
                def s_sp(e, oc=oc):
                    e.activation(spl[:, 0:TT], psD[:], AF.Abs, bias=btc[:, oc:oc + 1])
                    e.activation(spl[:, TT:2 * TT], spl[:, 0:TT], AF.Exp, scale=-1.0)
                    e.activation(spl[:, 2 * TT:3 * TT], spl[:, TT:2 * TT], AF.Ln, bias=1.0)
                    return e.activation(spl[:, 3 * TT:4 * TT], psD[:], AF.Relu, bias=btc[:, oc:oc + 1])
                P.step("scalar", s_sp, wait=[("ch", P.ch())])
                def v_dt(e, oc=oc, par=par):
                    e.tensor_tensor(dtt[:, TT * oc:TT * (oc + 1)], spl[:, 2 * TT:3 * TT],
                                    spl[:, 3 * TT:4 * TT], op=ALU.add)
                    e.tensor_copy(dttb[:, TT * oc:TT * (oc + 1)], dtt[:, TT * oc:TT * (oc + 1)])
                    return e.tensor_tensor(utb[:, TT * oc:TT * (oc + 1)], dtt[:, TT * oc:TT * (oc + 1)],
                                           hft[par][:, TT * oc:TT * (oc + 1)], op=ALU.mult)
                P.step("vector", v_dt, wait=[("ch", P.ch())])
            def t_B(e, par=par):
                last = None
                for kc in range(16):
                    last = e.matmul(psB16[:], wsB_s[:, kc * 16:(kc + 1) * 16],
                                    hbt[par][:, TT * kc:TT * (kc + 1)], start=(kc == 0), stop=(kc == 15))
                return last
            P.step("tensor", t_B, wait=[("ch", P.ch())])
            P.step("vector", lambda e: e.tensor_copy(Btb[:], psB16[:]), wait=[("ch", P.ch())])
            done = P.ch()
            P.store("sync", rep_ap(dt_d, [(T, 128)], [(128 * T, 8), (1, TT)], t0), dttb[:], wait=[("ch", done)])
            P.store("sync", rep_ap(u_d, [(T, 128)], [(128 * T, 8), (1, TT)], t0), utb[:], wait=[("ch", done)])
            P.store("sync", rep_ap(B_d, [(T, 16)], [(1, TT)], t0), Btb[:], wait=[("ch", done)])
            tend[i] = P.ch()
            if i + 2 < NTT:
                ldq[i + 2] = load_tile(i + 2, tend[i])
        P.finish()
    return nc


def _dt1_inputs(ins, core, hbf_full_b, hf_my):
    b, g = core // 2, core % 2
    Wt = ins["L1_attn_W_time"]
    w = Wt[:, 1024 * g:1024 * (g + 1)]
    wt_t = np.ascontiguousarray(w.reshape(16, 128, 1024).transpose(1, 0, 2).reshape(128, 16 * 1024))
    Ws = ins["L1_attn_W_state"][:, 16:32]
    wsB = np.ascontiguousarray(Ws.reshape(16, 128, 16).transpose(1, 0, 2).reshape(128, 16 * 16))
    return {
        "h_pair": np.ascontiguousarray(hbf_full_b),
        "wt_t": wt_t,
        "btc": np.ascontiguousarray(ins["L1_attn_b_time"][1024 * g:1024 * (g + 1)].reshape(8, 128).T),
        "wsB": wsB,
        "h_f": np.ascontiguousarray(hf_my),
    }


def build_scan1(nc=None, semctx=None, fused=False):
    TS = 2048
    NG = 128
    standalone = nc is None
    if standalone:
        nc = bass.Bass("TRN2", target_bir_lowering=False, debug=False)
    dr = DramHelper(nc, fused)
    dt_d = dr("dt_bf", [1024, T], BF16, "ExternalInput")
    u_d = dr("u_bf", [1024, T], BF16, "ExternalInput")
    B_d = dr("B_bf", [16, T], BF16, "ExternalInput")
    hbf_d = dr("h_bf", [1024, T], BF16, "ExternalInput")
    sg_d = dr("sg", [1024, T], BF16, "ExternalInput")
    alog_d = dr("alog_my", [128, 128], F32, "ExternalInput", host=True)
    skd_d = dr("skipdiag", [8, 128 * 8], F32, "ExternalInput", host=True)
    s16_d = dr("s16", [128, 8], F32, "ExternalInput", host=True)
    z_d = dr("z_bf", [1024, T], BF16, "ExternalOutput")

    with ExitStack() as ctx:
        sb = lambda *a: ctx.enter_context(nc.sbuf_tensor(*a, allow_name_mangling=True))
        pst = lambda *a: ctx.enter_context(nc.psum_tensor(*a, allow_name_mangling=True))
        negA = sb("negA_s", [128, 128], F32)
        s16 = sb("s16_s", [128, 8], BF16)
        skd = sb("skd_s", [8, 128 * 8], BF16)
        NBUF = 3
        dtr = [sb(f"dtr{i}", [128, TS], BF16) for i in range(NBUF)]
        ur = [sb(f"ur{i}", [128, TS], BF16) for i in range(NBUF)]
        Brt = [sb(f"Brt{i}", [128, TS], BF16) for i in range(2)]
        hst = [sb(f"hst{i}", [8, TS], BF16) for i in range(NBUF)]
        sgt = [sb(f"sgt{i}", [8, TS], BF16) for i in range(NBUF)]
        dA = [sb(f"dA{i}", [128, TS], F32) for i in range(NBUF)]
        dB = [sb(f"dB{i}", [128, TS], BF16) for i in range(NBUF)]
        mem = [sb(f"mem{i}", [128, TS], F32) for i in range(NBUF)]
        membf = [sb(f"membf{i}", [128, TS], BF16) for i in range(NBUF)]
        zst = [sb(f"zst{i}", [8, TS], BF16) for i in range(NBUF)]
        psY = [pst(f"psY{i}", [8, TS], F32) for i in range(2)]

        P = Phase(nc, ctx, semctx)
        lda = P.load("sync", negA[:], alog_d)
        P.load("gpsimd", skd[:], skd_d)
        P.load("gpsimd", s16[:], s16_d)
        P.load("gpsimd", Brt[0][:], rep_ap(B_d, [(0, 8), (T, 16)], [(1, TS)], 0))
        lgs = P.load("gpsimd", Brt[1][:], rep_ap(B_d, [(0, 8), (T, 16)], [(1, TS)], TS))
        P.step("scalar", lambda e: e.activation(negA[:], negA[:], AF.Exp), wait=[("ld", lda), ("ch", P.ch())])
        P.step("vector", lambda e: e.tensor_scalar(negA[:], negA[:], -1.0, None, op0=ALU.mult),
               wait=[("ch", P.ch())])
        ng_tick = P.ch()

        seq = [(g, t) for g in range(NG) for t in range(2)]
        NIT = len(seq)
        for i in range(NBUF):
            P.add_sem(f"lgS{i}")
            P.add_sem(f"ldS{i}")
            P.add_sem(f"stS{i}")
        P.add_sem("zs")

        lgq = {}
        ldqs = {}

        def load_gt(idx, gg, gh, gs):
            gi, ti = seq[idx]
            t0 = ti * TS
            ch0 = gi * 8
            par = idx % NBUF
            P.load("gpsimd", dtr[par][:], rep_ap(dt_d, [(T, 8), (0, 16)], [(1, TS)], ch0 * T + t0),
                   sem=f"lgS{par}", wait=gg)
            lgq[idx] = P.load("gpsimd", ur[par][:], rep_ap(u_d, [(T, 8), (0, 16)], [(1, TS)], ch0 * T + t0),
                              sem=f"lgS{par}")
            P.load("sync", hst[par][:], rep_ap(hbf_d, [(T, 8)], [(1, TS)], ch0 * T + t0),
                   sem=f"ldS{par}", wait=gh)
            ldqs[idx] = P.load("sync", sgt[par][:], rep_ap(sg_d, [(T, 8)], [(1, TS)], ch0 * T + t0),
                               sem=f"ldS{par}", wait=gs)

        for i in range(NBUF):
            load_gt(i, [], [], [])
        stq = {}

        def emit_z(zi):
            par = zi % NBUF
            gi, ti = seq[zi]
            P.step("vector", lambda e, par=par: e.tensor_tensor(
                zst[par][:], psY[zi % 2][:], sgt[par][:], op=ALU.mult),
                inc="zs", wait=[("p5_4", zi + 1), (f"ldS{par}", ldqs[zi]),
                                (f"stS{par}", stq.get(zi - NBUF, 0))], st_guard=False)
            stq[zi] = P.store("sync", rep_ap(z_d, [(T, 8)], [(1, TS)], gi * 8 * T + ti * TS),
                              zst[par][:], wait=[("zs", zi + 1)], sem=f"stS{par}")

        for idx, (gi, ti) in enumerate(seq):
            par = idx % NBUF
            if idx >= 2:
                emit_z(idx - 2)

            def s0(e, j, gi=gi, par=par):
                return e.activation(dA[par][:], dtr[par][:], AF.Exp, scale=negA[:, gi:gi + 1])

            def s1(e, j, par=par, ti=ti):
                return e.tensor_tensor(dB[par][:], ur[par][:], Brt[ti][:], op=ALU.mult)

            if ti == 0:
                def s2(e, j, par=par):
                    return e.tensor_tensor_scan(mem[par][:], dA[par][:], dB[par][:], 0.0,
                                                op0=ALU.mult, op1=ALU.add)
            else:
                def s2(e, j, par=par):
                    return e.tensor_tensor_scan(mem[par][:], dA[par][:], dB[par][:],
                                                mem[(par + NBUF - 1) % NBUF][:, TS - 1:TS],
                                                op0=ALU.mult, op1=ALU.add)

            def s3(e, j, par=par):
                return e.activation(membf[par][:], mem[par][:], AF.Copy)

            def s4(e, j, gi=gi, par=par):
                last = None
                ps = psY[j % 2]
                for nt in range(4):
                    e.matmul(ps[:, 512 * nt:512 * (nt + 1)], s16[:],
                             membf[par][:, 512 * nt:512 * (nt + 1)], start=True, stop=False)
                    last = e.matmul(ps[:, 512 * nt:512 * (nt + 1)],
                                    skd[:, gi * 8:(gi + 1) * 8],
                                    hst[par][:, 512 * nt:512 * (nt + 1)], start=False, stop=True,
                                    skip_group_check=True)
                return last

            P.pipe([("scalar", s0), ("vector", s1), ("vector", s2), ("scalar", s3),
                    ("tensor", s4)], nbanks=NBUF, st_guard=False, stage_waits={
                0: [(f"lgS{par}", lgq[idx]), ("ch", ng_tick)],
                1: [(f"lgS{par}", lgq[idx]), ("lg", lgs)],
                4: [("lg", lgs), (f"ldS{par}", ldqs[idx]), ("zs", idx - 1)],
            })
            if idx + NBUF < NIT:
                load_gt(idx + NBUF, [("p5_1", idx + 1)], [("p5_4", idx + 1)],
                        [("zs", idx + 1)])
        emit_z(NIT - 2)
        emit_z(NIT - 1)
        fin = [P.cnt[f"stS{i}"] for i in range(NBUF)]

        def drain(eng):
            for i in range(NBUF):
                if fin[i] > 0:
                    eng.wait_ge(P.sem[f"stS{i}"], fin[i])
        P._engine("sync")(drain)
        P.finish()
    if standalone:
        return nc


def _scan1_inputs(ins, core):
    b, g = core // 2, core % 2
    alog = ins["L1_attn_A_log"][1024 * g:1024 * (g + 1)]
    skip = ins["L1_attn_skip"][1024 * g:1024 * (g + 1)]
    alog_t = np.zeros((128, 128), np.float32)
    for p in range(128):
        alog_t[p, :] = alog[np.arange(128) * 8 + p // 16, p % 16]
    s16 = np.zeros((128, 8), np.float32)
    for p in range(128):
        s16[p, p // 16] = 1.0
    skd = np.zeros((8, 128 * 8), np.float32)
    for g in range(128):
        for j in range(8):
            skd[j, g * 8 + j] = skip[8 * g + j]
    return {
        "alog_my": alog_t,
        "skipdiag": skd,
        "s16": s16,
    }


def build_l1tail(nc=None, semctx=None, fused=False, debug=False, stop_after=None):
    NTOK = 2048
    NT4 = NTOK // TT
    E = 8
    standalone = nc is None
    if standalone:
        nc = bass.Bass("TRN2", target_bir_lowering=False, debug=False)
    dr = DramHelper(nc, fused)
    zpair_d = dr("z_pair", [2048, T], BF16, "ExternalInput")
    z2g = zpair_d.rearrange("c (h t) -> (c h) t", h=2)
    zidx_d = dr("zidx", [128, 16], mybir.dt.int32, "ExternalInput", host=True)
    x1_d = dr("x1_my", [D, NTOK], F32, "ExternalInput")
    xn1_d = dr("xn1_my", [D, NTOK], BF16, "ExternalInput")
    wout_d = dr("wout_t", [128, 16 * 1024], F32, "ExternalInput", host=True)
    gw_d = dr("gw_t", [128, 8 * 8], F32, "ExternalInput", host=True)
    gb_d = dr("gb", [8, 1], F32, "ExternalInput", host=True)
    w1_d = dr("w1_h", [E, 32 * 128 * 1024], F32, "ExternalInput", host=True)
    b1_d = dr("b1_t", [128, 8 * 32], F32, "ExternalInput", host=True)
    w2_d = dr("w2_h", [E, 8 * 128 * 4096], F32, "ExternalInput", host=True)
    b2_d = dr("b2_t", [8, 1024], F32, "ExternalInput", host=True)
    ws1_d = dr("ws1_h", [32, 128 * 1024], F32, "ExternalInput", host=True)
    bs1_d = dr("bs1_t", [128, 32], F32, "ExternalInput", host=True)
    ws2_d = dr("ws2_h", [8, 128 * 4096], F32, "ExternalInput", host=True)
    bs2_d = dr("bs2_t", [128, 8], F32, "ExternalInput", host=True)
    idf_d = dr("idf3", [128, 128], F32, "ExternalInput", host=True)
    onesrb_d = dr("onesrb3", [1, 128], F32, "ExternalInput", host=True)
    out_d = dr("out_my", [D, NTOK], F32, "ExternalOutput", host="out")
    comb_d = dr("comb_scr", [8, NTOK], BF16,
                "ExternalOutput" if debug else "Internal")
    if debug:
        hdbg_d = dr("hdbg", [128, 32 * TT], BF16, "ExternalOutput")
        w1dbg_d = dr("w1dbg", [128, 64 * 128], BF16, "ExternalOutput")
        w2dbg_d = dr("w2dbg", [128, 64 * 128], BF16, "ExternalOutput")
        cbdbg_d = dr("cbdbg", [128, 4 * TT], F32, "ExternalOutput")

    with ExitStack() as ctx:
        sb = lambda *a: ctx.enter_context(nc.sbuf_tensor(*a, allow_name_mangling=True))
        pst = lambda *a: ctx.enter_context(nc.psum_tensor(*a, allow_name_mangling=True))
        gw_s = sb("gw_s", [128, 8 * 8], BF16)
        gb_s = sb("gb_s", [8, 1], F32)
        b1_s = sb("b1_s", [128, 8 * 32], F32)
        b2_s = sb("b2_s", [8, 1024], BF16)
        bs1_s = sb("bs1_s", [128, 32], F32)
        bs2_s = sb("bs2_s", [128, 8], F32)
        idf = sb("idf_s", [128, 128], F32)
        onesrb = sb("onesrb_s", [1, 128], BF16)
        zidx_s = sb("zidx_s", [128, 16], mybir.dt.int32)
        xnt = [sb(f"xnt{i}", [128, 8 * TT], BF16) for i in range(NT4)]
        x2acc = [sb(f"x2acc{i}", [128, 8 * TT], F32) for i in range(NT4)]
        cbcT = [sb(f"cbcT{i}", [128, TT], F32) for i in range(NT4)]
        combf = sb("combf", [8, TT], BF16)
        crow = sb("crow", [1, TT], BF16)
        scr8 = sb("scr8", [8, 1], F32)
        psAx = [pst("psA", [128, TT], F32), pst("psA2", [128, TT], F32)]
        psHx = [pst("psH", [128, TT], F32), pst("psH2", [128, TT], F32)]
        psG = pst("psG", [8, TT], F32)
        psC = pst("psC", [128, TT], F32)

        cpool, cemit = setup_consts(nc, (semctx or ctx), [1.0, -1.0], uid='l1')
        P = Phase(nc, ctx, semctx)
        P.add_sem("lgw1")
        P.add_sem("lgw2")
        P.step("gpsimd", cemit)
        P.load("sync", gb_s[:], gb_d)
        P.load("sync", b1_s[:], b1_d)
        P.load("sync", bs1_s[:], bs1_d)
        P.load("sync", bs2_s[:], bs2_d)
        lds = P.load("sync", idf[:], idf_d)
        P.load("gpsimd", gw_s[:], gw_d)
        P.load("gpsimd", b2_s[:], b2_d)
        P.load("gpsimd", zidx_s[:], zidx_d)
        lgc = P.load("gpsimd", onesrb[:], onesrb_d)

        ldxn = 0
        with ExitStack() as ctxA:
            sbA = lambda *a: ctxA.enter_context(nc.sbuf_tensor(*a, allow_name_mangling=True))
            wout_s = sbA("wout_s", [128, 16 * 1024], BF16)
            zt = sbA("zt0", [128, 16 * TT], BF16)
            x1t = sbA("x1t0", [128, 8 * TT], F32)
            lgs = sbA("lgs", [8, TT], F32)
            ltt = sbA("ltt", [128, 4 * 8], F32)
            m1 = sbA("m1", [128, 4], F32)
            m2 = sbA("m2", [128, 4], F32)
            mask1 = sbA("mask1", [128, 4 * 8], F32)
            mask2 = sbA("mask2", [128, 4 * 8], F32)
            maskd = sbA("maskd", [128, 4 * 8], F32)
            w1t_ = sbA("w1t_", [128, 4], F32)
            w2t_ = sbA("w2t_", [128, 4], F32)
            combt = sbA("combt", [128, 4 * 8], F32)
            lgw = P.load("gpsimd", wout_s[:], wout_d)

            for i in range(NT4):
                t0 = i * TT
                zgate = [("lg", lgc), ("ch", 0)] if i == 0 else \
                    [("lg", lgc), P.pipe_mid(2, 0), ("ch", P.ch())]
                for kc in range(16):
                    P.load_idx("gpsimd", zt[:, kc * TT:(kc + 1) * TT], z2g,
                               zidx_s[:, kc:kc + 1], element_offset=t0, wait=zgate)
                lgz = P.cnt["lg"]
                pgate = [] if i == 0 else [P.pipe_mid(2, 1)]
                P.load("sync", x1t[:], rep_ap(x1_d, [(NTOK, 128)], [(128 * NTOK, 8), (1, TT)], t0),
                       wait=pgate)
                ldxn = P.load("sync", xnt[i][:],
                              rep_ap(xn1_d, [(NTOK, 128)], [(128 * NTOK, 8), (1, TT)], t0))
                ldv_i = ldxn
                for oc in range(8):
                    def t_a(e, j, oc=oc):
                        last = None
                        for kc in range(16):
                            last = e.matmul(psAx[j % 2][:],
                                            wout_s[:, kc * 1024 + oc * 128: kc * 1024 + (oc + 1) * 128],
                                            zt[:, TT * kc:TT * (kc + 1)], start=(kc == 0), stop=(kc == 15))
                        return last
                    def v_a(e, j, oc=oc, i=i):
                        return e.tensor_tensor(
                            x2acc[i][:, TT * oc:TT * (oc + 1)], x1t[:, TT * oc:TT * (oc + 1)],
                            psAx[j % 2][:], op=ALU.add)
                    P.pipe([("tensor", t_a), ("vector", v_a)],
                           stage_waits={0: [("ld", ldv_i), ("lg", max(lgz, lgw))], 1: [("ld", ldv_i)]})
                def t_g(e, i=i):
                    last = None
                    for kc in range(8):
                        last = e.matmul(psG[:], gw_s[:, kc * 8:(kc + 1) * 8],
                                        xnt[i][:, TT * kc:TT * (kc + 1)], start=(kc == 0), stop=(kc == 7))
                    return last
                P.step("tensor", t_g, wait=[("ch", P.ch())])
                P.step("vector", lambda e: e.tensor_scalar(lgs[:], psG[:], gb_s[:, 0:1], None, op0=ALU.add),
                       wait=[("ch", P.ch())])
                def t_tr(e):
                    last = None
                    for tc in range(4):
                        last = e.transpose(psC[:, 8 * tc:8 * (tc + 1)], lgs[:, 128 * tc:128 * (tc + 1)],
                                           idf[0:8, 0:8])
                    return last
                P.step("tensor", t_tr, wait=[("ld", lds), ("ch", P.ch())])
                P.step("vector", lambda e: e.tensor_copy(ltt[:], psC[:, 0:32]), wait=[("ch", P.ch())])
                P.step("vector", lambda e: e.reduce_max(m1[:], ltt.rearrange("p (tc e) -> p tc e", tc=4),
                                                        axis=AX.X),
                       wait=[("ch", P.ch())])
                def v_g1a(e):
                    last = None
                    for tc in range(4):
                        last = e.tensor_scalar(mask1[:, 8 * tc:8 * (tc + 1)], ltt[:, 8 * tc:8 * (tc + 1)],
                                               m1[:, tc:tc + 1], None, op0=ALU.is_equal)
                    return last
                P.step("vector", v_g1a, wait=[("ch", P.ch())])
                def v_g1b(e):
                    last = None
                    for tc in range(4):
                        last = e.scalar_tensor_tensor(maskd[:, 8 * tc:8 * (tc + 1)], mask1[:, 8 * tc:8 * (tc + 1)],
                                                      -1e30, ltt[:, 8 * tc:8 * (tc + 1)], op0=ALU.mult, op1=ALU.add)
                    return last
                P.step("vector", v_g1b, wait=[("ch", P.ch())])
                P.step("vector", lambda e: e.reduce_max(m2[:], maskd.rearrange("p (tc e) -> p tc e", tc=4),
                                                        axis=AX.X),
                       wait=[("ch", P.ch())])
                def v_g2(e):
                    last = None
                    for tc in range(4):
                        last = e.tensor_scalar(mask2[:, 8 * tc:8 * (tc + 1)], maskd[:, 8 * tc:8 * (tc + 1)],
                                               m2[:, tc:tc + 1], None, op0=ALU.is_equal)
                    e.tensor_tensor(w1t_[:], m1[:], m2[:], op=ALU.subtract)
                    return last
                P.step("vector", v_g2, wait=[("ch", P.ch())])
                P.step("scalar", lambda e: e.activation(w1t_[:], w1t_[:], AF.Sigmoid), wait=[("ch", P.ch())])
                def v_g3a(e):
                    e.tensor_scalar(w2t_[:], w1t_[:], -1.0, 1.0, op0=ALU.mult, op1=ALU.add)
                    last = None
                    for tc in range(4):
                        last = e.tensor_scalar(combt[:, 8 * tc:8 * (tc + 1)], mask1[:, 8 * tc:8 * (tc + 1)],
                                               w1t_[:, tc:tc + 1], None, op0=ALU.mult)
                    return last
                P.step("vector", v_g3a, wait=[("ch", P.ch())])
                def v_g3b(e):
                    last = None
                    for tc in range(4):
                        last = e.scalar_tensor_tensor(combt[:, 8 * tc:8 * (tc + 1)], mask2[:, 8 * tc:8 * (tc + 1)],
                                                      w2t_[:, tc:tc + 1], combt[:, 8 * tc:8 * (tc + 1)],
                                                      op0=ALU.mult, op1=ALU.add)
                    return last
                P.step("vector", v_g3b, wait=[("ch", P.ch())])
                P.step("tensor", lambda e: [e.transpose(psG[0:8, 128 * tc:128 * (tc + 1)],
                                                        combt[:, 8 * tc:8 * (tc + 1)], idf[:, :])
                                            for tc in range(4)][-1],
                       wait=[("ch", P.ch())])
                P.step("vector", lambda e: e.tensor_copy(combf[:], psG[0:8, :]),
                       wait=[("ch", P.ch()), ("st", P.cnt["st"])])
                P.store("sync", rep_ap(comb_d, [(NTOK, 8)], [(1, TT)], t0), combf[:],
                        wait=[("ch", P.ch())])
            cf_all = P.cnt["st"]
        if stop_after == "pro":
            for t in range(NT4):
                P.store("sync", rep_ap(out_d, [(NTOK, 128)], [(128 * NTOK, 8), (1, TT)], t * TT),
                        x2acc[t][:], wait=[P.pipe_done(2), ("ch", P.ch())])
            P.finish()
            return nc
        hbuf = sb("hbuf", [128, 32 * TT], BF16)
        w1S = [sb(f"w1S{i}", [128, 64 * 128], BF16) for i in range(2)]
        w2S = sb("w2S0", [128, 64 * 128], BF16)

        def w1slab(sx):
            ex, mcg = sx // 4, sx % 4
            if ex < E:
                return rep_ap(w1_d, [(1024, 128)], [(128 * 1024, 8), (128, 8), (1, 128)],
                              ex * 32 * 128 * 1024 + mcg * 8 * 128 * 1024)
            return rep_ap(ws1_d, [(1024, 128)], [(128 * 1024, 8), (128, 8), (1, 128)],
                          mcg * 8 * 128 * 1024)

        def w2slab(sx):
            ex, mcg = sx // 4, sx % 4
            if ex < E:
                return rep_ap(w2_d, [(4096, 128)], [(128 * 4096, 8), (128, 8), (1, 128)],
                              ex * 8 * 128 * 4096 + mcg * 8 * 128)
            return rep_ap(ws2_d, [(4096, 128)], [(128 * 4096, 8), (128, 8), (1, 128)],
                          mcg * 8 * 128)

        NS = (E + 1) * 4
        if isinstance(stop_after, int):
            NS = stop_after
        pro_end = P.ch()
        lw1 = {0: P.load("gpsimd", w1S[0][:], w1slab(0), sem="lgw1",
                         wait=[("ch", pro_end)])}
        lw2 = {}
        cbc_tick = 0
        for sx in range(NS):
            ex, mcg = sx // 4, sx % 4
            shared = ex == E
            if mcg == 0 and not shared:
                for t in range(NT4):
                    lcr = P.load("sync", crow[:],
                                 rep_ap(comb_d, [(NTOK, 1)], [(1, TT)], ex * NTOK + t * TT),
                                 wait=[("st", cf_all), ("ch", P.ch())])
                    P.step("tensor", lambda e: e.matmul(psC[:], onesrb[0:1, :], crow[0:1, :],
                                                        start=True, stop=True),
                           wait=[("ld", lcr), ("lg", lgc), ("ch", P.ch())])
                    P.step("vector", lambda e, t=t: e.tensor_copy(cbcT[t][:], psC[:]),
                           wait=[("ch", P.ch())])
                cbc_tick = P.ch()
            w2gate = [P.pipe_done(2)] + ([("ch", pro_end)] if sx == 0 else [])
            lw2[sx] = P.load("gpsimd", w2S[:], w2slab(sx), sem="lgw2", wait=w2gate)
            pd21 = ("p2_1", P.cnt.get("p2_1", 0))
            for h2 in range(8):
                for t in range(NT4):
                    def t_h(e, j, h2=h2, t=t, sx=sx):
                        last = None
                        for kc in range(8):
                            last = e.matmul(psHx[j % 2][:],
                                            w1S[sx % 2][:, (h2 * 8 + kc) * 128:(h2 * 8 + kc + 1) * 128],
                                            xnt[t][:, TT * kc:TT * (kc + 1)],
                                            start=(kc == 0), stop=(kc == 7))
                        return last
                    def s_h(e, j, h2=h2, t=t, ex=ex, mcg=mcg, shared=shared):
                        bcol = (bs1_s[:, mcg * 8 + h2:mcg * 8 + h2 + 1] if shared
                                else b1_s[:, ex * 32 + mcg * 8 + h2:ex * 32 + mcg * 8 + h2 + 1])
                        return e.activation(hbuf[:, (h2 * 4 + t) * TT:(h2 * 4 + t + 1) * TT],
                                            psHx[j % 2][:], AF.Gelu, bias=bcol)
                    if shared:
                        def v_h(e, j):
                            return e.tensor_scalar(scr8[:], scr8[:], 1.0, None, op0=ALU.mult)
                    else:
                        def v_h(e, j, h2=h2, t=t):
                            return e.tensor_tensor(hbuf[:, (h2 * 4 + t) * TT:(h2 * 4 + t + 1) * TT],
                                                   hbuf[:, (h2 * 4 + t) * TT:(h2 * 4 + t + 1) * TT],
                                                   cbcT[t][:], op=ALU.mult)
                    P.pipe([("tensor", t_h), ("scalar", s_h), ("vector", v_h)],
                           stage_waits={0: [("lgw1", lw1[sx]), ("ld", ldxn)],
                                        1: [pd21],
                                        2: [("ch", cbc_tick)]})
            hd = P.pipe_done(3)
            if sx + 1 < NS:
                lw1[sx + 1] = P.load("gpsimd", w1S[(sx + 1) % 2][:], w1slab(sx + 1),
                                     sem="lgw1", wait=[P.pipe_mid(3, 0)])
            for t in range(NT4):
                for oc in range(8):
                    def t_y(e, j, t=t, oc=oc):
                        last = None
                        for kc in range(8):
                            last = e.matmul(psAx[j % 2][:],
                                            w2S[:, (oc * 8 + kc) * 128:(oc * 8 + kc + 1) * 128],
                                            hbuf[:, (kc * 4 + t) * TT:(kc * 4 + t + 1) * TT],
                                            start=(kc == 0), stop=(kc == 7))
                        return last
                    if shared and mcg == 3:
                        def v_y(e, j, t=t, oc=oc):
                            return e.scalar_tensor_tensor(
                                x2acc[t][:, TT * oc:TT * (oc + 1)], psAx[j % 2][:], bs2_s[:, oc:oc + 1],
                                x2acc[t][:, TT * oc:TT * (oc + 1)], op0=ALU.add, op1=ALU.add)
                    else:
                        def v_y(e, j, t=t, oc=oc):
                            return e.tensor_tensor(
                                x2acc[t][:, TT * oc:TT * (oc + 1)], x2acc[t][:, TT * oc:TT * (oc + 1)],
                                psAx[j % 2][:], op=ALU.add)
                    P.pipe([("tensor", t_y), ("vector", v_y)],
                           stage_waits={0: [("lgw2", lw2[sx]), hd]})
        if isinstance(stop_after, int):
            for t in range(NT4):
                P.store("sync", rep_ap(out_d, [(NTOK, 128)], [(128 * NTOK, 8), (1, TT)], t * TT),
                        x2acc[t][:], wait=[P.pipe_done(2), ("ch", P.ch())])
            if debug:
                for t in range(NT4):
                    P.store("sync", rep_ap(cbdbg_d, [(4 * TT, 128)], [(1, TT)], t * TT),
                            cbcT[t][:], wait=[P.pipe_done(2)])
                P.store("sync", hdbg_d, hbuf[:], wait=[P.pipe_done(2)])
                P.store("sync", w1dbg_d, w1S[(NS - 1) % 2][:], wait=[P.pipe_done(2)])
                P.store("sync", w2dbg_d, w2S[:], wait=[P.pipe_done(2)])
            P.finish()
            return nc
        for t in range(NT4):
            lcb = P.load("sync", combf[:], rep_ap(comb_d, [(NTOK, 8)], [(1, TT)], t * TT),
                         wait=[("st", cf_all), P.pipe_done(2), ("ch", P.ch())])
            for oc in range(8):
                def t_b2(e, j, oc=oc):
                    return e.matmul(psHx[j % 2][:], b2_s[:, oc * 128:(oc + 1) * 128], combf[:],
                                    start=True, stop=True)
                def v_b2(e, j, t=t, oc=oc):
                    return e.tensor_tensor(
                        x2acc[t][:, TT * oc:TT * (oc + 1)], x2acc[t][:, TT * oc:TT * (oc + 1)],
                        psHx[j % 2][:], op=ALU.add)
                P.pipe([("tensor", t_b2), ("vector", v_b2)],
                       stage_waits={0: [("ld", lcb), P.pipe_done(3), P.pipe_done(2)]})
            P.store("sync", rep_ap(out_d, [(NTOK, 128)], [(128 * NTOK, 8), (1, TT)], t * TT),
                    x2acc[t][:], wait=[P.pipe_done(2), ("ch", P.ch())])
        P.finish()
    return nc


def _l1tail_inputs(ins, core, z_pair_b, x1_my, xn1_my):
    b, g = core // 2, core % 2
    p = np.arange(128)
    zidx = np.zeros((128, 16), np.int32)
    for kc in range(16):
        zidx[:, kc] = 256 * kc + 2 * p + g
    Wout = ins["L1_attn_W_out"]
    wout_t = np.ascontiguousarray(Wout.reshape(16, 128, 1024).transpose(1, 0, 2).reshape(128, 16 * 1024))
    gw = ins["L1_ffn_gate_w"]
    gw_t = np.ascontiguousarray(gw.reshape(8, 128, 8).transpose(1, 0, 2).reshape(128, 8 * 8))
    w1 = ins["L1_ffn_w1"]
    w1_h = np.ascontiguousarray(
        w1.reshape(8, 8, 128, 32, 128).transpose(0, 3, 2, 1, 4).reshape(8, 32 * 128 * 1024))
    b1_t = np.ascontiguousarray(ins["L1_ffn_b1"].reshape(8, 32, 128).transpose(2, 0, 1).reshape(128, 8 * 32))
    w2 = ins["L1_ffn_w2"]
    w2_h = np.ascontiguousarray(
        w2.reshape(8, 32, 128, 8, 128).transpose(0, 3, 2, 1, 4).reshape(8, 8 * 128 * 4096))
    ws1_h = np.ascontiguousarray(
        ins["L1_ffn_ws1"].reshape(8, 128, 32, 128).transpose(2, 1, 0, 3).reshape(32, 128 * 1024))
    ws2_h = np.ascontiguousarray(
        ins["L1_ffn_ws2"].reshape(32, 128, 8, 128).transpose(2, 1, 0, 3).reshape(8, 128 * 4096))
    return {
        "z_pair": z_pair_b, "zidx": zidx,
        "x1_my": np.ascontiguousarray(x1_my), "xn1_my": np.ascontiguousarray(xn1_my),
        "wout_t": wout_t, "gw_t": gw_t,
        "gb": np.ascontiguousarray(ins["L1_ffn_gate_b"].reshape(8, 1)),
        "w1_h": w1_h, "b1_t": b1_t, "w2_h": w2_h,
        "b2_t": np.ascontiguousarray(ins["L1_ffn_b2"]),
        "ws1_h": ws1_h,
        "bs1_t": np.ascontiguousarray(ins["L1_ffn_bs1"].reshape(32, 128).T),
        "ws2_h": ws2_h,
        "bs2_t": np.ascontiguousarray(ins["L1_ffn_bs2"].reshape(8, 128).T),
        "idf3": np.eye(128, dtype=np.float32),
        "onesrb3": np.ones((1, 128), np.float32),
    }


def _pre0_inputs(ins, core):
    b, g = core // 2, core % 2
    x = ins["L0_attn_x"]
    x_b = np.ascontiguousarray(x[b].T)
    Wq, Wk, Wv = ins["L0_attn_Wq"], ins["L0_attn_Wk"], ins["L0_attn_Wv"]
    wqkv = np.concatenate([Wq[:, 512 * g:512 * (g + 1)],
                           Wk[:, 128 * g:128 * (g + 1)],
                           Wv[:, 128 * g:128 * (g + 1)]], axis=1)
    wqkv_t = np.ascontiguousarray(wqkv.reshape(8, 128, 768).transpose(1, 0, 2).reshape(128, 8 * 768))
    wgab = np.concatenate([ins["L0_attn_Wa"][:, 8 * g:8 * (g + 1)],
                           ins["L0_attn_Walpha"][:, 8 * g:8 * (g + 1)],
                           ins["L0_attn_Wb"][:, 2 * g:2 * (g + 1)],
                           np.zeros((1024, 6), np.float32)], axis=1)
    wgab_t = np.ascontiguousarray(wgab.reshape(8, 128, 24).transpose(1, 0, 2).reshape(128, 8 * 24))
    wconv = np.concatenate([ins["L0_attn_Wconv"][512 * g:512 * (g + 1)],
                            ins["L0_attn_Wconv"][1024 + 128 * g:1024 + 128 * (g + 1)],
                            ins["L0_attn_Wconv"][1280 + 128 * g:1280 + 128 * (g + 1)]], axis=0)
    return {
        "x_b": x_b,
        "wqkv": wqkv_t,
        "wgab": wgab_t,
        "wconv": np.ascontiguousarray(wconv),
        "alog8": np.ascontiguousarray(ins["L0_attn_A_log"][8 * g:8 * (g + 1)].reshape(8, 1)),
        "dtb8": np.ascontiguousarray(ins["L0_attn_dt_bias"][8 * g:8 * (g + 1)].reshape(8, 1)),
        "onesc": np.ones((128, 1), np.float32),
        "onesr": np.ones((1, 128), np.float32),
    }


PAIRS = [[0, 1], [2, 3], [4, 5], [6, 7]]
HYP2 = [[0, 2], [1, 3], [4, 6], [5, 7]]
HYP4 = [[0, 4], [1, 5], [2, 6], [3, 7]]


def emit_ag_block(nc, semctx, steps):
    Phase._uid[0] += 1
    sem = semctx.enter_context(nc.semaphore(f"cc{Phase._uid[0]}"))
    with ExitStack() as ctx:
        blk = ctx.enter_context(nc.Block())

        def body(g):
            n = 0
            for (src, dst, rg) in steps:
                g.collective_compute("AllGather", ALU.bypass, replica_groups=rg,
                                     ins=[src.opt()], outs=[dst.opt()]).then_inc(sem, 1)
                n += 1
                g.wait_ge(sem, n)
        blk.gpsimd(body)


def build_fused():
    nc = bass.Bass("TRN2", target_bir_lowering=False, debug=False)
    semctx = ExitStack()
    nc._fused_semctx = semctx
    dr = DramHelper(nc, True)
    _, nc._cemit_all = setup_consts(nc, semctx, [1e-5, 1.0, -1.0, -0.5, 1e-6], uid='all')
    build_pre0(nc, semctx, fused=True)
    build_scan0(nc, semctx, fused=True)
    build_ufold(nc, semctx, fused=True)
    reg = nc._fusedreg
    o_pair = dr("o_pair", [2 * T, 512], BF16, "Internal")
    we2 = dr("we2", [256, 4096], BF16, "Internal")
    we4 = dr("we4", [512, 4096], BF16, "Internal")
    wexpfA = dr("wexpfA", [1024, 4096], BF16, "Internal")
    wc2 = dr("wc2", [256, 32 * 128], BF16, "Internal")
    wc4 = dr("wc4", [512, 32 * 128], BF16, "Internal")
    wconfA = dr("wconfA", [1024, 4096], BF16, "Internal")
    bu2 = dr("bu2", [256, 4], F32, "Internal")
    bu4 = dr("bu4", [512, 4], F32, "Internal")
    bupfA = dr("bupfA", [1024, 4], F32, "Internal")
    emit_ag_block(nc, semctx, [
        (reg["o_out"], o_pair, PAIRS),
        (reg["wexpf"], we2, PAIRS), (we2, we4, HYP2), (we4, wexpfA, HYP4),
        (reg["wconf"], wc2, PAIRS), (wc2, wc4, HYP2), (wc4, wconfA, HYP4),
        (reg["bupf"], bu2, PAIRS), (bu2, bu4, HYP2), (bu4, bupfA, HYP4),
    ])
    build_l0tail(nc, semctx, fused=True)
    xn1_pair = dr("xn1_pair", [2048, 2048], BF16, "Internal")
    emit_ag_block(nc, semctx, [(reg["xn1_my"], xn1_pair, PAIRS)])
    build_pre1(nc, semctx, fused=True)
    h_pair = dr("h_pair", [2048, T], BF16, "Internal")
    emit_ag_block(nc, semctx, [(reg["h_bf"], h_pair, PAIRS)])
    build_dt1(nc, semctx, fused=True)
    build_scan1(nc, semctx, fused=True)
    z_pair = dr("z_pair", [2048, T], BF16, "Internal")
    emit_ag_block(nc, semctx, [(reg["z_bf"], z_pair, PAIRS)])
    build_l1tail(nc, semctx, fused=True)
    return nc


def _fused_inputs(ins, core):
    du_opair = np.zeros((2 * T, 512), BF16NP)
    du_we = np.zeros((1024, 4096), BF16NP)
    du_bu = np.zeros((1024, 4), np.float32)
    du_wc = np.zeros((1024, 4096), BF16NP)
    du_xn1p = np.zeros((2048, 2048), BF16NP)
    du_hp = np.zeros((2048, T), BF16NP)
    du_hf = np.zeros((1024, T), np.float32)
    du_zp = np.zeros((2048, T), BF16NP)
    du_x1 = np.zeros((1024, 2048), np.float32)
    du_xn1 = np.zeros((1024, 2048), BF16NP)
    m = {}
    m.update(_pre0_inputs(ins, core))
    m.update(_scan0_inputs(ins, core))
    m.update(_ufold_inputs(ins, core))
    m.update(_l0tail_inputs(ins, core, du_opair, du_we, du_bu, du_wc))
    m.update(_pre1_inputs(ins, core, du_xn1p))
    m.update(_dt1_inputs(ins, core, du_hp, du_hf))
    m.update(_scan1_inputs(ins, core))
    m.update(_l1tail_inputs(ins, core, du_zp, du_x1, du_xn1))
    return m


USE_FUSED = True


_NC_CACHE = {}


def _get_nc(name, builder):
    if name not in _NC_CACHE:
        _NC_CACHE[name] = builder()
    return _NC_CACHE[name]


def kernel(**inputs):
    ins = inputs
    cores = list(range(NCORE))

    if USE_FUSED:
        nc = _get_nc("fused", build_fused)
        hostkeys = nc._fused_hostkeys
        in_maps = []
        for c in cores:
            m = _fused_inputs(ins, c)
            in_maps.append({k: v for k, v in m.items() if k in hostkeys})
        res = run_bass_kernel_spmd(nc, in_maps, core_ids=cores)
        out = np.empty((4, 4096, 1024), np.float32)
        for c in cores:
            b, g = c // 2, c % 2
            out[b, 2048 * g:2048 * (g + 1), :] = np.asarray(res.results[c]["out_my"]).T
        return out

    nc = _get_nc("pre0", build_pre0)
    res = run_bass_kernel_spmd(nc, [_pre0_inputs(ins, c) for c in cores], core_ids=cores)
    xn0bf = [np.asarray(res.results[c]["xn0bf"]) for c in cores]
    act = [np.asarray(res.results[c]["act"]) for c in cores]
    gab = [np.asarray(res.results[c]["gab"]) for c in cores]

    nc = _get_nc("scan0", build_scan0)
    maps = []
    for c in cores:
        m = _scan0_inputs(ins, c)
        m["act"] = act[c]
        m["gab"] = gab[c]
        maps.append(m)
    res = run_bass_kernel_spmd(nc, maps, core_ids=cores)
    o = [np.asarray(res.results[c]["o_out"]) for c in cores]

    nc = _get_nc("ufold", build_ufold)
    res = run_bass_kernel_spmd(nc, [_ufold_inputs(ins, c) for c in cores], core_ids=cores)
    wexpfA = np.ascontiguousarray(np.concatenate([np.asarray(res.results[c]["wexpf"]) for c in cores], axis=0))
    bupfA = np.ascontiguousarray(np.concatenate([np.asarray(res.results[c]["bupf"]) for c in cores], axis=0))
    wconfA = np.ascontiguousarray(np.concatenate([np.asarray(res.results[c]["wconf"]) for c in cores], axis=0))

    o_pair = [np.ascontiguousarray(np.concatenate([o[2 * b], o[2 * b + 1]], axis=0)) for b in range(4)]
    nc = _get_nc("l0tail", build_l0tail)
    res = run_bass_kernel_spmd(
        nc, [_l0tail_inputs(ins, c, o_pair[c // 2], wexpfA, bupfA, wconfA) for c in cores],
        core_ids=cores)
    x1 = [np.asarray(res.results[c]["x1_my"]) for c in cores]
    xn1 = [np.asarray(res.results[c]["xn1_my"]) for c in cores]

    xn1_full = [np.ascontiguousarray(np.concatenate([xn1[2 * b], xn1[2 * b + 1]], axis=0)) for b in range(4)]
    nc = _get_nc("pre1", build_pre1)
    res = run_bass_kernel_spmd(nc, [_pre1_inputs(ins, c, xn1_full[c // 2]) for c in cores], core_ids=cores)
    h_f = [np.asarray(res.results[c]["h_f"]) for c in cores]
    h_bf = [np.asarray(res.results[c]["h_bf"]) for c in cores]
    sg = [np.asarray(res.results[c]["sg"]) for c in cores]

    hbf_full = [np.ascontiguousarray(np.concatenate([h_bf[2 * b], h_bf[2 * b + 1]], axis=0)) for b in range(4)]
    nc = _get_nc("dt1", build_dt1)
    res = run_bass_kernel_spmd(nc, [_dt1_inputs(ins, c, hbf_full[c // 2], h_f[c]) for c in cores],
                               core_ids=cores)
    dtb = [np.asarray(res.results[c]["dt_bf"]) for c in cores]
    ub = [np.asarray(res.results[c]["u_bf"]) for c in cores]
    Bb = [np.asarray(res.results[c]["B_bf"]) for c in cores]

    nc = _get_nc("scan1", build_scan1)
    maps = []
    for c in cores:
        m = _scan1_inputs(ins, c)
        m["dt_bf"] = dtb[c]
        m["u_bf"] = ub[c]
        m["B_bf"] = Bb[c]
        m["h_f"] = h_f[c]
        m["sg"] = sg[c]
        maps.append(m)
    res = run_bass_kernel_spmd(nc, maps, core_ids=cores)
    zb = [np.asarray(res.results[c]["z_bf"]) for c in cores]

    z_full = [np.ascontiguousarray(np.concatenate([zb[2 * b], zb[2 * b + 1]], axis=0)) for b in range(4)]
    nc = _get_nc("l1tail", build_l1tail)
    res = run_bass_kernel_spmd(
        nc, [_l1tail_inputs(ins, c, z_full[c // 2], x1[c], xn1[c]) for c in cores], core_ids=cores)

    out = np.empty((4, 4096, 1024), np.float32)
    for c in cores:
        b, g = c // 2, c % 2
        out[b, 2048 * g:2048 * (g + 1), :] = np.asarray(res.results[c]["out_my"]).T
    return out


# version 19 (final)
# speedup vs baseline: 1.0080x; 1.0080x over previous
import numpy as np
import ml_dtypes
import concourse.bass as bass
import concourse.mybir as mybir
from concourse.bass_utils import run_bass_kernel_spmd
from contextlib import ExitStack

F32 = mybir.dt.float32
BF16 = mybir.dt.bfloat16
AF = mybir.ActivationFunctionType
ALU = mybir.AluOpType
AX = mybir.AxisListType
BF16NP = ml_dtypes.bfloat16

B, T, D = 4, 4096, 1024
NCORE = 8
TT = 512
NTT = T // TT


def rep_ap(dram_ap, part_dims, free_dims, offset_elems=0):
    return bass.AP(
        tensor=dram_ap.tensor,
        offset=offset_elems,
        ap=[[s, c] for (s, c) in part_dims] + [[s, c] for (s, c) in free_dims],
    )


def setup_consts(nc, ctx, values, uid=""):
    seen = getattr(nc, "_constvals", None)
    if seen is None:
        seen = nc._constvals = set()
    new = [v for v in values if v not in seen]
    if not new:
        return None, (lambda eng: eng.nop())
    tile = ctx.enter_context(nc.sbuf_tensor("cpool" + uid, [128, len(new)], F32))
    for j, v in enumerate(new):
        nc.const_aps.register(F32, v, tile[:, j:j + 1])
        seen.add(v)

    def emit(eng):
        last = eng.nop()
        for j, v in enumerate(new):
            last = eng.memset(tile[:, j:j + 1], v)
        return last
    return tile, emit


class DramHelper:

    def __init__(self, nc, fused):
        self.nc = nc
        self.fused = fused
        if fused and not hasattr(nc, "_fusedreg"):
            nc._fusedreg = {}

    def __call__(self, name, shape, dtype, kind, host=False):
        if not self.fused:
            return self.nc.dram_tensor(name, shape, dtype, kind=kind).ap()
        reg = self.nc._fusedreg
        if name in reg:
            return reg[name]
        if host is True:
            k = "ExternalInput"
            if not hasattr(self.nc, "_fused_hostkeys"):
                self.nc._fused_hostkeys = set()
            self.nc._fused_hostkeys.add(name)
        elif host == "out":
            k = "ExternalOutput"
        else:
            k = "Internal"
        ap = self.nc.dram_tensor(name, shape, dtype, kind=k).ap()
        reg[name] = ap
        return ap


class Phase:

    _uid = [0]

    def __init__(self, nc, ctx, semctx=None):
        self.nc = nc
        self.ctx = ctx
        self.blk = ctx.enter_context(nc.Block())
        self.sem = {}
        self.cnt = {}
        Phase._uid[0] += 1
        u = Phase._uid[0]
        self._pipe_ctx = (semctx or ctx)
        for nm in ("ld", "lg", "ch", "st"):
            self.sem[nm] = (semctx or ctx).enter_context(nc.semaphore(f"{nm}{u}"))
            self.cnt[nm] = 0

    def _engine(self, name):
        return getattr(self.blk, name)

    def step(self, engine_name, fn, inc="ch", inc_by=1, wait=None, st_guard=True):
        waits = list(wait or []) + ([("st", self.cnt["st"])] if st_guard else [])
        me = self

        def body(eng):
            for smn, cval in waits:
                if cval > 0:
                    eng.wait_ge(me.sem[smn], cval)
            last = fn(eng)
            if inc is not None:
                last.then_inc(me.sem[inc], inc_by)
                me.cnt[inc] += inc_by
        self._engine(engine_name)(body)
        return self.cnt[inc] if inc else None

    def load(self, queue, out, in_, sem=None, wait=None):
        sem = sem or ("lg" if queue == "gpsimd" else "ld")
        me = self
        waits = list(wait or [])

        def body(eng):
            for smn, cval in waits:
                if cval > 0:
                    eng.wait_ge(me.sem[smn], cval)
            eng.dma_start(out=out, in_=in_).then_inc(me.sem[sem], 16)
            me.cnt[sem] += 16
        self._engine(queue)(body)
        return self.cnt[sem]

    def pipe(self, stages, nbanks=2, stage_waits=None, st_guard=True):
        n = len(stages)
        if not hasattr(self, "pjn"):
            self.pjn = {}
        for k in range(n):
            nm = f"p{n}_{k}"
            if nm not in self.sem:
                Phase._uid[0] += 1
                self.sem[nm] = self._pipe_ctx.enter_context(
                    self.nc.semaphore(f"{nm}u{Phase._uid[0]}"))
                self.cnt[nm] = 0
        j = self.pjn.get(n, 0)
        self.pjn[n] = j + 1
        me = self
        for k, (engine_name, fn) in enumerate(stages):
            waits = list((stage_waits or {}).get(k, []))
            if k == 0:
                waits.append((f"p{n}_{n - 1}", j - nbanks + 1))
            else:
                waits.append((f"p{n}_{k - 1}", j + 1))
            if st_guard:
                waits.append(("st", self.cnt["st"]))
            sem_inc = f"p{n}_{k}"

            def body(eng, waits=waits, fn=fn, sem_inc=sem_inc, j=j):
                for smn, cval in waits:
                    if cval > 0:
                        eng.wait_ge(me.sem[smn], cval)
                last = fn(eng, j)
                last.then_inc(me.sem[sem_inc], 1)
                me.cnt[sem_inc] += 1
            self._engine(engine_name)(body)
        return j

    def pipe_done(self, n):
        nm = f"p{n}_{n - 1}"
        return (nm, self.cnt.get(nm, 0))

    def pipe_mid(self, n, k):
        nm = f"p{n}_{k}"
        return (nm, self.cnt.get(nm, 0))

    def load_idx(self, queue, out, in_, idx_ap, element_offset=0, wait=None):
        me = self
        waits = list(wait or [])

        def body(eng):
            for smn, cval in waits:
                if cval > 0:
                    eng.wait_ge(me.sem[smn], cval)
            eng.indirect_dma_start(
                out=out, out_offset=None, in_=in_,
                in_offset=bass.IndirectOffsetOnAxis(ap=idx_ap, axis=0),
                element_offset=element_offset,
            ).then_inc(me.sem["lg"], 16)
            me.cnt["lg"] += 16
        self._engine(queue)(body)
        return self.cnt["lg"]

    def store(self, queue, out, in_, wait=None, sem="st"):
        me = self
        waits = list(wait or [])

        def body(eng):
            for smn, cval in waits:
                if cval > 0:
                    eng.wait_ge(me.sem[smn], cval)
            eng.dma_start(out=out, in_=in_).then_inc(me.sem[sem], 16)
            me.cnt[sem] += 16
        self._engine(queue)(body)
        return self.cnt[sem]

    def add_sem(self, nm):
        Phase._uid[0] += 1
        self.sem[nm] = self._pipe_ctx.enter_context(
            self.nc.semaphore(f"{nm}u{Phase._uid[0]}"))
        self.cnt[nm] = 0

    def finish(self):
        me = self
        tot = self.cnt["st"]

        def body(eng):
            eng.wait_ge(me.sem["st"], tot)
        self._engine("sync")(body)

    def ch(self):
        return self.cnt["ch"]


def build_pre0(nc=None, semctx=None, fused=False):
    standalone = nc is None
    if standalone:
        nc = bass.Bass("TRN2", target_bir_lowering=False, debug=False)
    dr = DramHelper(nc, fused)
    x_d = dr("x_b", [D, T], F32, "ExternalInput", host=True)
    wqkv_d = dr("wqkv", [128, 8 * 768], F32, "ExternalInput", host=True)
    wgab_d = dr("wgab", [128, 8 * 24], F32, "ExternalInput", host=True)
    wconv_d = dr("wconv", [768, 4], F32, "ExternalInput", host=True)
    alog_d = dr("alog8", [8, 1], F32, "ExternalInput", host=True)
    dtb_d = dr("dtb8", [8, 1], F32, "ExternalInput", host=True)
    onesc_d = dr("onesc", [128, 1], F32, "ExternalInput", host=True)
    onesr_d = dr("onesr", [1, 128], F32, "ExternalInput", host=True)

    xn0_d = dr("xn0bf", [D, T], BF16, "ExternalOutput")
    act_d = dr("act", [768, T], F32, "ExternalOutput")
    gab_d = dr("gab", [24, T], F32, "ExternalOutput")

    with ExitStack() as ctx:
        sb = lambda *a: ctx.enter_context(nc.sbuf_tensor(*a, allow_name_mangling=True))
        pst = lambda *a: ctx.enter_context(nc.psum_tensor(*a, allow_name_mangling=True))
        wqkv_s = sb("wqkv_s", [128, 8 * 768], BF16)
        wgab_s = sb("wgab_s", [128, 8 * 24], BF16)
        wconv_s = sb("wconv_s", [128, 6 * 4], F32)
        alog_s = sb("alog_s", [8, 1], F32)
        dtb_s = sb("dtb_s", [8, 1], F32)
        negA = sb("negA", [8, 1], F32)
        onesc = sb("onesc_s", [128, 1], BF16)
        onesr = sb("onesr_s", [1, 128], F32)
        xt = [sb(f"xt{i}", [128, 8 * TT], F32) for i in range(2)]
        xsq = sb("xsq", [128, 8 * TT], BF16)
        xbf = sb("xbf", [128, 8 * TT], BF16)
        lnscr = sb("lnscr", [128, TT], F32)
        stat = sb("stat", [1, 6 * TT], F32)
        xnt = sb("xnt", [128, 8 * TT], BF16)
        pre_ext = [sb(f"pre_ext{i}", [128, 6 * (TT + 3)], F32) for i in range(2)]
        convA = sb("convA", [128, TT], F32)
        convB = sb("convB", [128, TT], F32)
        actt = sb("actt", [128, 6 * TT], F32)
        gsp = sb("gsp", [8, 5 * TT], F32)
        g8 = sb("g8", [8, TT], F32)
        al8 = sb("al8", [8, TT], F32)
        be2 = sb("be2", [2, TT], F32)

        ps_sum = pst("ps_sum", [1, TT], F32)
        ps_sq = pst("ps_sq", [1, TT], F32)
        ps_b1 = pst("ps_b1", [128, TT], F32)
        ps_b2 = pst("ps_b2", [128, TT], F32)
        ps_mm = pst("ps_mm", [128, TT], F32)
        ps_g = pst("ps_g", [8, TT], F32)
        ps_al = pst("ps_al", [8, TT], F32)
        ps_be = pst("ps_be", [2, TT], F32)

        cpool, cemit = setup_consts(nc, (semctx or ctx), [1e-5, 1.0, -1.0, -0.5], uid='p0')
        cemit = getattr(nc, "_cemit_all", None) or cemit
        P = Phase(nc, ctx, semctx)
        P.step("gpsimd", cemit)

        P.load("sync", wconv_s[:, :], rep_ap(wconv_d, [(4, 128)], [(512, 6), (1, 4)]))
        P.load("sync", dtb_s[:], dtb_d)
        P.load("sync", alog_s[:], alog_d)
        ld_const = P.load("sync", onesr[:], onesr_d)
        P.load("gpsimd", wqkv_s[:], wqkv_d)
        P.load("gpsimd", wgab_s[:], wgab_d)
        lg_const = P.load("gpsimd", onesc[:], onesc_d)

        P.step("gpsimd", lambda e: (e.memset(pre_ext[0][:, :], 0.0), e.memset(pre_ext[1][:, :], 0.0))[-1])
        zer = P.ch()

        P.step("scalar", lambda e: e.activation(negA[:], alog_s[:], AF.Exp),
               wait=[("ld", ld_const)])
        P.step("vector", lambda e: e.tensor_scalar(negA[:], negA[:], -1.0, None, op0=ALU.mult),
               wait=[("ch", P.ch())])

        def load_xtile(i, gate_tick):
            t0 = i * TT
            return P.load(
                "sync", xt[i % 2][:],
                rep_ap(x_d, [(T, 128)], [(128 * T, 8), (1, TT)], t0),
                wait=[("ch", gate_tick)])

        ld_x = {0: load_xtile(0, 0), 1: load_xtile(1, 0)}
        tile_end = {}

        for i in range(NTT):
            t0 = i * TT
            par = i % 2
            xti = xt[par]
            w_ld = [("ld", ld_x[i]), ("lg", lg_const), ("ch", P.ch())]
            P.step("scalar", lambda e, xti=xti: e.activation(xsq[:], xti[:], AF.Square), wait=w_ld)
            P.step("vector", lambda e, xti=xti: e.tensor_copy(xbf[:], xti[:]), wait=[("ch", P.ch())])
            def mm_sums(e):
                for kc in range(8):
                    e.matmul(ps_sum[:], onesc[:, 0:1], xbf[:, kc * TT:(kc + 1) * TT],
                             start=(kc == 0), stop=(kc == 7))
                for kc in range(8):
                    i2 = e.matmul(ps_sq[:], onesc[:, 0:1], xsq[:, kc * TT:(kc + 1) * TT],
                                  start=(kc == 0), stop=(kc == 7))
                return i2
            P.step("tensor", mm_sums, wait=[("ch", P.ch())])
            def v_stats(e):
                e.tensor_scalar(stat[:, 0:TT], ps_sum[:], 1.0 / D, None, op0=ALU.mult)
                e.tensor_scalar(stat[:, TT:2 * TT], ps_sq[:], 1.0 / D, None, op0=ALU.mult)
                e.tensor_tensor(stat[:, 2 * TT:3 * TT], stat[:, 0:TT], stat[:, 0:TT], op=ALU.mult)
                return e.tensor_tensor(stat[:, 3 * TT:4 * TT], stat[:, TT:2 * TT],
                                       stat[:, 2 * TT:3 * TT], op=ALU.subtract)
            P.step("vector", v_stats, wait=[("ch", P.ch())])
            def s_rstd(e):
                e.activation(stat[:, 4 * TT:5 * TT], stat[:, 3 * TT:4 * TT], AF.Ln, bias=1e-5)
                return e.activation(stat[:, 4 * TT:5 * TT], stat[:, 4 * TT:5 * TT], AF.Exp, scale=-0.5)
            P.step("scalar", s_rstd, wait=[("ch", P.ch())])
            def mm_bc(e):
                e.matmul(ps_b1[:], onesr[0:1, :], stat[0:1, 0:TT], start=True, stop=True)
                return e.matmul(ps_b2[:], onesr[0:1, :], stat[0:1, 4 * TT:5 * TT], start=True, stop=True)
            P.step("tensor", mm_bc, wait=[("ch", P.ch())])
            def v_xn(e, xti=xti):
                last = None
                for kc in range(8):
                    e.tensor_tensor(lnscr[:], xti[:, kc * TT:(kc + 1) * TT], ps_b1[:], op=ALU.subtract)
                    last = e.tensor_tensor(xnt[:, kc * TT:(kc + 1) * TT], lnscr[:], ps_b2[:], op=ALU.mult)
                return last
            P.step("vector", v_xn, wait=[("ch", P.ch())])
            xn_done = P.ch()
            P.store("sync", rep_ap(xn0_d, [(T, 128)], [(128 * T, 8), (1, TT)], t0), xnt[:],
                    wait=[("ch", xn_done)])
            for oc in range(6):
                def mm_qkv(e, oc=oc):
                    last = None
                    for kc in range(8):
                        last = e.matmul(ps_mm[:], wqkv_s[:, kc * 768 + oc * 128: kc * 768 + (oc + 1) * 128],
                                        xnt[:, kc * TT:(kc + 1) * TT],
                                        start=(kc == 0), stop=(kc == 7))
                    return last
                P.step("tensor", mm_qkv, wait=[("ch", P.ch())])
                off = oc * (TT + 3)
                def v_conv(e, oc=oc, off=off, par=par, i=i):
                    e.tensor_copy(pre_ext[par][:, off + 3: off + 3 + TT], ps_mm[:])
                    if i > 0:
                        e.tensor_copy(pre_ext[par][:, off: off + 3],
                                      pre_ext[1 - par][:, off + TT: off + TT + 3])
                    e.tensor_scalar(convA[:], pre_ext[par][:, off: off + TT],
                                    wconv_s[:, oc * 4: oc * 4 + 1], None, op0=ALU.mult)
                    e.scalar_tensor_tensor(convB[:], pre_ext[par][:, off + 1: off + 1 + TT],
                                           wconv_s[:, oc * 4 + 1: oc * 4 + 2], convA[:],
                                           op0=ALU.mult, op1=ALU.add)
                    e.scalar_tensor_tensor(convA[:], pre_ext[par][:, off + 2: off + 2 + TT],
                                           wconv_s[:, oc * 4 + 2: oc * 4 + 3], convB[:],
                                           op0=ALU.mult, op1=ALU.add)
                    return e.scalar_tensor_tensor(convB[:], pre_ext[par][:, off + 3: off + 3 + TT],
                                                  wconv_s[:, oc * 4 + 3: oc * 4 + 4], convA[:],
                                                  op0=ALU.mult, op1=ALU.add)
                P.step("vector", v_conv, wait=[("ch", P.ch())])
                P.step("scalar", lambda e, oc=oc: e.activation(actt[:, oc * TT:(oc + 1) * TT], convB[:], AF.Silu),
                       wait=[("ch", P.ch())])
            act_done = P.ch()
            P.store("sync", rep_ap(act_d, [(T, 128)], [(128 * T, 6), (1, TT)], t0), actt[:],
                    wait=[("ch", act_done)])
            def mm_gab(e):
                last = None
                for kc in range(8):
                    e.matmul(ps_g[:], wgab_s[:, kc * 24: kc * 24 + 8],
                             xnt[:, kc * TT:(kc + 1) * TT], start=(kc == 0), stop=(kc == 7))
                for kc in range(8):
                    e.matmul(ps_al[:], wgab_s[:, kc * 24 + 8: kc * 24 + 16],
                             xnt[:, kc * TT:(kc + 1) * TT], start=(kc == 0), stop=(kc == 7))
                for kc in range(8):
                    last = e.matmul(ps_be[:], wgab_s[:, kc * 24 + 16: kc * 24 + 18],
                                    xnt[:, kc * TT:(kc + 1) * TT], start=(kc == 0), stop=(kc == 7))
                return last
            P.step("tensor", mm_gab, wait=[("ch", P.ch())])
            def s_gab(e):
                e.activation(gsp[:, 0:TT], ps_g[:], AF.Abs, bias=dtb_s[:, 0:1])
                e.activation(gsp[:, TT:2 * TT], gsp[:, 0:TT], AF.Exp, scale=-1.0)
                e.activation(gsp[:, 2 * TT:3 * TT], gsp[:, TT:2 * TT], AF.Ln, bias=1.0)
                e.activation(gsp[:, 3 * TT:4 * TT], ps_g[:], AF.Relu, bias=dtb_s[:, 0:1])
                e.activation(gsp[:, 4 * TT:5 * TT], ps_al[:], AF.Sigmoid)
                return e.activation(be2[:], ps_be[:], AF.Sigmoid)
            P.step("scalar", s_gab, wait=[("ch", P.ch())])
            def v_gab(e):
                e.tensor_tensor(g8[:], gsp[:, 2 * TT:3 * TT], gsp[:, 3 * TT:4 * TT], op=ALU.add)
                e.tensor_scalar(g8[:], g8[:], negA[:, 0:1], None, op0=ALU.mult)
                return e.tensor_scalar(al8[:], gsp[:, 4 * TT:5 * TT], 0.125, None, op0=ALU.mult)
            P.step("vector", v_gab, wait=[("ch", P.ch())])
            gdone = P.ch()
            P.store("sync", rep_ap(gab_d, [(T, 8)], [(1, TT)], t0), g8[:], wait=[("ch", gdone)])
            P.store("sync", rep_ap(gab_d, [(T, 8)], [(1, TT)], 8 * T + t0), al8[:], wait=[("ch", gdone)])
            P.store("sync", rep_ap(gab_d, [(T, 2)], [(1, TT)], 16 * T + t0), be2[:], wait=[("ch", gdone)])
            tile_end[i] = P.ch()
            if i + 2 < NTT:
                ld_x[i + 2] = load_xtile(i + 2, tile_end[i])
        P.finish()
    return nc


def build_scan0(nc=None, semctx=None, fused=False, debug=False):
    C = 128
    NCH = T // C
    standalone = nc is None
    if standalone:
        nc = bass.Bass("TRN2", target_bir_lowering=False, debug=False)
    dr = DramHelper(nc, fused)
    act_d = dr("act", [768, T], F32, "ExternalInput")
    gab_d = dr("gab", [24, T], F32, "ExternalInput")
    ltri_d = dr("ltri", [128, 128], F32, "ExternalInput", host=True)
    mneg_d = dr("mneg", [128, 128], F32, "ExternalInput", host=True)
    idf_d = dr("idf", [128, 128], F32, "ExternalInput", host=True)
    idb_d = dr("idb", [128, 128], BF16, "ExternalInput", host=True)
    eh8_d = dr("eh8", [8, 8 * 128], F32, "ExternalInput", host=True)
    onesr_d = dr("onesr", [1, 128], F32, "ExternalInput", host=True)
    nw8_d = dr("nw8", [128, 512], F32, "ExternalInput", host=True)
    o_d = dr("o_out", [T, 512], BF16, "ExternalOutput")
    if debug:
        dbg_d = {nm: nc.dram_tensor("dbg_" + nm, shp, F32, kind="ExternalOutput").ap()
                 for nm, shp in [("Gs", [128, 8]), ("Grow", [8, 128]), ("Msb", [128, 8 * 128]),
                                 ("lam8", [128, 8]), ("ktm", [128, 128]), ("ktm_r", [128, 128]), ("kss", [128, 2]), ("Asb", [128, 8 * 128]),
                                 ("qtm", [128, 512]), ("Oraw", [128, 512]), ("wcol8", [128, 8]),
                                 ("Hkk0", [64, 512]), ("obuf", [128, 512]), ("ms8", [128, 8]), ("rstd8", [128, 8]), ("O2", [128, 512]), ("O3", [128, 512]), ("nwbc", [128, 512])]}

    with ExitStack() as ctx:
        sb = lambda *a: ctx.enter_context(nc.sbuf_tensor(*a, allow_name_mangling=True))
        pst = lambda *a: ctx.enter_context(nc.psum_tensor(*a, allow_name_mangling=True))
        ltri = sb("ltri_s", [128, 128], F32)
        mneg = sb("mneg_s", [128, 128], F32)
        idf = sb("idf_s", [128, 128], F32)
        idb = sb("idb_s", [128, 128], BF16)
        eh8 = sb("eh8_s", [8, 8 * 128], F32)
        onesr = sb("onesr_s", [1, 128], F32)
        nwbc = sb("nwbc", [128, 512], F32)
        gab_t = [sb(f"gab_t{i}", [18, 128], F32) for i in range(2)]
        gat = [sb(f"gat{i}", [128, 18], F32) for i in range(2)]
        qf = [sb(f"qf{i}", [64, 8 * 128], BF16) for i in range(2)]
        kfv = [sb(f"kfv{i}", [64, 2 * 128], BF16) for i in range(2)]
        vfv = [sb(f"vfv{i}", [64, 2 * 128], BF16) for i in range(2)]
        Gs = sb("Gs", [128, 8], F32)
        lam8 = sb("lam8", [128, 8], F32)
        Grow = sb("Grow", [8, 128], F32)
        glrow = sb("glrow", [1, 8], F32)
        wpre = sb("wpre", [128, 8], F32)
        wcol8 = sb("wcol8", [128, 8], F32)
        lamC8 = sb("lamC8", [128, 8], F32)
        ktm_r = sb("ktm_r", [128, 2 * 64], BF16)
        vtm_r = sb("vtm_r", [128, 2 * 64], BF16)
        ktm = sb("ktm", [128, 2 * 64], BF16)
        vtm = sb("vtm", [128, 2 * 64], BF16)
        kfm = sb("kfm", [64, 2 * 128], BF16)
        ksq = sb("ksq", [128, 2 * 64], F32)
        kss = sb("kss", [128, 2], F32)
        krcp = sb("krcp", [128, 2], F32)
        Ysb = sb("Ysb", [128, 8 * 128], F32)
        Msb = sb("Msb", [128, 8 * 128], F32)
        Asb = sb("Asb", [128, 8 * 128], BF16)
        qtm = sb("qtm", [128, 8 * 64], BF16)
        xq1f = sb("xq1f", [64, 8 * 128], BF16)
        xq2f = sb("xq2f", [64, 8 * 128], BF16)
        T1 = sb("T1", [128, 8 * 64], F32)
        T2 = sb("T2", [128, 8 * 64], F32)
        xqtm = sb("xqtm", [128, 8 * 64], BF16)
        Oraw = sb("Oraw", [128, 8 * 64], F32)
        O2 = sb("O2", [128, 8 * 64], F32)
        Osq = sb("Osq", [128, 8 * 64], F32)
        ms8 = sb("ms8", [128, 8], F32)
        rstd8 = sb("rstd8", [128, 8], F32)
        O3 = sb("O3", [128, 8 * 64], F32)
        obuf = sb("obuf", [128, 512], BF16)
        Kwb = sb("Kwb", [128, 8 * 64], BF16)
        Hkk = [sb(f"Hkk{i}", [64, 8 * 64], F32) for i in range(2)]
        Hkv = [sb(f"Hkv{i}", [64, 8 * 64], F32) for i in range(2)]
        Hkkbf = sb("Hkkbf", [64, 8 * 64], BF16)
        Hkvbf = sb("Hkvbf", [64, 8 * 64], BF16)
        psA = pst("psA", [128, 8 * 128], F32)
        psB = pst("psB", [128, 8 * 64], F32)
        psC = pst("psC", [128, 8 * 64], F32)
        psD = pst("psD", [128, 1024], BF16)
        psE = pst("psE", [128, 128], F32)

        cpool, cemit = setup_consts(nc, (semctx or ctx), [1e-6, -0.5, -1.0], uid='s0')
        P = Phase(nc, ctx, semctx)
        P.step("gpsimd", cemit)

        P.load("sync", ltri[:], ltri_d)
        P.load("sync", mneg[:], mneg_d)
        P.load("sync", idf[:], idf_d)
        P.load("sync", eh8[:], eh8_d)
        P.load("sync", onesr[:], onesr_d)
        lds = P.load("sync", nwbc[:], nw8_d)
        ldg = P.load("gpsimd", idb[:], idb_d)

        def zinit(e):
            e.memset(Hkk[1][:], 0.0)
            e.memset(Hkv[1][:], 0.0)
            e.memset(Hkkbf[:], 0.0)
            return e.memset(Hkvbf[:], 0.0)
        P.step("gpsimd", zinit)

        ld_q = {}
        def load_chunk(j, gate):
            t0 = j * C
            par = j % 2
            P.load("sync", gab_t[par][:], rep_ap(gab_d, [(T, 18)], [(1, C)], t0), wait=[("ch", gate)])
            P.load("gpsimd", qf[par][:], rep_ap(act_d, [(T, 64)], [(64 * T, 8), (1, C)], t0),
                   wait=[("ch", gate)])
            P.load("gpsimd", kfv[par][:], rep_ap(act_d, [(T, 64)], [(64 * T, 2), (1, C)], 512 * T + t0),
                   wait=[("ch", gate)])
            return P.load("gpsimd", vfv[par][:], rep_ap(act_d, [(T, 64)], [(64 * T, 2), (1, C)], 640 * T + t0),
                          wait=[("ch", gate)]), P.cnt["ld"]

        ld_q[0] = load_chunk(0, 0)
        ld_q[1] = load_chunk(1, 0)
        chunk_end = {}

        for j in range(NCH):
            t0 = j * C
            par = j % 2
            lgv, ldv = ld_q[j]
            qfj, kfj, vfj = qf[par], kfv[par], vfv[par]
            wld = [("lg", lgv), ("ld", ldv), ("ch", P.ch())]
            P.step("tensor", lambda e: e.transpose(psE[:, 0:18], gab_t[par][:], idf[0:18, 0:18]), wait=wld)
            P.step("vector", lambda e: e.tensor_copy(gat[par][:], psE[:, 0:18]), wait=[("ch", P.ch())])
            def s_gkv(e):
                e.matmul(psE[:, 32:40], ltri[:], gat[par][:, 0:8], start=True, stop=True)
                for kv in range(2):
                    e.transpose(psD[:, 64 * kv: 64 * kv + 64], kfj[:, 128 * kv:128 * (kv + 1)], idb[0:64, 0:64])
                last = None
                for kv in range(2):
                    last = e.transpose(psD[:, 128 + 64 * kv: 128 + 64 * kv + 64],
                                       vfj[:, 128 * kv:128 * (kv + 1)], idb[0:64, 0:64])
                return last
            P.step("tensor", s_gkv, wait=[("ch", P.ch())])
            def v_gkv(e):
                e.tensor_copy(Gs[:], psE[:, 32:40])
                e.tensor_copy(ktm_r[:], psD[:, 0:128])
                return e.tensor_copy(vtm_r[:], psD[:, 128:256])
            P.step("vector", v_gkv, wait=[("ch", P.ch())])
            def s_lamk(e):
                e.activation(lam8[:], Gs[:], AF.Exp)
                return e.activation(ksq[:], ktm_r[:], AF.Square)
            P.step("scalar", s_lamk, wait=[("ch", P.ch())])
            P.step("vector", lambda e: e.reduce_sum(kss[:], ksq.rearrange("p (kv f) -> p kv f", kv=2), axis=AX.X),
                   wait=[("ch", P.ch())])
            def s_krcp(e):
                e.activation(kss[:], kss[:], AF.Sqrt)
                e.activation(krcp[:], kss[:], AF.Ln, bias=1e-6)
                return e.activation(krcp[:], krcp[:], AF.Exp, scale=-1.0)
            P.step("scalar", s_krcp, wait=[("ch", P.ch())])
            def v_knorm(e):
                for kv in range(2):
                    e.tensor_scalar(ktm[:, 64 * kv:64 * (kv + 1)], ktm_r[:, 64 * kv:64 * (kv + 1)],
                                    krcp[:, kv:kv + 1], None, op0=ALU.mult)
                last = None
                for kv in range(2):
                    last = e.tensor_scalar(vtm[:, 64 * kv:64 * (kv + 1)], vtm_r[:, 64 * kv:64 * (kv + 1)],
                                           gat[par][:, 16 + kv:17 + kv], None, op0=ALU.mult)
                return last
            P.step("vector", v_knorm, wait=[("ch", P.ch())])
            def t_kfm(e):
                last = None
                for kv in range(2):
                    last = e.transpose(psD[0:64, 256 + 128 * kv: 256 + 128 * (kv + 1)],
                                       ktm[:, 64 * kv:64 * (kv + 1)], idb[:, :])
                return last
            P.step("tensor", t_kfm, wait=[("ch", P.ch())])
            P.step("vector", lambda e: e.tensor_copy(kfm[:], psD[0:64, 256:512]), wait=[("ch", P.ch())])
            P.step("tensor", lambda e: e.transpose(psE[0:8, 0:128], Gs[:], idf[:, :]), wait=[("ch", P.ch())])
            P.step("vector", lambda e: e.tensor_copy(Grow[:], psE[0:8, 0:128]), wait=[("ch", P.ch())])
            P.step("tensor", lambda e: e.transpose(psE[0:1, 120:128], Grow[:, 127:128], idf[0:8, 0:8]),
                   wait=[("ch", P.ch())])
            P.step("vector", lambda e: e.tensor_copy(glrow[:], psE[0:1, 120:128]), wait=[("ch", P.ch())])
            def t_bcast(e):
                e.matmul(psE[:, 96:104], onesr[0:1, :], glrow[0:1, :], start=True, stop=True)
                last = None
                for h in range(8):
                    last = e.matmul(psA[:, 128 * h:128 * (h + 1)], eh8[:, 128 * h:128 * (h + 1)],
                                    Grow[:], start=True, stop=True)
                return last
            P.step("tensor", t_bcast, wait=[("ch", P.ch())])
            def v_masks(e):
                e.tensor_tensor(wpre[:], psE[:, 96:104], Gs[:], op=ALU.subtract)
                last = None
                for h in range(8):
                    last = e.scalar_tensor_tensor(Ysb[:, 128 * h:128 * (h + 1)], psA[:, 128 * h:128 * (h + 1)],
                                                  Gs[:, h:h + 1], mneg[:], op0=ALU.subtract, op1=ALU.add)
                return last
            P.step("vector", v_masks, wait=[("ch", P.ch())])
            def s_masks(e):
                e.activation(Msb[:], Ysb[:], AF.Exp)
                e.activation(wcol8[:], wpre[:], AF.Exp)
                return e.activation(lamC8[:], psE[:, 96:104], AF.Exp)
            P.step("scalar", s_masks, wait=[("ch", P.ch())])

            def emit_apply(U_fm, is_last):
                def t_A(e):
                    last = None
                    for h in range(8):
                        kv = h // 4
                        last = e.matmul(psA[:, 128 * h:128 * (h + 1)], kfm[:, 128 * kv:128 * (kv + 1)],
                                        U_fm[:, 128 * h:128 * (h + 1)], start=True, stop=True)
                    return last
                P.step("tensor", t_A, wait=[("ch", P.ch())])
                def v_A(e):
                    last = None
                    for h in range(8):
                        last = e.tensor_tensor(Asb[:, 128 * h:128 * (h + 1)], psA[:, 128 * h:128 * (h + 1)],
                                               Msb[:, 128 * h:128 * (h + 1)], op=ALU.mult)
                    return last
                P.step("vector", v_A, wait=[("ch", P.ch())])
                def t_z(e):
                    for h in range(8):
                        kv = h // 4
                        rhs = vtm if is_last else ktm
                        e.matmul(psB[:, 64 * h:64 * (h + 1)], Asb[:, 128 * h:128 * (h + 1)],
                                 rhs[:, 64 * kv:64 * (kv + 1)], start=True, stop=True)
                    last = None
                    for h in range(8):
                        hs = Hkvbf if is_last else Hkkbf
                        last = e.matmul(psC[:, 64 * h:64 * (h + 1)], U_fm[:, 128 * h:128 * (h + 1)],
                                        hs[:, 64 * h:64 * (h + 1)], start=True, stop=True)
                    if U_fm is qf[par]:
                        for h in range(8):
                            last = e.transpose(psD[:, 64 * h:64 * (h + 1)],
                                               U_fm[:, 128 * h:128 * (h + 1)], idb[0:64, 0:64])
                    return last
                P.step("tensor", t_z, wait=[("ch", P.ch())])
                return None
            emit_apply(qfj, False)
            def v_comb1(e):
                e.tensor_copy(qtm[:], psD[:, 0:512])
                for h in range(8):
                    e.tensor_scalar(T1[:, 64 * h:64 * (h + 1)], psC[:, 64 * h:64 * (h + 1)],
                                    lam8[:, h:h + 1], -0.25, op0=ALU.mult, op1=ALU.mult)
                for h in range(8):
                    e.scalar_tensor_tensor(T2[:, 64 * h:64 * (h + 1)], psB[:, 64 * h:64 * (h + 1)],
                                           -0.25, qtm[:, 64 * h:64 * (h + 1)], op0=ALU.mult, op1=ALU.add)
                return e.tensor_tensor(xqtm[:], T1[:], T2[:], op=ALU.add)
            P.step("vector", v_comb1, wait=[("ch", P.ch())])
            def t_xf1(e):
                last = None
                for h in range(8):
                    last = e.transpose(psD[0:64, 128 * h:128 * (h + 1)],
                                       xqtm[:, 64 * h:64 * (h + 1)], idb[:, :])
                return last
            P.step("tensor", t_xf1, wait=[("ch", P.ch())])
            P.step("vector", lambda e: e.tensor_copy(xq1f[:], psD[0:64, :]), wait=[("ch", P.ch())])
            emit_apply(xq1f, False)
            def v_comb2(e):
                for h in range(8):
                    e.tensor_scalar(T1[:, 64 * h:64 * (h + 1)], psC[:, 64 * h:64 * (h + 1)],
                                    lam8[:, h:h + 1], -0.25, op0=ALU.mult, op1=ALU.mult)
                for h in range(8):
                    e.scalar_tensor_tensor(T2[:, 64 * h:64 * (h + 1)], psB[:, 64 * h:64 * (h + 1)],
                                           -0.25, qtm[:, 64 * h:64 * (h + 1)], op0=ALU.mult, op1=ALU.add)
                return e.tensor_tensor(xqtm[:], T1[:], T2[:], op=ALU.add)
            P.step("vector", v_comb2, wait=[("ch", P.ch())])
            def t_xf2(e):
                last = None
                for h in range(8):
                    last = e.transpose(psD[0:64, 128 * h:128 * (h + 1)],
                                       xqtm[:, 64 * h:64 * (h + 1)], idb[:, :])
                return last
            P.step("tensor", t_xf2, wait=[("ch", P.ch())])
            P.step("vector", lambda e: e.tensor_copy(xq2f[:], psD[0:64, :]), wait=[("ch", P.ch())])
            emit_apply(xq2f, True)
            def v_oraw(e):
                for h in range(8):
                    e.tensor_scalar(T1[:, 64 * h:64 * (h + 1)], psC[:, 64 * h:64 * (h + 1)],
                                    lam8[:, h:h + 1], None, op0=ALU.mult)
                e.tensor_tensor(Oraw[:], psB[:, :], T1[:], op=ALU.add)
                last = None
                for h in range(8):
                    last = e.tensor_scalar(O2[:, 64 * h:64 * (h + 1)], Oraw[:, 64 * h:64 * (h + 1)],
                                           gat[par][:, 8 + h:9 + h], None, op0=ALU.mult)
                return last
            P.step("vector", v_oraw, wait=[("ch", P.ch())])
            P.step("scalar", lambda e: e.activation(Osq[:], O2[:], AF.Square), wait=[("ch", P.ch())])
            P.step("vector", lambda e: e.reduce_sum(ms8[:], Osq.rearrange("p (h f) -> p h f", h=8), axis=AX.X),
                   wait=[("ch", P.ch())])
            P.step("vector", lambda e: e.tensor_scalar(rstd8[:], ms8[:], 1.0 / 64, 1e-6, op0=ALU.mult, op1=ALU.add),
                   wait=[("ch", P.ch())])
            def s_rstd(e):
                e.activation(rstd8[:], rstd8[:], AF.Ln)
                return e.activation(rstd8[:], rstd8[:], AF.Exp, scale=-0.5)
            P.step("scalar", s_rstd, wait=[("ch", P.ch())])
            def v_o(e):
                for h in range(8):
                    e.tensor_scalar(O3[:, 64 * h:64 * (h + 1)], O2[:, 64 * h:64 * (h + 1)],
                                    rstd8[:, h:h + 1], None, op0=ALU.mult)
                return e.tensor_tensor(obuf[:], O3[:], nwbc[:], op=ALU.mult)
            P.step("vector", v_o, wait=[("ch", P.ch())])
            odone = P.ch()
            P.store("sync", rep_ap(o_d, [(512, 128)], [(1, 512)], t0 * 512), obuf[:], wait=[("ch", odone)])
            def v_kw(e):
                last = None
                for h in range(8):
                    kv = h // 4
                    last = e.tensor_scalar(Kwb[:, 64 * h:64 * (h + 1)], ktm[:, 64 * kv:64 * (kv + 1)],
                                           wcol8[:, h:h + 1], None, op0=ALU.mult)
                return last
            P.step("vector", v_kw, wait=[("ch", P.ch())])
            def t_state(e):
                for h in range(8):
                    kv = h // 4
                    e.matmul(psB[0:64, 64 * h:64 * (h + 1)], Kwb[:, 64 * h:64 * (h + 1)],
                             ktm[:, 64 * kv:64 * (kv + 1)], start=True, stop=True)
                last = None
                for h in range(8):
                    kv = h // 4
                    last = e.matmul(psC[0:64, 64 * h:64 * (h + 1)], Kwb[:, 64 * h:64 * (h + 1)],
                                    vtm[:, 64 * kv:64 * (kv + 1)], start=True, stop=True)
                return last
            P.step("tensor", t_state, wait=[("ch", P.ch())])
            def v_state(e):
                for h in range(8):
                    e.scalar_tensor_tensor(Hkk[par][:, 64 * h:64 * (h + 1)], Hkk[1 - par][:, 64 * h:64 * (h + 1)],
                                           lamC8[0:64, h:h + 1], psB[0:64, 64 * h:64 * (h + 1)],
                                           op0=ALU.mult, op1=ALU.add)
                for h in range(8):
                    e.scalar_tensor_tensor(Hkv[par][:, 64 * h:64 * (h + 1)], Hkv[1 - par][:, 64 * h:64 * (h + 1)],
                                           lamC8[0:64, h:h + 1], psC[0:64, 64 * h:64 * (h + 1)],
                                           op0=ALU.mult, op1=ALU.add)
                e.tensor_copy(Hkkbf[:], Hkk[par][:])
                return e.tensor_copy(Hkvbf[:], Hkv[par][:])
            P.step("vector", v_state, wait=[("ch", P.ch())])
            if debug and j == 0:
                dd = P.ch()
                for nm, tile in [("Gs", Gs), ("Grow", Grow), ("Msb", Msb), ("lam8", lam8),
                                 ("ktm", ktm), ("ktm_r", ktm_r), ("kss", kss), ("Asb", Asb), ("qtm", qtm), ("Oraw", Oraw),
                                 ("wcol8", wcol8), ("Hkk0", Hkk[0]), ("obuf", obuf), ("ms8", ms8), ("rstd8", rstd8), ("O2", O2), ("O3", O3), ("nwbc", nwbc)]:
                    scr = sb("dbg_s_" + nm, list(tile.shape), F32)
                    P.step("vector", lambda e, scr=scr, tile=tile: e.tensor_copy(scr[:], tile[:]),
                           wait=[("ch", P.ch())])
                    P.store("sync", dbg_d[nm], scr[:], wait=[("ch", P.ch())])
            chunk_end[j] = P.ch()
            if j + 2 < NCH:
                ld_q[j + 2] = load_chunk(j + 2, chunk_end[j])
        P.finish()
    return nc


def _scan0_inputs(ins, core):
    b, g = core // 2, core % 2
    C = 128
    s = np.arange(C)[:, None]
    t = np.arange(C)[None, :]
    ltri = (s <= t).astype(np.float32)
    mneg = np.where(s <= t, 0.0, -1e30).astype(np.float32)
    idf = np.eye(128, dtype=np.float32)
    idb = np.eye(128).astype(BF16NP)
    eh8 = np.zeros((8, 8 * 128), np.float32)
    for h in range(8):
        eh8[h, 128 * h:128 * (h + 1)] = 1.0
    nw8 = np.tile(np.tile(ins["L0_attn_norm_w"], 8)[None, :].astype(np.float32), (128, 1))
    return {
        "ltri": ltri, "mneg": mneg, "idf": idf, "idb": idb, "eh8": eh8,
        "onesr": np.ones((1, 128), np.float32), "nw8": np.ascontiguousarray(nw8),
    }


def sb_ap(tile, row_len, part_count, free_dims, offset_cols=0, part0=0):
    base = tile[:]
    return bass.AP(tensor=base.tensor, offset=part0 * row_len + offset_cols,
                   ap=[[row_len, part_count]] + [[s, c] for (s, c) in free_dims])


def build_ufold(nc=None, semctx=None, fused=False, stop_after=None):
    standalone = nc is None
    if standalone:
        nc = bass.Bass("TRN2", target_bir_lowering=False, debug=False)
    dr = DramHelper(nc, fused)
    din = {}
    for nm, shp, dt in [
        ("l_self_up", [7, 4096], F32), ("l_part_up", [7, 4096], F32),
        ("l_self_dn", [7, 4096], F32), ("l_part_dn", [7, 4096], F32),
        ("h_self_up", [128, 5 * 1024], F32), ("h_part_up", [128, 5 * 1024], F32),
        ("h_self_dn", [128, 5 * 1024], F32), ("h_part_dn", [128, 5 * 1024], F32),
        ("idfu", [128, 128], F32), ("idbu", [128, 128], BF16),
        ("onesrb", [1, 128], F32), ("selu", [128, 8 * 128], F32),
        ("wexpT_h", [8, 32 * 128 * 128], F32), ("bexp_t", [128, 32], F32),
        ("upbias", [1, 512], F32), ("wcon_sl", [32, 128 * 128], F32),
    ]:
        din[nm] = dr(nm, shp, dt, "ExternalInput", host=True)
    wexpf_d = dr("wexpf", [128, 8 * 512], BF16, "ExternalOutput")
    bupf_d = dr("bupf", [128, 4], F32, "ExternalOutput")
    wconf_d = dr("wconf", [128, 32 * 128], BF16, "ExternalOutput")

    with ExitStack() as ctx:
        sb = lambda *a: ctx.enter_context(nc.sbuf_tensor(*a, allow_name_mangling=True))
        pst = lambda *a: ctx.enter_context(nc.psum_tensor(*a, allow_name_mangling=True))
        idf = sb("idf_s", [128, 128], F32)
        idb = sb("idb_s", [128, 128], BF16)
        onesrb = sb("onesrb_s", [1, 128], BF16)
        selu = sb("selu_s", [128, 8 * 128], BF16)
        coefHs = sb("coefHs", [128, 5 * 1024], BF16)
        coefHp = sb("coefHp", [128, 5 * 1024], BF16)
        selfrow = sb("selfrow", [1, 4096], BF16)
        partrow = sb("partrow", [1, 4096], BF16)
        Lts = [sb(f"Lts{i}", [128, 32 * 128], F32) for i in range(2)]
        Lnon = sb("Lnon", [128, 32 * 128], F32)
        MhA = [sb(f"MhA{i}", [128, 1024], F32) for i in range(2)]
        scr512 = sb("scr512", [128, 512], F32)
        t1g = sb("t1g", [128, 1024], F32)
        MhT = sb("MhT", [128, 8 * 128], BF16)
        Mcols = sb("Mcols", [128, 128], F32)
        tmpU = sb("tmpU", [128, 128], BF16)
        UT = sb("UT", [128, 32 * 512], BF16)
        lhsS = [sb(f"lhsS{i}", [128, 32 * 128], BF16) for i in range(2)]

        outW = sb("outW", [128, 512], BF16)
        outB = sb("outB", [1, 512], F32)
        upb = sb("upb_s", [1, 512], F32)
        bexp = sb("bexp_s", [128, 32], BF16)
        psCf = pst("psCf", [128, 1024], F32)
        psCf2 = pst("psCf2", [128, 1024], F32)
        psF = pst("psF", [128, 512], F32)
        psT = pst("psT", [128, 512], BF16)
        psM = pst("psM", [128, 512], F32)
        psBias = pst("psBias", [1, 512], F32)

        P = Phase(nc, ctx, semctx)
        P.load("sync", idf[:], din["idfu"])
        P.load("sync", upb[:], din["upbias"])
        P.load("gpsimd", coefHs[:], din["h_self_up"])
        lds0 = P.load("gpsimd", coefHp[:], din["h_part_up"], sem="ld")
        P.load("gpsimd", idb[:], din["idbu"])
        P.load("gpsimd", onesrb[:], din["onesrb"])
        P.load("gpsimd", selu[:], din["selu"])
        P.load("gpsimd", bexp[:], din["bexp_t"])

        LROW = 32 * 128
        MROW = 1024

        def build_bfly(l_self_d, l_part_d, hs_gate):
            def vinit(e):
                for b in range(32):
                    e.tensor_copy(Lts[0][:, 128 * b:128 * (b + 1)], idf[:])
                e.memset(MhA[0][:], 0.0)
                last = None
                for b in range(32):
                    last = e.memset(MhA[0][:, 33 * b:33 * b + 1], 1.0)
                return last
            P.step("vector", vinit, wait=[("ld", lds0), ("ch", P.ch())])
            par = 0
            for s in range(7):
                sg = 1 << s
                lr1 = P.load("gpsimd", selfrow[:], rep_ap(l_self_d, [(4096, 1)], [(1, 4096)], 4096 * s),
                             wait=[("ch", P.ch())])
                lr2 = P.load("gpsimd", partrow[:], rep_ap(l_part_d, [(4096, 1)], [(1, 4096)], 4096 * s),
                             wait=[("ch", P.ch())])
                for cg in range(4):
                    def t_bc(e, cg=cg):
                        for half in range(2):
                            e.matmul(psCf[:, 512 * half:512 * (half + 1)], onesrb[0:1, :],
                                     selfrow[0:1, 1024 * cg + 512 * half: 1024 * cg + 512 * (half + 1)],
                                     start=True, stop=True)
                        last = None
                        for half in range(2):
                            last = e.matmul(psCf2[:, 512 * half:512 * (half + 1)], onesrb[0:1, :],
                                            partrow[0:1, 1024 * cg + 512 * half: 1024 * cg + 512 * (half + 1)],
                                            start=True, stop=True)
                        return last
                    P.step("tensor", t_bc, wait=[("lg", lr2), ("ch", P.ch())])
                    def v_stage(e, cg=cg, s=s, sg=sg, par=par):
                        OLD = Lts[par]
                        NEW = Lts[1 - par]
                        last = None
                        for bl in range(8):
                            b = 8 * cg + bl
                            off = 128 * b
                            cf = 128 * bl
                            e.tensor_tensor(t1g[:, 128 * bl:128 * (bl + 1)],
                                            OLD[:, off:off + 128], psCf[:, cf:cf + 128], op=ALU.mult)
                            for a in range(2):
                                dsl = sb_ap(Lts[par], LROW, 128, [(2 * sg, 64 // sg), (1, sg)],
                                            off + (1 - a) * sg)
                                csl = bass.AP(tensor=psCf2[:].tensor, offset=psCf2[:].offset + cf + a * sg,
                                              ap=[[1024, 128], [2 * sg, 64 // sg], [1, sg]])
                                e.tensor_tensor(scr512[:, 0:64], dsl, csl, op=ALU.mult)
                                t1sl = sb_ap(t1g, 1024, 128, [(2 * sg, 64 // sg), (1, sg)], 128 * bl + a * sg)
                                nsl = sb_ap(Lts[1 - par], LROW, 128, [(2 * sg, 64 // sg), (1, sg)],
                                            off + a * sg)
                                last = e.tensor_tensor(nsl, t1sl, scr512[:, 0:64], op=ALU.add)
                        return last
                    P.step("vector", v_stage, wait=[("ch", P.ch())])
                par = 1 - par
            lpar = par
            mpar = 0
            for s in range(5):
                sgb = 1 << s
                w = 32 * sgb
                def v_mh(e, s=s, sgb=sgb, w=w, mpar=mpar):
                    MOLD, MNEW = MhA[mpar], MhA[1 - mpar]
                    e.tensor_tensor(t1g[:], MOLD[:], coefHs[:, 1024 * s: 1024 * (s + 1)] if hs_gate is None
                                    else coefHs[:, 1024 * s: 1024 * (s + 1)], op=ALU.mult)
                    last = None
                    for a in range(2):
                        dsl = sb_ap(MhA[mpar], MROW, 128, [(2 * w, 16 // sgb), (1, w)], (1 - a) * w)
                        csl = sb_ap(coefHp, 5 * 1024, 128, [(2 * w, 16 // sgb), (1, w)], 1024 * s + a * w)
                        e.tensor_tensor(scr512[:], dsl, csl, op=ALU.mult)
                        t1sl = sb_ap(t1g, 1024, 128, [(2 * w, 16 // sgb), (1, w)], a * w)
                        nsl = sb_ap(MhA[1 - mpar], MROW, 128, [(2 * w, 16 // sgb), (1, w)], a * w)
                        last = e.tensor_tensor(nsl, t1sl, scr512[:], op=ALU.add)
                    return last
                P.step("vector", v_mh, wait=[("ch", P.ch())])
                mpar = 1 - mpar
            for grp in range(8):
                def t_tr(e, grp=grp, lpar=lpar):
                    last = None
                    for i in range(4):
                        b = 4 * grp + i
                        last = e.transpose(psF[:, 128 * i:128 * (i + 1)],
                                          Lts[lpar][:, 128 * b:128 * (b + 1)], idf[:])
                    return last
                P.step("tensor", t_tr, wait=[("ch", P.ch())])
                P.step("vector", lambda e, grp=grp: e.tensor_copy(Lnon[:, 512 * grp:512 * (grp + 1)], psF[:]),
                       wait=[("ch", P.ch())])
            return mpar

        if stop_after == "A":
            P.finish()
            return nc
        mpar_up = build_bfly(din["l_self_up"], din["l_part_up"], None)
        if stop_after == "B":
            P.finish()
            return nc
        for grp in range(2):
            def t_mt(e, grp=grp):
                last = None
                for i in range(4):
                    cidx = 4 * grp + i
                    last = e.transpose(psF[:, 128 * i:128 * (i + 1)],
                                      MhA[mpar_up][:, 128 * cidx:128 * (cidx + 1)], idf[:])
                return last
            P.step("tensor", t_mt, wait=[("ch", P.ch())])
            P.step("vector", lambda e, grp=grp: e.tensor_copy(MhT[:, 512 * grp:512 * (grp + 1)], psF[:]),
                   wait=[("ch", P.ch())])
        if stop_after == "C":
            P.finish()
            return nc
        def t_mc(e):
            last = None
            for pc in range(8):
                last = e.matmul(psF[:, 0:128], MhT[:, 128 * pc:128 * (pc + 1)],
                                selu[:, 128 * pc:128 * (pc + 1)], start=(pc == 0), stop=(pc == 7))
            return last
        P.step("tensor", t_mc, wait=[("ch", P.ch())])
        P.step("vector", lambda e: e.tensor_copy(Mcols[:], psF[:, 0:128]), wait=[("ch", P.ch())])
        if stop_after == "D":
            P.finish()
            return nc
        tmpU4 = sb("tmpU4", [128, 4 * 128], BF16)
        for bp in range(32):
            def v_ts(e, bp=bp):
                last = None
                for bb in range(4):
                    last = e.tensor_scalar(tmpU4[:, 128 * bb:128 * (bb + 1)],
                                           Lnon[:, 128 * bp:128 * (bp + 1)],
                                           Mcols[:, 32 * bb + bp:32 * bb + bp + 1], None, op0=ALU.mult)
                return last
            P.step("vector", v_ts, wait=[("ch", P.ch())])
            def t_tr4(e, bp=bp):
                last = None
                for bb in range(4):
                    last = e.transpose(psT[:, 128 * bb:128 * (bb + 1)],
                                      tmpU4[:, 128 * bb:128 * (bb + 1)], idb[:])
                return last
            P.step("tensor", t_tr4, wait=[("ch", P.ch())])
            P.step("vector", lambda e, bp=bp: e.tensor_copy(UT[:, 512 * bp:512 * (bp + 1)], psT[:]),
                   wait=[("ch", P.ch())])
        if stop_after == "E":
            P.finish()
            return nc
        lw = {0: P.load("gpsimd", lhsS[0][:], rep_ap(din["wexpT_h"], [(128, 128)], [(128 * 128, 32), (1, 128)], 0),
                        wait=[("ch", P.ch())])}
        for ic in range(8):
            if ic + 1 < 8:
                lw[ic + 1] = P.load("gpsimd", lhsS[(ic + 1) % 2][:],
                                    rep_ap(din["wexpT_h"], [(128, 128)], [(128 * 128, 32), (1, 128)],
                                           (ic + 1) * 32 * 128 * 128),
                                    wait=[("ch", P.ch() if ic == 0 else fold_done[ic - 1])])
            def t_fold(e, ic=ic):
                last = None
                for jc in range(32):
                    last = e.matmul(psM[:], lhsS[ic % 2][:, 128 * jc:128 * (jc + 1)],
                                    UT[:, 512 * jc:512 * (jc + 1)], start=(jc == 0), stop=(jc == 31))
                return last
            P.step("tensor", t_fold, wait=[("lg", lw[ic]), ("ch", P.ch())])
            P.step("vector", lambda e: e.tensor_copy(outW[:], psM[:]), wait=[("ch", P.ch())])
            if ic == 0:
                fold_done = {}
            fold_done[ic] = P.ch()
            P.store("sync", rep_ap(wexpf_d, [(8 * 512, 128)], [(1, 512)], ic * 512), outW[:],
                    wait=[("ch", fold_done[ic])])
        if stop_after == "F":
            P.finish()
            return nc
        def t_bias(e):
            last = None
            for jc in range(32):
                last = e.matmul(psBias[:], bexp[:, jc:jc + 1], UT[:, 512 * jc:512 * (jc + 1)],
                                start=(jc == 0), stop=(jc == 31))
            return last
        P.step("tensor", t_bias, wait=[("ch", P.ch())])
        P.step("vector", lambda e: e.tensor_tensor(outB[:], psBias[:], upb[:], op=ALU.add),
               wait=[("ch", P.ch())])
        bupscr_d = dr("bup_scr", [1, 512], F32, "Internal")
        P.store("sync", bupscr_d, outB[:], wait=[("ch", P.ch())])
        with nc.allow_non_contiguous_dma(reason="tiny 512-elem bias relayout"):
            P.store("sync", bupf_d, rep_ap(bupscr_d, [(1, 128)], [(128, 4)]),
                    wait=[("st", P.cnt["st"])])

        if stop_after == "G":
            P.finish()
            return nc
        gate = P.ch()
        P.load("gpsimd", coefHs[:], din["h_self_dn"], wait=[("ch", gate)])
        lds1 = P.load("gpsimd", coefHp[:], din["h_part_dn"], sem="ld", wait=[("ch", gate)])
        lgw2 = P.load("gpsimd", UT[:, 0:32 * 128],
                      rep_ap(din["wcon_sl"], [(128, 128)], [(128 * 128, 32), (1, 128)]),
                      wait=[("ch", gate)])
        P.step("gpsimd", lambda e: e.nop(), wait=[("lg", lgw2), ("ld", lds1)])
        mpar_dn = build_bfly(din["l_self_dn"], din["l_part_dn"], None)
        if stop_after == "H":
            P.finish()
            return nc
        for mc in range(32):
            def v_dnl(e, mc=mc):
                last = None
                for jc in range(32):
                    last = e.tensor_scalar(lhsS[0][:, 128 * jc:128 * (jc + 1)],
                                           Lnon[:, 128 * mc:128 * (mc + 1)],
                                           MhA[mpar_dn][:, 32 * jc + mc:32 * jc + mc + 1], None, op0=ALU.mult)
                return last
            P.step("vector", v_dnl, wait=[("ch", P.ch())])
            def t_dfold(e, mc=mc):
                last = None
                for jc in range(32):
                    last = e.matmul(psM[:, 0:128], lhsS[0][:, 128 * jc:128 * (jc + 1)],
                                    UT[:, 128 * jc:128 * (jc + 1)], start=(jc == 0), stop=(jc == 31))
                return last
            P.step("tensor", t_dfold, wait=[("ch", P.ch())])
            P.step("vector", lambda e: e.tensor_copy(outW[:, 0:128], psM[:, 0:128]), wait=[("ch", P.ch())])
            P.store("sync", rep_ap(wconf_d, [(32 * 128, 128)], [(1, 128)], mc * 128), outW[:, 0:128],
                    wait=[("ch", P.ch())])
            if stop_after == "I":
                break
        P.finish()
    return nc


def _bfly_coefs(weights):
    l_self = np.zeros((7, 4096), np.float32)
    l_part = np.zeros((7, 4096), np.float32)
    for s in range(7):
        srt = 1 << s
        nb = 4096 // (2 * srt)
        w = weights[s].reshape(nb, srt, 2, 2)
        i = np.arange(4096)
        q = i // (2 * srt)
        a = (i // srt) % 2
        lo = i % srt
        l_self[s] = w[q, lo, a, a]
        l_part[s] = w[q, lo, a, 1 - a]
    h_self = np.zeros((5, 128, 32), np.float32)
    h_part = np.zeros((5, 128, 32), np.float32)
    for s5 in range(5):
        s = 7 + s5
        srt = 1 << s
        nb = 4096 // (2 * srt)
        w = weights[s].reshape(nb, srt, 2, 2)
        i = np.arange(4096)
        q = i // (2 * srt)
        a = (i // srt) % 2
        lo = i % srt
        sc = w[q, lo, a, a].reshape(32, 128)
        pc = w[q, lo, a, 1 - a].reshape(32, 128)
        h_self[s5] = sc.T
        h_part[s5] = pc.T
    hs = np.repeat(h_self.transpose(1, 0, 2).reshape(128, 5 * 32), 32, axis=1)
    hp = np.repeat(h_part.transpose(1, 0, 2).reshape(128, 5 * 32), 32, axis=1)
    return l_self, l_part, np.ascontiguousarray(hs), np.ascontiguousarray(hp)


def _ufold_inputs(ins, core):
    c = core
    lsu, lpu, hsu, hpu = _bfly_coefs(ins["L0_ffn_up_weights"])
    lsd, lpd, hsd, hpd = _bfly_coefs(ins["L0_ffn_down_weights"])
    sel = np.zeros((1024, 128), np.float32)
    for col in range(128):
        bb, bp = col // 32, col % 32
        sel[32 * (4 * c + bb) + bp, col] = 1.0
    Wexp = ins["L0_ffn_W_exp"]
    wexpT = np.ascontiguousarray(
        Wexp.T.reshape(32, 128, 8, 128).transpose(2, 0, 1, 3).reshape(8, 32 * 128 * 128))
    bexp_t = np.ascontiguousarray(ins["L0_ffn_b_exp"].reshape(32, 128).T)
    Wcon = ins["L0_ffn_W_con"]
    wcon_sl = np.ascontiguousarray(
        Wcon[:, 128 * c:128 * (c + 1)].reshape(32, 128 * 128))
    return {
        "l_self_up": lsu, "l_part_up": lpu, "l_self_dn": lsd, "l_part_dn": lpd,
        "h_self_up": hsu, "h_part_up": hpu, "h_self_dn": hsd, "h_part_dn": hpd,
        "idfu": np.eye(128, dtype=np.float32), "idbu": np.eye(128).astype(BF16NP),
        "onesrb": np.ones((1, 128), np.float32),
        "selu": np.ascontiguousarray(sel.reshape(8, 128, 128).transpose(1, 0, 2).reshape(128, 8 * 128)),
        "wexpT_h": wexpT, "bexp_t": bexp_t,
        "upbias": np.ascontiguousarray(ins["L0_ffn_up_bias"][512 * c:512 * (c + 1)].reshape(1, 512)),
        "wcon_sl": wcon_sl,
    }


def build_l0tail(nc=None, semctx=None, fused=False):
    NTOK = 2048
    NT4 = NTOK // TT
    standalone = nc is None
    if standalone:
        nc = bass.Bass("TRN2", target_bir_lowering=False, debug=False)
    dr = DramHelper(nc, fused)
    opair_d = dr("o_pair", [2 * T, 512], BF16, "ExternalInput")
    oidx_d = dr("oidx", [128, 32], mybir.dt.int32, "ExternalInput", host=True)
    x_d = dr("x_my", [D, NTOK], F32, "ExternalInput", host=True)
    wo_d = dr("wo_t", [128, 8 * 1024], F32, "ExternalInput", host=True)
    wexpA_d = dr("wexpfA", [1024, 4096], BF16, "ExternalInput")
    wconA_d = dr("wconfA", [1024, 4096], BF16, "ExternalInput")
    bupA_d = dr("bupfA", [1024, 4], F32, "ExternalInput")
    bcon_d = dr("bcon_t", [128, 8], F32, "ExternalInput", host=True)
    db_d = dr("db_t", [128, 8], F32, "ExternalInput", host=True)
    idb_d = dr("idb2", [128, 128], BF16, "ExternalInput", host=True)
    onesc_d = dr("onesc2", [128, 1], F32, "ExternalInput", host=True)
    onesr_d = dr("onesr2", [1, 128], F32, "ExternalInput", host=True)

    x1_d = dr("x1_my", [D, NTOK], F32, "ExternalOutput")
    xn1_d = dr("xn1_my", [D, NTOK], BF16, "ExternalOutput")

    with ExitStack() as ctx:
        sb = lambda *a: ctx.enter_context(nc.sbuf_tensor(*a, allow_name_mangling=True))
        pst = lambda *a: ctx.enter_context(nc.psum_tensor(*a, allow_name_mangling=True))
        wo_s = sb("wo_s", [128, 8 * 1024], BF16)
        idb = sb("idb_s", [128, 128], BF16)
        onesc = sb("onesc_s", [128, 1], BF16)
        onesr = sb("onesr_s", [1, 128], F32)
        oidx_s = sb("oidx_s", [128, 32], mybir.dt.int32)
        bup = sb("bup_s", [128, 32], F32)
        bfin = sb("bfin_s", [128, 8], F32)
        dbv = sb("dbv_s", [128, 8], F32)
        otmA = [sb(f"otmA{i}", [128, 4 * 512], BF16) for i in range(2)]
        otmB = [sb(f"otmB{i}", [128, 4 * 512], BF16) for i in range(2)]
        ofm = sb("ofm", [128, 8 * TT], BF16)
        xnt = sb("xnt", [128, 8 * TT], BF16)
        xmt = [sb(f"xmt{i}", [128, 8 * TT], F32) for i in range(2)]
        wexpS = [sb(f"wexpS{i}", [128, 8 * 128], BF16) for i in range(2)]
        wconS = [sb(f"wconS{i}", [128, 32 * 128], BF16) for i in range(2)]
        hbuf = sb("hbuf", [128, 32 * TT], BF16)
        x1f = sb("x1f", [128, 8 * TT], F32)
        x1bf = sb("x1bf", [128, 8 * TT], BF16)
        x1sq = sb("x1sq", [128, 8 * TT], BF16)
        lnscr = sb("lnscr", [128, TT], F32)
        stat = sb("stat", [1, 6 * TT], F32)
        xn1t = sb("xn1t", [128, 8 * TT], BF16)
        psA = pst("psA", [128, TT], F32)
        psH = pst("psH", [128, TT], F32)
        psT = pst("psT", [128, TT], BF16)
        ps_sum = pst("ps_sum", [1, TT], F32)
        ps_sq = pst("ps_sq", [1, TT], F32)
        ps_b1 = pst("ps_b1", [128, TT], F32)
        ps_b2 = pst("ps_b2", [128, TT], F32)

        cpool, cemit = setup_consts(nc, (semctx or ctx), [1e-5, -0.5], uid='l0')
        P = Phase(nc, ctx, semctx)
        P.step("gpsimd", cemit)
        P.load("sync", bup[:], rep_ap(bupA_d, [(4, 128)], [(512, 8), (1, 4)]))
        P.load("sync", bfin[:], bcon_d)
        P.load("sync", onesr[:], onesr_d)
        lds = P.load("sync", dbv[:], db_d)
        P.load("gpsimd", wo_s[:], wo_d)
        P.load("gpsimd", idb[:], idb_d)
        P.load("gpsimd", oidx_s[:], oidx_d)
        lgc = P.load("gpsimd", onesc[:], onesc_d)
        P.step("vector", lambda e: e.tensor_tensor(bfin[:], bfin[:], dbv[:], op=ALU.add),
               wait=[("ld", lds), ("ch", P.ch())])

        def emit_ln(src_f32, out_bf, gate_tick, extra_wait=()):
            def s_sq(e):
                e.activation(x1sq[:], src_f32[:], AF.Square)
                return e.activation(x1bf[:], src_f32[:], AF.Copy)
            P.step("scalar", s_sq, wait=[("ch", gate_tick)] + list(extra_wait))
            def mm_sums(e):
                for kc in range(8):
                    e.matmul(ps_sum[:], onesc[:, 0:1], x1bf[:, kc * TT:(kc + 1) * TT],
                             start=(kc == 0), stop=(kc == 7))
                last = None
                for kc in range(8):
                    last = e.matmul(ps_sq[:], onesc[:, 0:1], x1sq[:, kc * TT:(kc + 1) * TT],
                                    start=(kc == 0), stop=(kc == 7))
                return last
            P.step("tensor", mm_sums, wait=[("ch", P.ch())])
            def v_stats(e):
                e.tensor_scalar(stat[:, 0:TT], ps_sum[:], 1.0 / D, None, op0=ALU.mult)
                e.tensor_scalar(stat[:, TT:2 * TT], ps_sq[:], 1.0 / D, None, op0=ALU.mult)
                e.tensor_tensor(stat[:, 2 * TT:3 * TT], stat[:, 0:TT], stat[:, 0:TT], op=ALU.mult)
                return e.tensor_tensor(stat[:, 3 * TT:4 * TT], stat[:, TT:2 * TT],
                                       stat[:, 2 * TT:3 * TT], op=ALU.subtract)
            P.step("vector", v_stats, wait=[("ch", P.ch())])
            def s_rstd(e):
                e.activation(stat[:, 4 * TT:5 * TT], stat[:, 3 * TT:4 * TT], AF.Ln, bias=1e-5)
                return e.activation(stat[:, 4 * TT:5 * TT], stat[:, 4 * TT:5 * TT], AF.Exp, scale=-0.5)
            P.step("scalar", s_rstd, wait=[("ch", P.ch())])
            def mm_bc(e):
                e.matmul(ps_b1[:], onesr[0:1, :], stat[0:1, 0:TT], start=True, stop=True)
                return e.matmul(ps_b2[:], onesr[0:1, :], stat[0:1, 4 * TT:5 * TT], start=True, stop=True)
            P.step("tensor", mm_bc, wait=[("ch", P.ch())])
            def v_xn(e):
                last = None
                for kc in range(8):
                    e.tensor_tensor(lnscr[:], src_f32[:, kc * TT:(kc + 1) * TT], ps_b1[:], op=ALU.subtract)
                    last = e.tensor_tensor(out_bf[:, kc * TT:(kc + 1) * TT], lnscr[:], ps_b2[:], op=ALU.mult)
                return last
            P.step("vector", v_xn, wait=[("ch", P.ch())])

        ldq = {}
        def load_tile(i, gate):
            t0 = i * TT
            for tc in range(4):
                for half in range(2):
                    dst = (otmA if half == 0 else otmB)[i % 2]
                    P.load_idx("gpsimd", dst[:, tc * 512:(tc + 1) * 512], opair_d,
                               oidx_s[:, i * 8 + tc * 2 + half: i * 8 + tc * 2 + half + 1],
                               wait=[("ch", gate), ("lg", lgc)])
            lgo = P.cnt["lg"]
            ldv = P.load("sync", xmt[i % 2][:], rep_ap(x_d, [(NTOK, 128)], [(128 * NTOK, 8), (1, TT)], t0),
                         wait=[("ch", gate)])
            return ldv, lgo

        ldq[0] = load_tile(0, 0)
        ldq[1] = load_tile(1, 0)
        tile_end = {}
        for i in range(NT4):
            t0 = i * TT
            par = i % 2
            ldv, lgo = ldq[i]
            emit_ln(xmt[par], xnt, P.ch(), extra_wait=[("ld", ldv)])
            for kc in range(8):
                def t_tr(e, kc=kc, par=par):
                    last = None
                    src = otmA[par] if kc < 4 else otmB[par]
                    kk = kc % 4
                    for tc in range(4):
                        last = e.transpose(psT[:, 128 * tc:128 * (tc + 1)],
                                          src[:, tc * 512 + 128 * kk: tc * 512 + 128 * (kk + 1)],
                                          idb[:])
                    return last
                P.step("tensor", t_tr, wait=[("lg", lgo), ("ch", P.ch())])
                P.step("vector", lambda e, kc=kc: e.tensor_copy(ofm[:, TT * kc:TT * (kc + 1)], psT[:]),
                       wait=[("ch", P.ch())])
            for oc in range(8):
                def t_wo(e, oc=oc):
                    last = None
                    for kc in range(8):
                        last = e.matmul(psA[:], wo_s[:, kc * 1024 + oc * 128: kc * 1024 + (oc + 1) * 128],
                                        ofm[:, TT * kc:TT * (kc + 1)], start=(kc == 0), stop=(kc == 7))
                    return last
                P.step("tensor", t_wo, wait=[("ch", P.ch())])
                P.step("vector", lambda e, oc=oc, par=par: e.tensor_tensor(
                    x1f[:, TT * oc:TT * (oc + 1)], xmt[par][:, TT * oc:TT * (oc + 1)], psA[:], op=ALU.add),
                    wait=[("ch", P.ch())])
            lwx = {0: P.load("gpsimd", wexpS[0][:],
                             rep_ap(wexpA_d, [(4096, 128)], [(512, 8), (1, 128)], 0),
                             wait=[("ch", P.ch())])}
            for hc in range(32):
                if hc + 1 < 32:
                    hc2 = hc + 1
                    off2 = 128 * (hc2 // 4) * 4096 + (hc2 % 4) * 128
                    lwx[hc2] = P.load("gpsimd", wexpS[hc2 % 2][:],
                                      rep_ap(wexpA_d, [(4096, 128)], [(512, 8), (1, 128)], off2),
                                      wait=[("ch", P.ch() if hc == 0 else hdone[hc - 1])])
                def t_h(e, hc=hc):
                    last = None
                    for kc in range(8):
                        last = e.matmul(psH[:], wexpS[hc % 2][:, kc * 128:(kc + 1) * 128],
                                        xnt[:, TT * kc:TT * (kc + 1)], start=(kc == 0), stop=(kc == 7))
                    return last
                P.step("tensor", t_h, wait=[("lg", lwx[hc]), ("ch", P.ch())])
                if hc == 0:
                    hdone = {}
                P.step("scalar", lambda e, hc=hc: e.activation(
                    hbuf[:, TT * hc:TT * (hc + 1)], psH[:], AF.Gelu, bias=bup[:, hc:hc + 1]),
                    wait=[("ch", P.ch())])
                hdone[hc] = P.ch()
            lwc = {0: P.load("gpsimd", wconS[0][:],
                             rep_ap(wconA_d, [(4096, 128)], [(128, 32), (1, 128)], 0),
                             wait=[("ch", P.ch())])}
            for oc in range(8):
                if oc + 1 < 8:
                    lwc[oc + 1] = P.load("gpsimd", wconS[(oc + 1) % 2][:],
                                         rep_ap(wconA_d, [(4096, 128)], [(128, 32), (1, 128)],
                                                128 * (oc + 1) * 4096),
                                         wait=[("ch", P.ch() if oc == 0 else ydone[oc - 1])])
                def t_y(e, oc=oc):
                    last = None
                    for mc in range(32):
                        last = e.matmul(psH[:], wconS[oc % 2][:, mc * 128:(mc + 1) * 128],
                                        hbuf[:, TT * mc:TT * (mc + 1)], start=(mc == 0), stop=(mc == 31))
                    return last
                P.step("tensor", t_y, wait=[("lg", lwc[oc]), ("ch", P.ch())])
                if oc == 0:
                    ydone = {}
                P.step("vector", lambda e, oc=oc: e.scalar_tensor_tensor(
                    x1f[:, TT * oc:TT * (oc + 1)], psH[:], bfin[:, oc:oc + 1],
                    x1f[:, TT * oc:TT * (oc + 1)], op0=ALU.add, op1=ALU.add),
                    wait=[("ch", P.ch())])
                ydone[oc] = P.ch()
            x1done = P.ch()
            P.store("sync", rep_ap(x1_d, [(NTOK, 128)], [(128 * NTOK, 8), (1, TT)], t0), x1f[:],
                    wait=[("ch", x1done)])
            emit_ln(x1f, xn1t, P.ch())
            P.store("sync", rep_ap(xn1_d, [(NTOK, 128)], [(128 * NTOK, 8), (1, TT)], t0), xn1t[:],
                    wait=[("ch", P.ch())])
            tile_end[i] = P.ch()
            if i + 2 < NT4:
                ldq[i + 2] = load_tile(i + 2, tile_end[i])
        P.finish()
    if standalone:
        return nc


def _l0tail_inputs(ins, core, o_pair_b, wexpfA, bupfA, wconfA):
    b, g = core // 2, core % 2
    t0 = 2048 * g
    x_my = np.ascontiguousarray(ins["L0_attn_x"][b].T[:, t0:t0 + 2048])
    Wo = ins["L0_attn_Wo"]
    wo_t = np.ascontiguousarray(Wo.reshape(8, 128, 1024).transpose(1, 0, 2).reshape(128, 8 * 1024))
    oidx = np.zeros((128, 32), np.int32)
    p = np.arange(128)
    for i in range(4):
        for tc in range(4):
            for half in range(2):
                oidx[:, i * 8 + tc * 2 + half] = half * 4096 + 2048 * g + i * 512 + tc * 128 + p
    return {
        "o_pair": o_pair_b, "oidx": oidx, "x_my": x_my, "wo_t": wo_t,
        "wexpfA": wexpfA, "wconfA": wconfA, "bupfA": bupfA,
        "bcon_t": np.ascontiguousarray(ins["L0_ffn_b_con"].reshape(8, 128).T),
        "db_t": np.ascontiguousarray(ins["L0_ffn_down_bias"].reshape(8, 128).T),
        "idb2": np.eye(128).astype(BF16NP),
        "onesc2": np.ones((128, 1), np.float32),
        "onesr2": np.ones((1, 128), np.float32),
    }


def build_pre1(nc=None, semctx=None, fused=False):
    standalone = nc is None
    if standalone:
        nc = bass.Bass("TRN2", target_bir_lowering=False, debug=False)
    dr = DramHelper(nc, fused)
    xn1_d = dr("xn1_pair", [2048, 2048], BF16, "ExternalInput")
    win_d = dr("w_in_t", [128, 8 * 2048], F32, "ExternalInput", host=True)
    cw3_d = dr("cw3", [1024, 3], F32, "ExternalInput", host=True)
    cb_d = dr("cb", [128, 8], F32, "ExternalInput", host=True)
    hf_d = dr("h_f", [1024, T], F32, "ExternalOutput")
    hbf_d = dr("h_bf", [1024, T], BF16, "ExternalOutput")
    sg_d = dr("sg", [1024, T], BF16, "ExternalOutput")
    pre_d = dr("pre_scr", [1024, T + 2], F32, "Internal")

    with ExitStack() as ctx:
        sb = lambda *a: ctx.enter_context(nc.sbuf_tensor(*a, allow_name_mangling=True))
        pst = lambda *a: ctx.enter_context(nc.psum_tensor(*a, allow_name_mangling=True))
        win_s = sb("win_s", [128, 8 * 2048], BF16)
        cw3_s = sb("cw3_s", [128, 8 * 3], F32)
        cb_s = sb("cb_s", [128, 8], F32)
        xnt = [sb(f"xnt{i}", [128, 8 * TT], BF16) for i in range(2)]
        innt = sb("innt", [128, 8 * TT], F32)
        sgt = sb("sgt", [128, 8 * TT], BF16)
        prw = [sb(f"prw{i}", [128, 8 * (TT + 2)], F32) for i in range(2)]
        convA = sb("convA", [128, TT], F32)
        convB = sb("convB", [128, TT], F32)
        ht = sb("ht", [128, 8 * TT], F32)
        htb = sb("htb", [128, 8 * TT], BF16)
        zcol = sb("zcol", [128, 8], F32)
        psI = pst("psI", [128, TT], F32)
        psG = pst("psG", [128, TT], F32)

        P = Phase(nc, ctx, semctx)
        P.load("sync", cw3_s[:], rep_ap(cw3_d, [(3, 128)], [(384, 8), (1, 3)]))
        lds = P.load("sync", cb_s[:], cb_d)
        lgw = P.load("gpsimd", win_s[:], win_d)
        P.step("gpsimd", lambda e: e.memset(zcol[:], 0.0))
        zdone = P.ch()
        with nc.allow_non_contiguous_dma(reason="one-time 1-col zero pad stores"):
            P.store("sync", rep_ap(pre_d, [(T + 2, 128)], [(128 * (T + 2), 8), (1, 1)], 0),
                    zcol[:], wait=[("ch", zdone)])
            P.store("sync", rep_ap(pre_d, [(T + 2, 128)], [(128 * (T + 2), 8), (1, 1)], T + 1),
                    zcol[:], wait=[("ch", zdone)])

        ldq = {}
        def load_xtile(i, gate):
            blk, tloc = i // 4, (i % 4) * TT
            return P.load("sync", xnt[i % 2][:],
                          rep_ap(xn1_d, [(2048, 128)], [(128 * 2048, 8), (1, TT)],
                                 blk * 1024 * 2048 + tloc),
                          wait=[("ch", gate)])
        ldq[0] = load_xtile(0, 0)
        ldq[1] = load_xtile(1, 0)
        tend = {}
        for i in range(NTT):
            t0 = i * TT
            par = i % 2
            for oc in range(8):
                def t_in(e, oc=oc, par=par):
                    last = None
                    for kc in range(8):
                        last = e.matmul(psI[:], win_s[:, kc * 2048 + oc * 128: kc * 2048 + (oc + 1) * 128],
                                        xnt[par][:, TT * kc:TT * (kc + 1)], start=(kc == 0), stop=(kc == 7))
                    return last
                P.step("tensor", t_in, wait=[("lg", lgw), ("ld", ldq[i]), ("ch", P.ch())])
                P.step("vector", lambda e, oc=oc: e.tensor_copy(innt[:, TT * oc:TT * (oc + 1)], psI[:]),
                       wait=[("ch", P.ch())])
            for oc in range(8):
                def t_g(e, oc=oc, par=par):
                    last = None
                    for kc in range(8):
                        last = e.matmul(psG[:], win_s[:, kc * 2048 + 1024 + oc * 128: kc * 2048 + 1024 + (oc + 1) * 128],
                                        xnt[par][:, TT * kc:TT * (kc + 1)], start=(kc == 0), stop=(kc == 7))
                    return last
                P.step("tensor", t_g, wait=[("ch", P.ch())])
                P.step("scalar", lambda e, oc=oc: e.activation(sgt[:, TT * oc:TT * (oc + 1)], psG[:], AF.Silu),
                       wait=[("ch", P.ch())])
            done = P.ch()
            P.store("sync", rep_ap(pre_d, [(T + 2, 128)], [(128 * (T + 2), 8), (1, TT)], 1 + t0),
                    innt[:], wait=[("ch", done)])
            P.store("sync", rep_ap(sg_d, [(T, 128)], [(128 * T, 8), (1, TT)], t0), sgt[:],
                    wait=[("ch", done)])
            tend[i] = P.ch()
            if i + 2 < NTT:
                ldq[i + 2] = load_xtile(i + 2, tend[i])
        allst = P.cnt["st"]
        lpq = {}
        def load_ptile(i, gate):
            return P.load("sync", prw[i % 2][:],
                          rep_ap(pre_d, [(T + 2, 128)], [(128 * (T + 2), 8), (1, TT + 2)], i * TT),
                          wait=[("st", allst), ("ch", gate)])
        lpq[0] = load_ptile(0, 0)
        lpq[1] = load_ptile(1, 0)
        for i in range(NTT):
            t0 = i * TT
            par = i % 2
            for oc in range(8):
                W = TT + 2
                def v_c(e, oc=oc, par=par, W=W):
                    e.tensor_scalar(convA[:], prw[par][:, oc * W: oc * W + TT],
                                    cw3_s[:, oc * 3: oc * 3 + 1], None, op0=ALU.mult)
                    e.scalar_tensor_tensor(convB[:], prw[par][:, oc * W + 1: oc * W + 1 + TT],
                                           cw3_s[:, oc * 3 + 1: oc * 3 + 2], convA[:],
                                           op0=ALU.mult, op1=ALU.add)
                    return e.scalar_tensor_tensor(convA[:], prw[par][:, oc * W + 2: oc * W + 2 + TT],
                                                  cw3_s[:, oc * 3 + 2: oc * 3 + 3], convB[:],
                                                  op0=ALU.mult, op1=ALU.add)
                P.step("vector", v_c, wait=[("ld", lpq[i]), ("ch", P.ch())])
                def s_h(e, oc=oc):
                    e.activation(ht[:, TT * oc:TT * (oc + 1)], convA[:], AF.Silu, bias=cb_s[:, oc:oc + 1])
                    return e.activation(htb[:, TT * oc:TT * (oc + 1)], convA[:], AF.Silu, bias=cb_s[:, oc:oc + 1])
                P.step("scalar", s_h, wait=[("ch", P.ch())])
            done = P.ch()
            P.store("sync", rep_ap(hf_d, [(T, 128)], [(128 * T, 8), (1, TT)], t0), ht[:], wait=[("ch", done)])
            P.store("sync", rep_ap(hbf_d, [(T, 128)], [(128 * T, 8), (1, TT)], t0), htb[:], wait=[("ch", done)])
            tend2 = P.ch()
            if i + 2 < NTT:
                lpq[i + 2] = load_ptile(i + 2, tend2)
        P.finish()
    return nc


def _pre1_inputs(ins, core, xn1_full_b):
    b, g = core // 2, core % 2
    Win = ins["L1_attn_W_in"]
    wi = Win[:, 1024 * g:1024 * (g + 1)]
    wg = Win[:, 2048 + 1024 * g:2048 + 1024 * (g + 1)]
    w = np.concatenate([wi, wg], axis=1)
    w_t = np.ascontiguousarray(w.reshape(8, 128, 2048).transpose(1, 0, 2).reshape(128, 8 * 2048))
    return {
        "xn1_pair": np.ascontiguousarray(xn1_full_b),
        "w_in_t": w_t,
        "cw3": np.ascontiguousarray(ins["L1_attn_conv_w"][1024 * g:1024 * (g + 1)]),
        "cb": np.ascontiguousarray(ins["L1_attn_conv_b"][1024 * g:1024 * (g + 1)].reshape(8, 128).T),
    }


def build_dt1(nc=None, semctx=None, fused=False):
    standalone = nc is None
    if standalone:
        nc = bass.Bass("TRN2", target_bir_lowering=False, debug=False)
    dr = DramHelper(nc, fused)
    hbf_d = dr("h_pair", [2048, T], BF16, "ExternalInput")
    wt_d = dr("wt_t", [128, 16 * 1024], F32, "ExternalInput", host=True)
    btc_d = dr("btc", [128, 8], F32, "ExternalInput", host=True)
    wsB_d = dr("wsB", [128, 16 * 16], F32, "ExternalInput", host=True)
    hf_d = dr("h_f", [1024, T], F32, "ExternalInput")
    dt_d = dr("dt_bf", [1024, T], BF16, "ExternalOutput")
    u_d = dr("u_bf", [1024, T], BF16, "ExternalOutput")
    B_d = dr("B_bf", [16, T], BF16, "ExternalOutput")

    with ExitStack() as ctx:
        sb = lambda *a: ctx.enter_context(nc.sbuf_tensor(*a, allow_name_mangling=True))
        pst = lambda *a: ctx.enter_context(nc.psum_tensor(*a, allow_name_mangling=True))
        wt_s = sb("wt_s", [128, 16 * 1024], BF16)
        wsB_s = sb("wsB_s", [128, 16 * 16], BF16)
        btc = sb("btc_s", [128, 8], F32)
        hbt = [sb(f"hbt{i}", [128, 16 * TT], BF16) for i in range(2)]
        hft = [sb(f"hft{i}", [128, 8 * TT], F32) for i in range(2)]
        spl = sb("spl", [128, 4 * TT], F32)
        dtt = sb("dtt", [128, 8 * TT], F32)
        dttb = sb("dttb", [128, 8 * TT], BF16)
        utb = sb("utb", [128, 8 * TT], BF16)
        Btb = sb("Btb", [16, TT], BF16)
        psD = pst("psD", [128, TT], F32)
        psB16 = pst("psB16", [16, TT], F32)

        cpool, cemit = setup_consts(nc, (semctx or ctx), [1.0, -1.0], uid='dx')
        P = Phase(nc, ctx, semctx)
        P.step("gpsimd", cemit)
        lds = P.load("sync", btc[:], btc_d)
        P.load("gpsimd", wt_s[:], wt_d)
        lgw = P.load("gpsimd", wsB_s[:], wsB_d)

        ldq = {}
        def load_tile(i, gate):
            l1 = P.load("sync", hbt[i % 2][:],
                        rep_ap(hbf_d, [(T, 128)], [(128 * T, 16), (1, TT)], i * TT),
                        wait=[("ch", gate)])
            return P.load("sync", hft[i % 2][:],
                          rep_ap(hf_d, [(T, 128)], [(128 * T, 8), (1, TT)], i * TT),
                          wait=[("ch", gate)])
        ldq[0] = load_tile(0, 0)
        ldq[1] = load_tile(1, 0)
        tend = {}
        for i in range(NTT):
            t0 = i * TT
            par = i % 2
            for oc in range(8):
                def t_dt(e, oc=oc, par=par):
                    last = None
                    for kc in range(16):
                        last = e.matmul(psD[:], wt_s[:, kc * 1024 + oc * 128: kc * 1024 + (oc + 1) * 128],
                                        hbt[par][:, TT * kc:TT * (kc + 1)], start=(kc == 0), stop=(kc == 15))
                    return last
                P.step("tensor", t_dt, wait=[("ld", ldq[i]), ("lg", lgw), ("ch", P.ch())])
                def s_sp(e, oc=oc):
                    e.activation(spl[:, 0:TT], psD[:], AF.Abs, bias=btc[:, oc:oc + 1])
                    e.activation(spl[:, TT:2 * TT], spl[:, 0:TT], AF.Exp, scale=-1.0)
                    e.activation(spl[:, 2 * TT:3 * TT], spl[:, TT:2 * TT], AF.Ln, bias=1.0)
                    return e.activation(spl[:, 3 * TT:4 * TT], psD[:], AF.Relu, bias=btc[:, oc:oc + 1])
                P.step("scalar", s_sp, wait=[("ch", P.ch())])
                def v_dt(e, oc=oc, par=par):
                    e.tensor_tensor(dtt[:, TT * oc:TT * (oc + 1)], spl[:, 2 * TT:3 * TT],
                                    spl[:, 3 * TT:4 * TT], op=ALU.add)
                    e.tensor_copy(dttb[:, TT * oc:TT * (oc + 1)], dtt[:, TT * oc:TT * (oc + 1)])
                    return e.tensor_tensor(utb[:, TT * oc:TT * (oc + 1)], dtt[:, TT * oc:TT * (oc + 1)],
                                           hft[par][:, TT * oc:TT * (oc + 1)], op=ALU.mult)
                P.step("vector", v_dt, wait=[("ch", P.ch())])
            def t_B(e, par=par):
                last = None
                for kc in range(16):
                    last = e.matmul(psB16[:], wsB_s[:, kc * 16:(kc + 1) * 16],
                                    hbt[par][:, TT * kc:TT * (kc + 1)], start=(kc == 0), stop=(kc == 15))
                return last
            P.step("tensor", t_B, wait=[("ch", P.ch())])
            P.step("vector", lambda e: e.tensor_copy(Btb[:], psB16[:]), wait=[("ch", P.ch())])
            done = P.ch()
            P.store("sync", rep_ap(dt_d, [(T, 128)], [(128 * T, 8), (1, TT)], t0), dttb[:], wait=[("ch", done)])
            P.store("sync", rep_ap(u_d, [(T, 128)], [(128 * T, 8), (1, TT)], t0), utb[:], wait=[("ch", done)])
            P.store("sync", rep_ap(B_d, [(T, 16)], [(1, TT)], t0), Btb[:], wait=[("ch", done)])
            tend[i] = P.ch()
            if i + 2 < NTT:
                ldq[i + 2] = load_tile(i + 2, tend[i])
        P.finish()
    return nc


def _dt1_inputs(ins, core, hbf_full_b, hf_my):
    b, g = core // 2, core % 2
    Wt = ins["L1_attn_W_time"]
    w = Wt[:, 1024 * g:1024 * (g + 1)]
    wt_t = np.ascontiguousarray(w.reshape(16, 128, 1024).transpose(1, 0, 2).reshape(128, 16 * 1024))
    Ws = ins["L1_attn_W_state"][:, 16:32]
    wsB = np.ascontiguousarray(Ws.reshape(16, 128, 16).transpose(1, 0, 2).reshape(128, 16 * 16))
    return {
        "h_pair": np.ascontiguousarray(hbf_full_b),
        "wt_t": wt_t,
        "btc": np.ascontiguousarray(ins["L1_attn_b_time"][1024 * g:1024 * (g + 1)].reshape(8, 128).T),
        "wsB": wsB,
        "h_f": np.ascontiguousarray(hf_my),
    }


def build_scan1(nc=None, semctx=None, fused=False):
    TS = 2048
    NG = 128
    standalone = nc is None
    if standalone:
        nc = bass.Bass("TRN2", target_bir_lowering=False, debug=False)
    dr = DramHelper(nc, fused)
    dt_d = dr("dt_bf", [1024, T], BF16, "ExternalInput")
    u_d = dr("u_bf", [1024, T], BF16, "ExternalInput")
    B_d = dr("B_bf", [16, T], BF16, "ExternalInput")
    hbf_d = dr("h_bf", [1024, T], BF16, "ExternalInput")
    sg_d = dr("sg", [1024, T], BF16, "ExternalInput")
    alog_d = dr("alog_my", [128, 128], F32, "ExternalInput", host=True)
    skd_d = dr("skipdiag", [8, 128 * 8], F32, "ExternalInput", host=True)
    s16_d = dr("s16", [128, 8], F32, "ExternalInput", host=True)
    z_d = dr("z_bf", [1024, T], BF16, "ExternalOutput")

    with ExitStack() as ctx:
        sb = lambda *a: ctx.enter_context(nc.sbuf_tensor(*a, allow_name_mangling=True))
        pst = lambda *a: ctx.enter_context(nc.psum_tensor(*a, allow_name_mangling=True))
        negA = sb("negA_s", [128, 128], F32)
        s16 = sb("s16_s", [128, 8], BF16)
        skd = sb("skd_s", [8, 128 * 8], BF16)
        NBUF = 3
        dtr = [sb(f"dtr{i}", [128, TS], BF16) for i in range(NBUF)]
        ur = [sb(f"ur{i}", [128, TS], BF16) for i in range(NBUF)]
        Brt = [sb(f"Brt{i}", [128, TS], BF16) for i in range(2)]
        hst = [sb(f"hst{i}", [8, TS], BF16) for i in range(NBUF)]
        sgt = [sb(f"sgt{i}", [8, TS], BF16) for i in range(NBUF)]
        dA = [sb(f"dA{i}", [128, TS], F32) for i in range(NBUF)]
        dB = [sb(f"dB{i}", [128, TS], BF16) for i in range(NBUF)]
        mem = [sb(f"mem{i}", [128, TS], F32) for i in range(NBUF)]
        membf = [sb(f"membf{i}", [128, TS], BF16) for i in range(NBUF)]
        zst = [sb(f"zst{i}", [8, TS], BF16) for i in range(NBUF)]
        psY = [pst(f"psY{i}", [8, TS], F32) for i in range(2)]

        P = Phase(nc, ctx, semctx)
        lda = P.load("sync", negA[:], alog_d)
        P.load("gpsimd", skd[:], skd_d)
        P.load("gpsimd", s16[:], s16_d)
        P.load("gpsimd", Brt[0][:], rep_ap(B_d, [(0, 8), (T, 16)], [(1, TS)], 0))
        lgs = P.load("gpsimd", Brt[1][:], rep_ap(B_d, [(0, 8), (T, 16)], [(1, TS)], TS))
        P.step("scalar", lambda e: e.activation(negA[:], negA[:], AF.Exp), wait=[("ld", lda), ("ch", P.ch())])
        P.step("vector", lambda e: e.tensor_scalar(negA[:], negA[:], -1.0, None, op0=ALU.mult),
               wait=[("ch", P.ch())])
        ng_tick = P.ch()

        seq = [(g, t) for g in range(NG) for t in range(2)]
        NIT = len(seq)
        for i in range(NBUF):
            P.add_sem(f"lgS{i}")
            P.add_sem(f"ldS{i}")
            P.add_sem(f"stS{i}")
        P.add_sem("zs")

        lgq = {}
        ldqs = {}

        def load_gt(idx, gg, gh, gs):
            gi, ti = seq[idx]
            t0 = ti * TS
            ch0 = gi * 8
            par = idx % NBUF
            P.load("gpsimd", dtr[par][:], rep_ap(dt_d, [(T, 8), (0, 16)], [(1, TS)], ch0 * T + t0),
                   sem=f"lgS{par}", wait=gg)
            lgq[idx] = P.load("gpsimd", ur[par][:], rep_ap(u_d, [(T, 8), (0, 16)], [(1, TS)], ch0 * T + t0),
                              sem=f"lgS{par}")
            P.load("sync", hst[par][:], rep_ap(hbf_d, [(T, 8)], [(1, TS)], ch0 * T + t0),
                   sem=f"ldS{par}", wait=gh)
            ldqs[idx] = P.load("sync", sgt[par][:], rep_ap(sg_d, [(T, 8)], [(1, TS)], ch0 * T + t0),
                               sem=f"ldS{par}", wait=gs)

        for i in range(NBUF):
            load_gt(i, [], [], [])
        stq = {}

        def emit_z(zi):
            par = zi % NBUF
            gi, ti = seq[zi]
            P.step("vector", lambda e, par=par: e.tensor_tensor(
                zst[par][:], psY[zi % 2][:], sgt[par][:], op=ALU.mult),
                inc="zs", wait=[("p5_4", zi + 1), (f"ldS{par}", ldqs[zi]),
                                (f"stS{par}", stq.get(zi - NBUF, 0))], st_guard=False)
            stq[zi] = P.store("sync", rep_ap(z_d, [(T, 8)], [(1, TS)], gi * 8 * T + ti * TS),
                              zst[par][:], wait=[("zs", zi + 1)], sem=f"stS{par}")

        for idx, (gi, ti) in enumerate(seq):
            par = idx % NBUF
            if idx >= 2:
                emit_z(idx - 2)

            def s0(e, j, gi=gi, par=par):
                return e.activation(dA[par][:], dtr[par][:], AF.Exp, scale=negA[:, gi:gi + 1])

            XC = 160
            def s1(e, j, par=par, ti=ti):
                return e.tensor_tensor(dB[par][:, XC:TS], ur[par][:, XC:TS], Brt[ti][:, XC:TS],
                                       op=ALU.mult)

            if ti == 0:
                def s2(e, j, par=par, ti=ti):
                    e.tensor_tensor(dB[par][:, 0:XC], ur[par][:, 0:XC], Brt[ti][:, 0:XC],
                                    op=ALU.mult)
                    return e.tensor_tensor_scan(mem[par][:], dA[par][:], dB[par][:], 0.0,
                                                op0=ALU.mult, op1=ALU.add)
            else:
                def s2(e, j, par=par, ti=ti):
                    e.tensor_tensor(dB[par][:, 0:XC], ur[par][:, 0:XC], Brt[ti][:, 0:XC],
                                    op=ALU.mult)
                    return e.tensor_tensor_scan(mem[par][:], dA[par][:], dB[par][:],
                                                mem[(par + NBUF - 1) % NBUF][:, TS - 1:TS],
                                                op0=ALU.mult, op1=ALU.add)

            def s3(e, j, par=par):
                return e.tensor_scalar(membf[par][:], mem[par][:], 1.0, None, op0=ALU.mult)

            def s4(e, j, gi=gi, par=par):
                last = None
                ps = psY[j % 2]
                for nt in range(4):
                    e.matmul(ps[:, 512 * nt:512 * (nt + 1)], s16[:],
                             membf[par][:, 512 * nt:512 * (nt + 1)], start=True, stop=False)
                    last = e.matmul(ps[:, 512 * nt:512 * (nt + 1)],
                                    skd[:, gi * 8:(gi + 1) * 8],
                                    hst[par][:, 512 * nt:512 * (nt + 1)], start=False, stop=True,
                                    skip_group_check=True)
                return last

            P.pipe([("scalar", s0), ("gpsimd", s1), ("vector", s2), ("vector", s3),
                    ("tensor", s4)], nbanks=NBUF, st_guard=False, stage_waits={
                0: [(f"lgS{par}", lgq[idx]), ("ch", ng_tick)],
                1: [(f"lgS{par}", lgq[idx]), ("lg", lgs)],
                4: [("lg", lgs), (f"ldS{par}", ldqs[idx]), ("zs", idx - 1)],
            })
            if idx + NBUF < NIT:
                load_gt(idx + NBUF, [("p5_1", idx + 1)], [("p5_4", idx + 1)],
                        [("zs", idx + 1)])
        emit_z(NIT - 2)
        emit_z(NIT - 1)
        fin = [P.cnt[f"stS{i}"] for i in range(NBUF)]

        def drain(eng):
            for i in range(NBUF):
                if fin[i] > 0:
                    eng.wait_ge(P.sem[f"stS{i}"], fin[i])
        P._engine("sync")(drain)
        P.finish()
    if standalone:
        return nc


def _scan1_inputs(ins, core):
    b, g = core // 2, core % 2
    alog = ins["L1_attn_A_log"][1024 * g:1024 * (g + 1)]
    skip = ins["L1_attn_skip"][1024 * g:1024 * (g + 1)]
    alog_t = np.zeros((128, 128), np.float32)
    for p in range(128):
        alog_t[p, :] = alog[np.arange(128) * 8 + p // 16, p % 16]
    s16 = np.zeros((128, 8), np.float32)
    for p in range(128):
        s16[p, p // 16] = 1.0
    skd = np.zeros((8, 128 * 8), np.float32)
    for g in range(128):
        for j in range(8):
            skd[j, g * 8 + j] = skip[8 * g + j]
    return {
        "alog_my": alog_t,
        "skipdiag": skd,
        "s16": s16,
    }


def build_l1tail(nc=None, semctx=None, fused=False, debug=False, stop_after=None):
    NTOK = 2048
    NT4 = NTOK // TT
    E = 8
    standalone = nc is None
    if standalone:
        nc = bass.Bass("TRN2", target_bir_lowering=False, debug=False)
    dr = DramHelper(nc, fused)
    zpair_d = dr("z_pair", [2048, T], BF16, "ExternalInput")
    z2g = zpair_d.rearrange("c (h t) -> (c h) t", h=2)
    zidx_d = dr("zidx", [128, 16], mybir.dt.int32, "ExternalInput", host=True)
    x1_d = dr("x1_my", [D, NTOK], F32, "ExternalInput")
    xn1_d = dr("xn1_my", [D, NTOK], BF16, "ExternalInput")
    wout_d = dr("wout_t", [128, 16 * 1024], F32, "ExternalInput", host=True)
    gw_d = dr("gw_t", [128, 8 * 8], F32, "ExternalInput", host=True)
    gb_d = dr("gb", [8, 1], F32, "ExternalInput", host=True)
    w1_d = dr("w1_h", [E, 32 * 128 * 1024], F32, "ExternalInput", host=True)
    b1_d = dr("b1_t", [128, 8 * 32], F32, "ExternalInput", host=True)
    w2_d = dr("w2_h", [E, 8 * 128 * 4096], F32, "ExternalInput", host=True)
    b2_d = dr("b2_t", [8, 1024], F32, "ExternalInput", host=True)
    ws1_d = dr("ws1_h", [32, 128 * 1024], F32, "ExternalInput", host=True)
    bs1_d = dr("bs1_t", [128, 32], F32, "ExternalInput", host=True)
    ws2_d = dr("ws2_h", [8, 128 * 4096], F32, "ExternalInput", host=True)
    bs2_d = dr("bs2_t", [128, 8], F32, "ExternalInput", host=True)
    idf_d = dr("idf3", [128, 128], F32, "ExternalInput", host=True)
    onesrb_d = dr("onesrb3", [1, 128], F32, "ExternalInput", host=True)
    out_d = dr("out_my", [D, NTOK], F32, "ExternalOutput", host="out")
    comb_d = dr("comb_scr", [8, NTOK], BF16,
                "ExternalOutput" if debug else "Internal")
    if debug:
        hdbg_d = dr("hdbg", [128, 32 * TT], BF16, "ExternalOutput")
        w1dbg_d = dr("w1dbg", [128, 64 * 128], BF16, "ExternalOutput")
        w2dbg_d = dr("w2dbg", [128, 64 * 128], BF16, "ExternalOutput")
        cbdbg_d = dr("cbdbg", [128, 4 * TT], F32, "ExternalOutput")

    with ExitStack() as ctx:
        sb = lambda *a: ctx.enter_context(nc.sbuf_tensor(*a, allow_name_mangling=True))
        pst = lambda *a: ctx.enter_context(nc.psum_tensor(*a, allow_name_mangling=True))
        gw_s = sb("gw_s", [128, 8 * 8], BF16)
        gb_s = sb("gb_s", [8, 1], F32)
        b1_s = sb("b1_s", [128, 8 * 32], F32)
        b2_s = sb("b2_s", [8, 1024], BF16)
        bs1_s = sb("bs1_s", [128, 32], F32)
        bs2_s = sb("bs2_s", [128, 8], F32)
        idf = sb("idf_s", [128, 128], F32)
        onesrb = sb("onesrb_s", [1, 128], BF16)
        zidx_s = sb("zidx_s", [128, 16], mybir.dt.int32)
        xnt = [sb(f"xnt{i}", [128, 8 * TT], BF16) for i in range(NT4)]
        x2acc = [sb(f"x2acc{i}", [128, 8 * TT], F32) for i in range(NT4)]
        cbcT = [sb(f"cbcT{i}", [128, TT], F32) for i in range(NT4)]
        combf = sb("combf", [8, TT], BF16)
        crow = sb("crow", [1, TT], BF16)
        scr8 = sb("scr8", [8, 1], F32)
        psAx = [pst("psA", [128, TT], F32), pst("psA2", [128, TT], F32)]
        psHx = [pst("psH", [128, TT], F32), pst("psH2", [128, TT], F32)]
        psG = pst("psG", [8, TT], F32)
        psC = pst("psC", [128, TT], F32)

        cpool, cemit = setup_consts(nc, (semctx or ctx), [1.0, -1.0], uid='l1')
        P = Phase(nc, ctx, semctx)
        P.add_sem("lgw1")
        P.add_sem("lgw2")
        P.step("gpsimd", cemit)
        P.load("sync", gb_s[:], gb_d)
        P.load("sync", b1_s[:], b1_d)
        P.load("sync", bs1_s[:], bs1_d)
        P.load("sync", bs2_s[:], bs2_d)
        lds = P.load("sync", idf[:], idf_d)
        P.load("gpsimd", gw_s[:], gw_d)
        P.load("gpsimd", b2_s[:], b2_d)
        P.load("gpsimd", zidx_s[:], zidx_d)
        lgc = P.load("gpsimd", onesrb[:], onesrb_d)

        ldxn = 0
        with ExitStack() as ctxA:
            sbA = lambda *a: ctxA.enter_context(nc.sbuf_tensor(*a, allow_name_mangling=True))
            wout_s = sbA("wout_s", [128, 16 * 1024], BF16)
            zt = sbA("zt0", [128, 16 * TT], BF16)
            x1t = sbA("x1t0", [128, 8 * TT], F32)
            lgs = sbA("lgs", [8, TT], F32)
            ltt = sbA("ltt", [128, 4 * 8], F32)
            m1 = sbA("m1", [128, 4], F32)
            m2 = sbA("m2", [128, 4], F32)
            mask1 = sbA("mask1", [128, 4 * 8], F32)
            mask2 = sbA("mask2", [128, 4 * 8], F32)
            maskd = sbA("maskd", [128, 4 * 8], F32)
            w1t_ = sbA("w1t_", [128, 4], F32)
            w2t_ = sbA("w2t_", [128, 4], F32)
            combt = sbA("combt", [128, 4 * 8], F32)
            lgw = P.load("gpsimd", wout_s[:], wout_d)

            for i in range(NT4):
                t0 = i * TT
                zgate = [("lg", lgc), ("ch", 0)] if i == 0 else \
                    [("lg", lgc), P.pipe_mid(2, 0), ("ch", P.ch())]
                for kc in range(16):
                    P.load_idx("gpsimd", zt[:, kc * TT:(kc + 1) * TT], z2g,
                               zidx_s[:, kc:kc + 1], element_offset=t0, wait=zgate)
                lgz = P.cnt["lg"]
                pgate = [] if i == 0 else [P.pipe_mid(2, 1)]
                P.load("sync", x1t[:], rep_ap(x1_d, [(NTOK, 128)], [(128 * NTOK, 8), (1, TT)], t0),
                       wait=pgate)
                ldxn = P.load("sync", xnt[i][:],
                              rep_ap(xn1_d, [(NTOK, 128)], [(128 * NTOK, 8), (1, TT)], t0))
                ldv_i = ldxn
                for oc in range(8):
                    def t_a(e, j, oc=oc):
                        last = None
                        for kc in range(16):
                            last = e.matmul(psAx[j % 2][:],
                                            wout_s[:, kc * 1024 + oc * 128: kc * 1024 + (oc + 1) * 128],
                                            zt[:, TT * kc:TT * (kc + 1)], start=(kc == 0), stop=(kc == 15))
                        return last
                    def v_a(e, j, oc=oc, i=i):
                        return e.tensor_tensor(
                            x2acc[i][:, TT * oc:TT * (oc + 1)], x1t[:, TT * oc:TT * (oc + 1)],
                            psAx[j % 2][:], op=ALU.add)
                    P.pipe([("tensor", t_a), ("vector", v_a)],
                           stage_waits={0: [("ld", ldv_i), ("lg", max(lgz, lgw))], 1: [("ld", ldv_i)]})
                def t_g(e, i=i):
                    last = None
                    for kc in range(8):
                        last = e.matmul(psG[:], gw_s[:, kc * 8:(kc + 1) * 8],
                                        xnt[i][:, TT * kc:TT * (kc + 1)], start=(kc == 0), stop=(kc == 7))
                    return last
                P.step("tensor", t_g, wait=[("ch", P.ch())])
                P.step("vector", lambda e: e.tensor_scalar(lgs[:], psG[:], gb_s[:, 0:1], None, op0=ALU.add),
                       wait=[("ch", P.ch())])
                def t_tr(e):
                    last = None
                    for tc in range(4):
                        last = e.transpose(psC[:, 8 * tc:8 * (tc + 1)], lgs[:, 128 * tc:128 * (tc + 1)],
                                           idf[0:8, 0:8])
                    return last
                P.step("tensor", t_tr, wait=[("ld", lds), ("ch", P.ch())])
                P.step("vector", lambda e: e.tensor_copy(ltt[:], psC[:, 0:32]), wait=[("ch", P.ch())])
                P.step("vector", lambda e: e.reduce_max(m1[:], ltt.rearrange("p (tc e) -> p tc e", tc=4),
                                                        axis=AX.X),
                       wait=[("ch", P.ch())])
                def v_g1a(e):
                    last = None
                    for tc in range(4):
                        last = e.tensor_scalar(mask1[:, 8 * tc:8 * (tc + 1)], ltt[:, 8 * tc:8 * (tc + 1)],
                                               m1[:, tc:tc + 1], None, op0=ALU.is_equal)
                    return last
                P.step("vector", v_g1a, wait=[("ch", P.ch())])
                def v_g1b(e):
                    last = None
                    for tc in range(4):
                        last = e.scalar_tensor_tensor(maskd[:, 8 * tc:8 * (tc + 1)], mask1[:, 8 * tc:8 * (tc + 1)],
                                                      -1e30, ltt[:, 8 * tc:8 * (tc + 1)], op0=ALU.mult, op1=ALU.add)
                    return last
                P.step("vector", v_g1b, wait=[("ch", P.ch())])
                P.step("vector", lambda e: e.reduce_max(m2[:], maskd.rearrange("p (tc e) -> p tc e", tc=4),
                                                        axis=AX.X),
                       wait=[("ch", P.ch())])
                def v_g2(e):
                    last = None
                    for tc in range(4):
                        last = e.tensor_scalar(mask2[:, 8 * tc:8 * (tc + 1)], maskd[:, 8 * tc:8 * (tc + 1)],
                                               m2[:, tc:tc + 1], None, op0=ALU.is_equal)
                    e.tensor_tensor(w1t_[:], m1[:], m2[:], op=ALU.subtract)
                    return last
                P.step("vector", v_g2, wait=[("ch", P.ch())])
                P.step("scalar", lambda e: e.activation(w1t_[:], w1t_[:], AF.Sigmoid), wait=[("ch", P.ch())])
                def v_g3a(e):
                    e.tensor_scalar(w2t_[:], w1t_[:], -1.0, 1.0, op0=ALU.mult, op1=ALU.add)
                    last = None
                    for tc in range(4):
                        last = e.tensor_scalar(combt[:, 8 * tc:8 * (tc + 1)], mask1[:, 8 * tc:8 * (tc + 1)],
                                               w1t_[:, tc:tc + 1], None, op0=ALU.mult)
                    return last
                P.step("vector", v_g3a, wait=[("ch", P.ch())])
                def v_g3b(e):
                    last = None
                    for tc in range(4):
                        last = e.scalar_tensor_tensor(combt[:, 8 * tc:8 * (tc + 1)], mask2[:, 8 * tc:8 * (tc + 1)],
                                                      w2t_[:, tc:tc + 1], combt[:, 8 * tc:8 * (tc + 1)],
                                                      op0=ALU.mult, op1=ALU.add)
                    return last
                P.step("vector", v_g3b, wait=[("ch", P.ch())])
                P.step("tensor", lambda e: [e.transpose(psG[0:8, 128 * tc:128 * (tc + 1)],
                                                        combt[:, 8 * tc:8 * (tc + 1)], idf[:, :])
                                            for tc in range(4)][-1],
                       wait=[("ch", P.ch())])
                P.step("vector", lambda e: e.tensor_copy(combf[:], psG[0:8, :]),
                       wait=[("ch", P.ch()), ("st", P.cnt["st"])])
                P.store("sync", rep_ap(comb_d, [(NTOK, 8)], [(1, TT)], t0), combf[:],
                        wait=[("ch", P.ch())])
            cf_all = P.cnt["st"]
        if stop_after == "pro":
            for t in range(NT4):
                P.store("sync", rep_ap(out_d, [(NTOK, 128)], [(128 * NTOK, 8), (1, TT)], t * TT),
                        x2acc[t][:], wait=[P.pipe_done(2), ("ch", P.ch())])
            P.finish()
            return nc
        hbuf = sb("hbuf", [128, 32 * TT], BF16)
        w1S = [sb(f"w1S{i}", [128, 64 * 128], BF16) for i in range(2)]
        w2S = sb("w2S0", [128, 64 * 128], BF16)

        def w1slab(sx):
            ex, mcg = sx // 4, sx % 4
            if ex < E:
                return rep_ap(w1_d, [(1024, 128)], [(128 * 1024, 8), (128, 8), (1, 128)],
                              ex * 32 * 128 * 1024 + mcg * 8 * 128 * 1024)
            return rep_ap(ws1_d, [(1024, 128)], [(128 * 1024, 8), (128, 8), (1, 128)],
                          mcg * 8 * 128 * 1024)

        def w2slab(sx):
            ex, mcg = sx // 4, sx % 4
            if ex < E:
                return rep_ap(w2_d, [(4096, 128)], [(128 * 4096, 8), (128, 8), (1, 128)],
                              ex * 8 * 128 * 4096 + mcg * 8 * 128)
            return rep_ap(ws2_d, [(4096, 128)], [(128 * 4096, 8), (128, 8), (1, 128)],
                          mcg * 8 * 128)

        NS = (E + 1) * 4
        if isinstance(stop_after, int):
            NS = stop_after
        pro_end = P.ch()
        lw1 = {0: P.load("gpsimd", w1S[0][:], w1slab(0), sem="lgw1",
                         wait=[("ch", pro_end)])}
        lw2 = {}
        cbc_tick = 0
        for sx in range(NS):
            ex, mcg = sx // 4, sx % 4
            shared = ex == E
            if mcg == 0 and not shared:
                for t in range(NT4):
                    lcr = P.load("sync", crow[:],
                                 rep_ap(comb_d, [(NTOK, 1)], [(1, TT)], ex * NTOK + t * TT),
                                 wait=[("st", cf_all), ("ch", P.ch())])
                    P.step("tensor", lambda e: e.matmul(psC[:], onesrb[0:1, :], crow[0:1, :],
                                                        start=True, stop=True),
                           wait=[("ld", lcr), ("lg", lgc), ("ch", P.ch())])
                    P.step("vector", lambda e, t=t: e.tensor_copy(cbcT[t][:], psC[:]),
                           wait=[("ch", P.ch())])
                cbc_tick = P.ch()
            w2gate = [P.pipe_done(2)] + ([("ch", pro_end)] if sx == 0 else [])
            lw2[sx] = P.load("gpsimd", w2S[:], w2slab(sx), sem="lgw2", wait=w2gate)
            pd21 = ("p2_1", P.cnt.get("p2_1", 0))
            for h2 in range(8):
                for t in range(NT4):
                    def t_h(e, j, h2=h2, t=t, sx=sx):
                        last = None
                        for kc in range(8):
                            last = e.matmul(psHx[j % 2][:],
                                            w1S[sx % 2][:, (h2 * 8 + kc) * 128:(h2 * 8 + kc + 1) * 128],
                                            xnt[t][:, TT * kc:TT * (kc + 1)],
                                            start=(kc == 0), stop=(kc == 7))
                        return last
                    def s_h(e, j, h2=h2, t=t, ex=ex, mcg=mcg, shared=shared):
                        bcol = (bs1_s[:, mcg * 8 + h2:mcg * 8 + h2 + 1] if shared
                                else b1_s[:, ex * 32 + mcg * 8 + h2:ex * 32 + mcg * 8 + h2 + 1])
                        return e.activation(hbuf[:, (h2 * 4 + t) * TT:(h2 * 4 + t + 1) * TT],
                                            psHx[j % 2][:], AF.Gelu, bias=bcol)
                    if shared:
                        def v_h(e, j):
                            return e.tensor_scalar(scr8[:], scr8[:], 1.0, None, op0=ALU.mult)
                    else:
                        def v_h(e, j, h2=h2, t=t):
                            return e.tensor_tensor(hbuf[:, (h2 * 4 + t) * TT:(h2 * 4 + t + 1) * TT],
                                                   hbuf[:, (h2 * 4 + t) * TT:(h2 * 4 + t + 1) * TT],
                                                   cbcT[t][:], op=ALU.mult)
                    P.pipe([("tensor", t_h), ("scalar", s_h), ("vector", v_h)],
                           stage_waits={0: [("lgw1", lw1[sx]), ("ld", ldxn)],
                                        1: [pd21],
                                        2: [("ch", cbc_tick)]})
            hd = P.pipe_done(3)
            if sx + 1 < NS:
                lw1[sx + 1] = P.load("gpsimd", w1S[(sx + 1) % 2][:], w1slab(sx + 1),
                                     sem="lgw1", wait=[P.pipe_mid(3, 0)])
            for t in range(NT4):
                for oc in range(8):
                    def t_y(e, j, t=t, oc=oc):
                        last = None
                        for kc in range(8):
                            last = e.matmul(psAx[j % 2][:],
                                            w2S[:, (oc * 8 + kc) * 128:(oc * 8 + kc + 1) * 128],
                                            hbuf[:, (kc * 4 + t) * TT:(kc * 4 + t + 1) * TT],
                                            start=(kc == 0), stop=(kc == 7))
                        return last
                    if shared and mcg == 3:
                        def v_y(e, j, t=t, oc=oc):
                            return e.scalar_tensor_tensor(
                                x2acc[t][:, TT * oc:TT * (oc + 1)], psAx[j % 2][:], bs2_s[:, oc:oc + 1],
                                x2acc[t][:, TT * oc:TT * (oc + 1)], op0=ALU.add, op1=ALU.add)
                    else:
                        def v_y(e, j, t=t, oc=oc):
                            return e.tensor_tensor(
                                x2acc[t][:, TT * oc:TT * (oc + 1)], x2acc[t][:, TT * oc:TT * (oc + 1)],
                                psAx[j % 2][:], op=ALU.add)
                    P.pipe([("tensor", t_y), ("vector", v_y)],
                           stage_waits={0: [("lgw2", lw2[sx]), hd]})
        if isinstance(stop_after, int):
            for t in range(NT4):
                P.store("sync", rep_ap(out_d, [(NTOK, 128)], [(128 * NTOK, 8), (1, TT)], t * TT),
                        x2acc[t][:], wait=[P.pipe_done(2), ("ch", P.ch())])
            if debug:
                for t in range(NT4):
                    P.store("sync", rep_ap(cbdbg_d, [(4 * TT, 128)], [(1, TT)], t * TT),
                            cbcT[t][:], wait=[P.pipe_done(2)])
                P.store("sync", hdbg_d, hbuf[:], wait=[P.pipe_done(2)])
                P.store("sync", w1dbg_d, w1S[(NS - 1) % 2][:], wait=[P.pipe_done(2)])
                P.store("sync", w2dbg_d, w2S[:], wait=[P.pipe_done(2)])
            P.finish()
            return nc
        for t in range(NT4):
            lcb = P.load("sync", combf[:], rep_ap(comb_d, [(NTOK, 8)], [(1, TT)], t * TT),
                         wait=[("st", cf_all), P.pipe_done(2), ("ch", P.ch())])
            for oc in range(8):
                def t_b2(e, j, oc=oc):
                    return e.matmul(psHx[j % 2][:], b2_s[:, oc * 128:(oc + 1) * 128], combf[:],
                                    start=True, stop=True)
                def v_b2(e, j, t=t, oc=oc):
                    return e.tensor_tensor(
                        x2acc[t][:, TT * oc:TT * (oc + 1)], x2acc[t][:, TT * oc:TT * (oc + 1)],
                        psHx[j % 2][:], op=ALU.add)
                P.pipe([("tensor", t_b2), ("vector", v_b2)],
                       stage_waits={0: [("ld", lcb), P.pipe_done(3), P.pipe_done(2)]})
            P.store("sync", rep_ap(out_d, [(NTOK, 128)], [(128 * NTOK, 8), (1, TT)], t * TT),
                    x2acc[t][:], wait=[P.pipe_done(2), ("ch", P.ch())])
        P.finish()
    return nc


def _l1tail_inputs(ins, core, z_pair_b, x1_my, xn1_my):
    b, g = core // 2, core % 2
    p = np.arange(128)
    zidx = np.zeros((128, 16), np.int32)
    for kc in range(16):
        zidx[:, kc] = 256 * kc + 2 * p + g
    Wout = ins["L1_attn_W_out"]
    wout_t = np.ascontiguousarray(Wout.reshape(16, 128, 1024).transpose(1, 0, 2).reshape(128, 16 * 1024))
    gw = ins["L1_ffn_gate_w"]
    gw_t = np.ascontiguousarray(gw.reshape(8, 128, 8).transpose(1, 0, 2).reshape(128, 8 * 8))
    w1 = ins["L1_ffn_w1"]
    w1_h = np.ascontiguousarray(
        w1.reshape(8, 8, 128, 32, 128).transpose(0, 3, 2, 1, 4).reshape(8, 32 * 128 * 1024))
    b1_t = np.ascontiguousarray(ins["L1_ffn_b1"].reshape(8, 32, 128).transpose(2, 0, 1).reshape(128, 8 * 32))
    w2 = ins["L1_ffn_w2"]
    w2_h = np.ascontiguousarray(
        w2.reshape(8, 32, 128, 8, 128).transpose(0, 3, 2, 1, 4).reshape(8, 8 * 128 * 4096))
    ws1_h = np.ascontiguousarray(
        ins["L1_ffn_ws1"].reshape(8, 128, 32, 128).transpose(2, 1, 0, 3).reshape(32, 128 * 1024))
    ws2_h = np.ascontiguousarray(
        ins["L1_ffn_ws2"].reshape(32, 128, 8, 128).transpose(2, 1, 0, 3).reshape(8, 128 * 4096))
    return {
        "z_pair": z_pair_b, "zidx": zidx,
        "x1_my": np.ascontiguousarray(x1_my), "xn1_my": np.ascontiguousarray(xn1_my),
        "wout_t": wout_t, "gw_t": gw_t,
        "gb": np.ascontiguousarray(ins["L1_ffn_gate_b"].reshape(8, 1)),
        "w1_h": w1_h, "b1_t": b1_t, "w2_h": w2_h,
        "b2_t": np.ascontiguousarray(ins["L1_ffn_b2"]),
        "ws1_h": ws1_h,
        "bs1_t": np.ascontiguousarray(ins["L1_ffn_bs1"].reshape(32, 128).T),
        "ws2_h": ws2_h,
        "bs2_t": np.ascontiguousarray(ins["L1_ffn_bs2"].reshape(8, 128).T),
        "idf3": np.eye(128, dtype=np.float32),
        "onesrb3": np.ones((1, 128), np.float32),
    }


def _pre0_inputs(ins, core):
    b, g = core // 2, core % 2
    x = ins["L0_attn_x"]
    x_b = np.ascontiguousarray(x[b].T)
    Wq, Wk, Wv = ins["L0_attn_Wq"], ins["L0_attn_Wk"], ins["L0_attn_Wv"]
    wqkv = np.concatenate([Wq[:, 512 * g:512 * (g + 1)],
                           Wk[:, 128 * g:128 * (g + 1)],
                           Wv[:, 128 * g:128 * (g + 1)]], axis=1)
    wqkv_t = np.ascontiguousarray(wqkv.reshape(8, 128, 768).transpose(1, 0, 2).reshape(128, 8 * 768))
    wgab = np.concatenate([ins["L0_attn_Wa"][:, 8 * g:8 * (g + 1)],
                           ins["L0_attn_Walpha"][:, 8 * g:8 * (g + 1)],
                           ins["L0_attn_Wb"][:, 2 * g:2 * (g + 1)],
                           np.zeros((1024, 6), np.float32)], axis=1)
    wgab_t = np.ascontiguousarray(wgab.reshape(8, 128, 24).transpose(1, 0, 2).reshape(128, 8 * 24))
    wconv = np.concatenate([ins["L0_attn_Wconv"][512 * g:512 * (g + 1)],
                            ins["L0_attn_Wconv"][1024 + 128 * g:1024 + 128 * (g + 1)],
                            ins["L0_attn_Wconv"][1280 + 128 * g:1280 + 128 * (g + 1)]], axis=0)
    return {
        "x_b": x_b,
        "wqkv": wqkv_t,
        "wgab": wgab_t,
        "wconv": np.ascontiguousarray(wconv),
        "alog8": np.ascontiguousarray(ins["L0_attn_A_log"][8 * g:8 * (g + 1)].reshape(8, 1)),
        "dtb8": np.ascontiguousarray(ins["L0_attn_dt_bias"][8 * g:8 * (g + 1)].reshape(8, 1)),
        "onesc": np.ones((128, 1), np.float32),
        "onesr": np.ones((1, 128), np.float32),
    }


PAIRS = [[0, 1], [2, 3], [4, 5], [6, 7]]
HYP2 = [[0, 2], [1, 3], [4, 6], [5, 7]]
HYP4 = [[0, 4], [1, 5], [2, 6], [3, 7]]


def emit_ag_block(nc, semctx, steps):
    Phase._uid[0] += 1
    sem = semctx.enter_context(nc.semaphore(f"cc{Phase._uid[0]}"))
    with ExitStack() as ctx:
        blk = ctx.enter_context(nc.Block())

        def body(g):
            n = 0
            for (src, dst, rg) in steps:
                g.collective_compute("AllGather", ALU.bypass, replica_groups=rg,
                                     ins=[src.opt()], outs=[dst.opt()]).then_inc(sem, 1)
                n += 1
                g.wait_ge(sem, n)
        blk.gpsimd(body)


def build_fused():
    nc = bass.Bass("TRN2", target_bir_lowering=False, debug=False)
    semctx = ExitStack()
    nc._fused_semctx = semctx
    dr = DramHelper(nc, True)
    _, nc._cemit_all = setup_consts(nc, semctx, [1e-5, 1.0, -1.0, -0.5, 1e-6], uid='all')
    build_pre0(nc, semctx, fused=True)
    build_scan0(nc, semctx, fused=True)
    build_ufold(nc, semctx, fused=True)
    reg = nc._fusedreg
    o_pair = dr("o_pair", [2 * T, 512], BF16, "Internal")
    we2 = dr("we2", [256, 4096], BF16, "Internal")
    we4 = dr("we4", [512, 4096], BF16, "Internal")
    wexpfA = dr("wexpfA", [1024, 4096], BF16, "Internal")
    wc2 = dr("wc2", [256, 32 * 128], BF16, "Internal")
    wc4 = dr("wc4", [512, 32 * 128], BF16, "Internal")
    wconfA = dr("wconfA", [1024, 4096], BF16, "Internal")
    bu2 = dr("bu2", [256, 4], F32, "Internal")
    bu4 = dr("bu4", [512, 4], F32, "Internal")
    bupfA = dr("bupfA", [1024, 4], F32, "Internal")
    emit_ag_block(nc, semctx, [
        (reg["o_out"], o_pair, PAIRS),
        (reg["wexpf"], we2, PAIRS), (we2, we4, HYP2), (we4, wexpfA, HYP4),
        (reg["wconf"], wc2, PAIRS), (wc2, wc4, HYP2), (wc4, wconfA, HYP4),
        (reg["bupf"], bu2, PAIRS), (bu2, bu4, HYP2), (bu4, bupfA, HYP4),
    ])
    build_l0tail(nc, semctx, fused=True)
    xn1_pair = dr("xn1_pair", [2048, 2048], BF16, "Internal")
    emit_ag_block(nc, semctx, [(reg["xn1_my"], xn1_pair, PAIRS)])
    build_pre1(nc, semctx, fused=True)
    h_pair = dr("h_pair", [2048, T], BF16, "Internal")
    emit_ag_block(nc, semctx, [(reg["h_bf"], h_pair, PAIRS)])
    build_dt1(nc, semctx, fused=True)
    build_scan1(nc, semctx, fused=True)
    z_pair = dr("z_pair", [2048, T], BF16, "Internal")
    emit_ag_block(nc, semctx, [(reg["z_bf"], z_pair, PAIRS)])
    build_l1tail(nc, semctx, fused=True)
    return nc


def _fused_inputs(ins, core):
    du_opair = np.zeros((2 * T, 512), BF16NP)
    du_we = np.zeros((1024, 4096), BF16NP)
    du_bu = np.zeros((1024, 4), np.float32)
    du_wc = np.zeros((1024, 4096), BF16NP)
    du_xn1p = np.zeros((2048, 2048), BF16NP)
    du_hp = np.zeros((2048, T), BF16NP)
    du_hf = np.zeros((1024, T), np.float32)
    du_zp = np.zeros((2048, T), BF16NP)
    du_x1 = np.zeros((1024, 2048), np.float32)
    du_xn1 = np.zeros((1024, 2048), BF16NP)
    m = {}
    m.update(_pre0_inputs(ins, core))
    m.update(_scan0_inputs(ins, core))
    m.update(_ufold_inputs(ins, core))
    m.update(_l0tail_inputs(ins, core, du_opair, du_we, du_bu, du_wc))
    m.update(_pre1_inputs(ins, core, du_xn1p))
    m.update(_dt1_inputs(ins, core, du_hp, du_hf))
    m.update(_scan1_inputs(ins, core))
    m.update(_l1tail_inputs(ins, core, du_zp, du_x1, du_xn1))
    return m


USE_FUSED = True


_NC_CACHE = {}


def _get_nc(name, builder):
    if name not in _NC_CACHE:
        _NC_CACHE[name] = builder()
    return _NC_CACHE[name]


def kernel(**inputs):
    ins = inputs
    cores = list(range(NCORE))

    if USE_FUSED:
        nc = _get_nc("fused", build_fused)
        hostkeys = nc._fused_hostkeys
        in_maps = []
        for c in cores:
            m = _fused_inputs(ins, c)
            in_maps.append({k: v for k, v in m.items() if k in hostkeys})
        res = run_bass_kernel_spmd(nc, in_maps, core_ids=cores)
        out = np.empty((4, 4096, 1024), np.float32)
        for c in cores:
            b, g = c // 2, c % 2
            out[b, 2048 * g:2048 * (g + 1), :] = np.asarray(res.results[c]["out_my"]).T
        return out

    nc = _get_nc("pre0", build_pre0)
    res = run_bass_kernel_spmd(nc, [_pre0_inputs(ins, c) for c in cores], core_ids=cores)
    xn0bf = [np.asarray(res.results[c]["xn0bf"]) for c in cores]
    act = [np.asarray(res.results[c]["act"]) for c in cores]
    gab = [np.asarray(res.results[c]["gab"]) for c in cores]

    nc = _get_nc("scan0", build_scan0)
    maps = []
    for c in cores:
        m = _scan0_inputs(ins, c)
        m["act"] = act[c]
        m["gab"] = gab[c]
        maps.append(m)
    res = run_bass_kernel_spmd(nc, maps, core_ids=cores)
    o = [np.asarray(res.results[c]["o_out"]) for c in cores]

    nc = _get_nc("ufold", build_ufold)
    res = run_bass_kernel_spmd(nc, [_ufold_inputs(ins, c) for c in cores], core_ids=cores)
    wexpfA = np.ascontiguousarray(np.concatenate([np.asarray(res.results[c]["wexpf"]) for c in cores], axis=0))
    bupfA = np.ascontiguousarray(np.concatenate([np.asarray(res.results[c]["bupf"]) for c in cores], axis=0))
    wconfA = np.ascontiguousarray(np.concatenate([np.asarray(res.results[c]["wconf"]) for c in cores], axis=0))

    o_pair = [np.ascontiguousarray(np.concatenate([o[2 * b], o[2 * b + 1]], axis=0)) for b in range(4)]
    nc = _get_nc("l0tail", build_l0tail)
    res = run_bass_kernel_spmd(
        nc, [_l0tail_inputs(ins, c, o_pair[c // 2], wexpfA, bupfA, wconfA) for c in cores],
        core_ids=cores)
    x1 = [np.asarray(res.results[c]["x1_my"]) for c in cores]
    xn1 = [np.asarray(res.results[c]["xn1_my"]) for c in cores]

    xn1_full = [np.ascontiguousarray(np.concatenate([xn1[2 * b], xn1[2 * b + 1]], axis=0)) for b in range(4)]
    nc = _get_nc("pre1", build_pre1)
    res = run_bass_kernel_spmd(nc, [_pre1_inputs(ins, c, xn1_full[c // 2]) for c in cores], core_ids=cores)
    h_f = [np.asarray(res.results[c]["h_f"]) for c in cores]
    h_bf = [np.asarray(res.results[c]["h_bf"]) for c in cores]
    sg = [np.asarray(res.results[c]["sg"]) for c in cores]

    hbf_full = [np.ascontiguousarray(np.concatenate([h_bf[2 * b], h_bf[2 * b + 1]], axis=0)) for b in range(4)]
    nc = _get_nc("dt1", build_dt1)
    res = run_bass_kernel_spmd(nc, [_dt1_inputs(ins, c, hbf_full[c // 2], h_f[c]) for c in cores],
                               core_ids=cores)
    dtb = [np.asarray(res.results[c]["dt_bf"]) for c in cores]
    ub = [np.asarray(res.results[c]["u_bf"]) for c in cores]
    Bb = [np.asarray(res.results[c]["B_bf"]) for c in cores]

    nc = _get_nc("scan1", build_scan1)
    maps = []
    for c in cores:
        m = _scan1_inputs(ins, c)
        m["dt_bf"] = dtb[c]
        m["u_bf"] = ub[c]
        m["B_bf"] = Bb[c]
        m["h_f"] = h_f[c]
        m["sg"] = sg[c]
        maps.append(m)
    res = run_bass_kernel_spmd(nc, maps, core_ids=cores)
    zb = [np.asarray(res.results[c]["z_bf"]) for c in cores]

    z_full = [np.ascontiguousarray(np.concatenate([zb[2 * b], zb[2 * b + 1]], axis=0)) for b in range(4)]
    nc = _get_nc("l1tail", build_l1tail)
    res = run_bass_kernel_spmd(
        nc, [_l1tail_inputs(ins, c, z_full[c // 2], x1[c], xn1[c]) for c in cores], core_ids=cores)

    out = np.empty((4, 4096, 1024), np.float32)
    for c in cores:
        b, g = c // 2, c % 2
        out[b, 2048 * g:2048 * (g + 1), :] = np.asarray(res.results[c]["out_my"]).T
    return out
